# version 12
# baseline (speedup 1.0000x reference)
import contextlib
import numpy as np
import ml_dtypes
import concourse.bass as bass
import concourse.mybir as mybir
from concourse.bass_utils import run_bass_kernel_spmd

F32 = mybir.dt.float32
BF16 = mybir.dt.bfloat16
AF = mybir.ActivationFunctionType
ALU = mybir.AluOpType
BF = ml_dtypes.bfloat16

D = 4096
NC = 32
H = 32
GT = 512
NA, NB = 16, 32
SCALE = float(1.0 / np.sqrt(128.0))
CBIAS = -12.0
DEADB = -1e30
EPS = 1e-5
E = 8
FFN_SCALE = 0.5


class Prog:
    ENGINES = ("t", "s", "v", "g", "y")

    def __init__(self):
        self.q = {e: [] for e in self.ENGINES}
        self.cnt = {}
        self.waited = {e: {} for e in self.ENGINES}

    def inc(self, sem, n=1):
        self.cnt[sem] = self.cnt.get(sem, 0) + n
        return self.cnt[sem]

    def at(self, sem):
        return self.cnt.get(sem, 0)

    def emit(self, e, fn, waits=(), inc=None):
        w = []
        for sem, tgt in waits:
            if tgt <= 0:
                continue
            if self.waited[e].get(sem, -1) >= tgt:
                continue
            self.waited[e][sem] = tgt
            w.append((sem, tgt))
        cnt = None
        if inc is not None:
            cnt = self.inc(*inc)
        self.q[e].append((fn, w, inc))
        return cnt


def build(debug=False):
    nc = bass.Bass("TRN2", target_bir_lowering=False)

    xq = nc.dram_tensor("xq", [128, NC * 1024], F32, kind="ExternalInput").ap()
    xkA = nc.dram_tensor("xkA", [NA * 128, 4096], F32, kind="ExternalInput").ap()
    xkB = nc.dram_tensor("xkB", [NB * 128, 4096], F32, kind="ExternalInput").ap()
    sbias = nc.dram_tensor("sbias", [128, NA + NB], F32, kind="ExternalInput").ap()
    wq_r = nc.dram_tensor("wq_r", [NC * 128, NC * 128], F32, kind="ExternalInput").ap()
    wo_r = nc.dram_tensor("wo_r", [NC * 128, NC * 128], F32, kind="ExternalInput").ap()
    wkv = nc.dram_tensor("wkv", [128, NC * 256], F32, kind="ExternalInput").ap()
    rw_in = nc.dram_tensor("rw_in", [128, NC * E], F32, kind="ExternalInput").ap()
    b_row = nc.dram_tensor("b_row", [1, E], F32, kind="ExternalInput").ap()
    a_in = nc.dram_tensor("a_in", [128, NC * 128], F32, kind="ExternalInput").ap()
    bm_in = nc.dram_tensor("bm_in", [128, NC * 128], F32, kind="ExternalInput").ap()
    eexp_in = nc.dram_tensor("eexp", [E, 128], F32, kind="ExternalInput").ap()
    eye_b_in = nc.dram_tensor("eye_b", [128, 128], BF16, kind="ExternalInput").ap()
    eye_f_in = nc.dram_tensor("eye_f", [128, 128], F32, kind="ExternalInput").ap()
    masks_in = nc.dram_tensor("masks", [128, 4 * GT], BF16, kind="ExternalInput").ap()

    y_out = nc.dram_tensor("y", [D, 1024], F32, kind="ExternalOutput").ap()
    spill = nc.dram_tensor("r1spill", [NC * 128, GT], F32, kind="Internal").ap()

    taps = {}
    if debug:
        for name, shape, dt in [
            ("tap_kA", [128, NA * 128], F32), ("tap_vA", [128, NA * 128], F32),
            ("tap_kB", [128, NB * 128], F32), ("tap_vB", [128, NB * 128], F32),
            ("tap_q", [128, H * 1024], F32), ("tap_attn", [128, H * 1024], F32),
            ("tap_r1", [128, NC * GT], F32), ("tap_x1", [128, NC * GT], F32),
            ("tap_w", [128, 4 * E], F32), ("tap_hw", [128, GT], F32),
            ("tap_r1b", [128, NC * GT], F32),
        ]:
            taps[name] = nc.dram_tensor(name, shape, dt, kind="ExternalOutput").ap()

    QA = nc.alloc_sbuf_tensor("QA", [128, H * 1024], BF16)
    AR2_f = nc.alloc_sbuf_tensor("AR2", [128, NC * GT], F32)
    XO = AR2_f.bitcast(BF16)
    KV = nc.alloc_sbuf_tensor("KV", [128, (NA + NB) * 128 * 2], BF16)
    kTA = KV.ap()[:, 0:NA * 128]
    kTB = KV.ap()[:, NA * 128:(NA + NB) * 128]
    VA = KV.ap()[:, (NA + NB) * 128:(2 * NA + NB) * 128]
    VB = KV.ap()[:, (2 * NA + NB) * 128:]
    WBt = [nc.alloc_sbuf_tensor(f"WB{i}", [128, 16 * 128], BF16) for i in range(2)]
    XKBt = [nc.alloc_sbuf_tensor(f"XKB{i}", [128, 8 * GT], BF16) for i in range(2)]
    PTt = [nc.alloc_sbuf_tensor(f"PT{i}", [128, GT], BF16) for i in range(4)]
    PTD = nc.alloc_sbuf_tensor("PTD", [128, GT], BF16)
    vstage = PTD
    masks = nc.alloc_sbuf_tensor("masks_sb", [128, 4 * GT], BF16)
    sbias_sb = nc.alloc_sbuf_tensor("sbias_sb", [128, NA + NB], F32)
    ones128 = nc.alloc_sbuf_tensor("ones128", [128, 128], BF16)
    onescol = nc.alloc_sbuf_tensor("onescol", [128, 1], BF16)
    onesrow_f = nc.alloc_sbuf_tensor("onesrow_f", [1, 128], F32)
    eye_b = nc.alloc_sbuf_tensor("eye_b_sb", [128, 128], BF16)
    eye_f = nc.alloc_sbuf_tensor("eye_f_sb", [128, 128], F32)
    wkv_sb = QA.ap()[:, 0:NC * 256]
    a_sb = KV.ap()[:, 0:NC * 128]
    bm_sb = KV.ap()[:, NC * 128:2 * NC * 128]
    rw_sb = KV.ap()[:, 2 * NC * 128:2 * NC * 128 + NC * E]
    b_sb = nc.alloc_sbuf_tensor("b_sb", [1, E], F32)
    eexp_sb = nc.alloc_sbuf_tensor("eexp_sb", [E, 128], F32)
    RESt = [nc.alloc_sbuf_tensor(f"RES{i}", [128, GT], F32) for i in range(2)]
    R1St = [nc.alloc_sbuf_tensor(f"R1S{i}", [128, GT], F32) for i in range(2)]
    lnbf = PTt[1]
    lnsq = PTt[2]
    M_sb = nc.alloc_sbuf_tensor("M_sb", [128, GT], F32)
    Rs_sb = nc.alloc_sbuf_tensor("Rs_sb", [128, GT], F32)
    rinv = Rs_sb
    wexp_sb = nc.alloc_sbuf_tensor("wexp_sb", [128, GT], F32)
    hwb = PTt[0]
    lgt = nc.alloc_sbuf_tensor("lgt", [128, E], F32)
    top8 = nc.alloc_sbuf_tensor("top8", [128, E], F32)
    m1d = nc.alloc_sbuf_tensor("m1d", [128, 1], F32)
    g1t = nc.alloc_sbuf_tensor("g1t", [128, 1], F32)
    g2t = nc.alloc_sbuf_tensor("g2t", [128, 1], F32)
    msk1 = nc.alloc_sbuf_tensor("msk1", [128, E], F32)
    msk2 = nc.alloc_sbuf_tensor("msk2", [128, E], F32)
    wful = nc.alloc_sbuf_tensor("wful", [128, 4 * E], F32)
    wtrow_full = masks.bitcast(F32)
    srow = [nc.alloc_sbuf_tensor(f"srow{i}", [1, GT], F32) for i in range(3)]

    assert nc.sbuf_bytes_remaining >= 0, f"SBUF over: {nc.sbuf_bytes_remaining}"

    PB = [nc.alloc_psum_tensor(f"PB{i}", [128, 512], F32) for i in range(7)]
    PTB = nc.alloc_psum_tensor("PTB", [128, 1024], BF16)
    assert nc.psum_banks_remaining == 0

    P = Prog()
    rd = {}

    def mark(name, sem, cnt):
        rd[name] = (sem, cnt)

    def need(name):
        return [rd[name]] if name in rd else []

    P.emit("g", lambda g: g.memset(ones128.ap(), 1.0))
    P.emit("g", lambda g: g.memset(onescol.ap(), 1.0))
    P.emit("g", lambda g: g.memset(onesrow_f.ap(), 1.0), inc=("GE", 1))
    mark("consts_g", "GE", P.at("GE"))

    for dst, src in [(masks.ap(), masks_in), (sbias_sb.ap(), sbias),
                     (eye_b.ap(), eye_b_in), (eye_f.ap(), eye_f_in),
                     (eexp_sb.ap(), eexp_in), (b_sb.ap(), b_row)]:
        P.emit("y", lambda sy, d=dst, s=src: sy.dma_start(d, s), inc=("DC", 16))
    mark("consts", "DC", P.at("DC"))

    P.emit("g", lambda g: g.dma_start(wkv_sb, wkv), inc=("DKV", 16))
    mark("wkv", "DKV", P.at("DKV"))

    for hf in range(4):
        src = xq[:, hf * 8 * 1024:(hf + 1) * 8 * 1024]
        dst = XO.ap()[:, hf * 8 * 1024:(hf + 1) * 8 * 1024]
        P.emit("g", lambda g, d=dst, s=src: g.dma_start(d, s), inc=("DXO", 16))
    mark("xo", "DXO", P.at("DXO"))

    def p2_quarter(xk, kT, V, ti, qtr, tag, ntiles):
        qi = ti * 4 + qtr
        bi = qi % 2
        buf = XKBt[bi]
        src = xk[qi * 128:(qi + 1) * 128, :]
        dsem = f"DXK{bi}"
        w = need(f"xkb{bi}_free")
        P.emit("g", lambda g, d=buf.ap(), s=src: g.dma_start(d, s), waits=w,
               inc=(dsem, 16))
        dtot = P.at(dsem)
        for which, pb, wofs in (("k", PB[4], 0), ("v", PB[5], 128)):
            w = [(dsem, dtot)] + need("wkv") + need(f"pb{which}_free")
            for ci in range(8):
                c = qtr * 8 + ci
                lhs = wkv_sb[:, c * 256 + wofs:c * 256 + wofs + 128]
                rhs = buf.ap()[:, ci * GT:(ci + 1) * GT]
                cnt = P.emit("t", lambda t, o=pb.ap(), l=lhs, r=rhs,
                             st=(c == 0), sp=(c == NC - 1):
                             t.matmul(o, l, r, start=st, stop=sp),
                             waits=(w if ci == 0 else ()), inc=("TE", 1))
            mark(f"p2{which}", "TE", cnt)
        mark(f"xkb{bi}_free", "TE", P.at("TE"))
        if qtr == 3:
            cnt = P.emit("s", lambda s, d=kT[:, ti * GT:(ti + 1) * GT], o=PB[4].ap():
                         s.activation(d, o, AF.Copy), waits=need("p2k"), inc=("SC", 1))
            mark("pbk_free", "SC", cnt)
            vst = P.emit("s", lambda s, d=vstage.ap(), o=PB[5].ap():
                         s.activation(d, o, AF.Copy), waits=need("p2v"), inc=("SC", 1))
            mark("pbv_free", "SC", vst)
            for cs in range(4):
                w = [("SC", vst)] + need("consts") + (need("ptb_free") if cs == 0 else [])
                cnt = P.emit("t", lambda t, o=PTB.ap()[:, cs * 128:(cs + 1) * 128],
                             i=vstage.ap()[:, cs * 128:(cs + 1) * 128], e=eye_b.ap():
                             t.transpose(o, i, e), waits=w, inc=("TE", 1))
            mark("vtr", "TE", cnt)
            cnt = P.emit("s", lambda s, d=V[:, ti * GT:(ti + 1) * GT],
                         o=PTB.ap()[:, 0:512]:
                         s.activation(d, o, AF.Copy), waits=need("vtr"), inc=("SC", 1))
            mark("ptb_free", "SC", cnt)
            if ti == ntiles - 1:
                mark(f"kv{tag}", "SC", P.at("SC"))

    qbank = {0: [PB[0], PB[2]], 1: [PB[1], PB[3]]}

    def p1_half(dq, half):
        bi = (dq * 2 + half) % 2
        buf = WBt[bi]
        dsem = f"DWB{bi}"
        w = need(f"wb{bi}_free")
        src = wq_r[dq * 128:(dq + 1) * 128, half * 2048:(half + 1) * 2048]
        P.emit("g", lambda g, d=buf.ap(), s=src: g.dma_start(d, s), waits=w,
               inc=(dsem, 16))
        dtot = P.at(dsem)
        for g2 in range(2):
            pb = qbank[g2][dq % 2]
            w = [(dsem, dtot)] + need("xo") + need(f"qb{g2}{dq % 2}_free")
            for ci in range(16):
                c = half * 16 + ci
                lhs = buf.ap()[:, ci * 128:(ci + 1) * 128]
                rhs = XO.ap()[:, c * 1024 + g2 * GT:c * 1024 + g2 * GT + GT]
                cnt = P.emit("t", lambda t, o=pb.ap(), l=lhs, r=rhs,
                             st=(c == 0), sp=(c == NC - 1):
                             t.matmul(o, l, r, start=st, stop=sp),
                             waits=(w if ci == 0 else ()), inc=("TE", 1))
        mark(f"wb{bi}_free", "TE", cnt)
        if half == 1:
            for g2 in range(2):
                pb = qbank[g2][dq % 2]
                dst = QA.ap()[:, dq * 1024 + g2 * GT: dq * 1024 + g2 * GT + GT]
                scnt = P.emit("s", lambda s, d=dst, o=pb.ap():
                              s.activation(d, o, AF.Copy), waits=[("TE", cnt)],
                              inc=("SC", 1))
                mark(f"qb{g2}{dq % 2}_free", "SC", scnt)

    p2_units = [("A", xkA, kTA, VA, NA // 4, ti, qtr)
                for ti in range(NA // 4) for qtr in range(4)]
    p2_units += [("B", xkB, kTB, VB, NB // 4, ti, qtr)
                 for ti in range(NB // 4) for qtr in range(4)]
    p1_order = list(range(8, NC)) + list(range(8))
    p1_units = [(dq, half) for dq in p1_order for half in range(2)]
    ip2 = ip1 = 0
    while ip2 < len(p2_units) or ip1 < len(p1_units):
        if ip2 < len(p2_units):
            tag, xk_, kT_, V_, nt_, ti_, qtr_ = p2_units[ip2]
            p2_quarter(xk_, kT_, V_, ti_, qtr_, tag, nt_)
            ip2 += 1
        if ip1 < len(p1_units):
            p1_half(*p1_units[ip1])
            ip1 += 1
    mark("qT", "SC", P.at("SC"))

    if debug:
        P.emit("g", lambda g: g.dma_start(taps["tap_kA"], kTA), waits=need("kvA"), inc=("DT", 16))
        P.emit("g", lambda g: g.dma_start(taps["tap_vA"], VA), waits=need("kvA"), inc=("DT", 16))
        P.emit("g", lambda g: g.dma_start(taps["tap_kB"], kTB), waits=need("kvB"), inc=("DT", 16))
        P.emit("g", lambda g: g.dma_start(taps["tap_vB"], VB), waits=need("kvB"), inc=("DT", 16))
        P.emit("g", lambda g: g.dma_start(taps["tap_q"], QA.ap()), waits=need("qT"), inc=("DT", 16))
        mark("tapq", "DT", P.at("DT"))

    srot = [PB[0], PB[1], PB[2]]
    obank = [PB[3], PB[4]]
    rbank = [PB[5], PB[6]]
    for k in range(2):
        mark(f"obank{k}_free", "SC", P.at("SC"))
        mark(f"rbank{k}_free", "SC", P.at("SC"))
    slots = []
    pairidx = 0
    for (Ng, kT, V, gbase, g2) in [(NA, kTA, VA, 0, 0), (NB, kTB, VB, NA, 1)]:
        for h in range(H):
            ctx = dict(Ng=Ng, kT=kT, V=V, gbase=gbase, pidx=pairidx,
                       qsl=QA.ap()[:, h * 1024 + g2 * GT: h * 1024 + g2 * GT + GT],
                       ob=obank[pairidx % 2], rb=rbank[pairidx % 2], pready={})
            for s in range(Ng):
                slots.append((ctx, s))
            pairidx += 1

    pending_tails = []

    def emit_S_exp(ctx, s, gi):
        bank = srot[gi % 3]
        w = need(f"srot{gi % 3}_free") + (need("qT") if ctx["pidx"] == 0 else [])
        scnt = P.emit("t", lambda t, o=bank.ap(),
                      l=ctx["kT"][:, s * 128:(s + 1) * 128], r=ctx["qsl"]:
                      t.matmul(o, l, r, start=True, stop=True),
                      waits=w, inc=("TE", 1))
        dstP = PTD.ap() if s < 4 else PTt[s % 4].ap()
        w = [("TE", scnt)] + need("consts") + \
            (need("ptd_free") if s < 4 else need(f"pt{s % 4}_free"))
        ecnt = P.emit("s", lambda ss, d=dstP, o=bank.ap(),
                      b=sbias_sb.ap()[:, ctx["gbase"] + s:ctx["gbase"] + s + 1]:
                      ss.activation(d, o, AF.Exp, bias=b, scale=SCALE),
                      waits=w, inc=("SC", 1))
        mark(f"srot{gi % 3}_free", "SC", ecnt)
        if s < 4:
            w = [("SC", ecnt)] + need(f"pt{s % 4}_free") + need("consts")
            vcnt = P.emit("v", lambda v, d=PTt[s % 4].ap(), a=PTD.ap(),
                          m=masks.ap()[:, s * GT:(s + 1) * GT]:
                          v.tensor_tensor(d, a, m, ALU.mult),
                          waits=w, inc=("VE", 1))
            mark("ptd_free", "VE", vcnt)
            ctx["pready"][s] = ("VE", vcnt)
        else:
            ctx["pready"][s] = ("SC", ecnt)

    def emit_RPV(ctx, s):
        Ng, pidx, rb, ob = ctx["Ng"], ctx["pidx"], ctx["rb"], ctx["ob"]
        w = [ctx["pready"][s]] + ([need("consts_g")[0]] if pidx == 0 and s == 0 else []) \
            + (need(f"rbank{pidx % 2}_free") if s == 0 else [])
        P.emit("t", lambda t, o=rb.ap(), l=ones128.ap(), r=PTt[s % 4].ap(),
               st=(s == 0), sp=(s == Ng - 1):
               t.matmul(o, l, r, start=st, stop=sp), waits=w, inc=("TE", 1))
        w = need(f"obank{pidx % 2}_free") if s == 0 else []
        cnt = P.emit("t", lambda t, o=ob.ap(), l=ctx["V"][:, s * 128:(s + 1) * 128],
                     r=PTt[s % 4].ap(), st=(s == 0), sp=(s == Ng - 1):
                     t.matmul(o, l, r, start=st, stop=sp), waits=w, inc=("TE", 1))
        mark(f"pt{s % 4}_free", "TE", cnt)
        if s == Ng - 1:
            ostop = P.at("TE")

            def tail(ostop=ostop, rb=rb, ob=ob, qsl=ctx["qsl"], pidx=pidx):
                rcnt = P.emit("v", lambda v, o=rb.ap(): v.reciprocal(rinv.ap(), o),
                              waits=[("TE", ostop)], inc=("VE", 1))
                mark(f"rbank{pidx % 2}_free", "VE", rcnt)
                mcnt = P.emit("v", lambda v, d=qsl, o=ob.ap():
                              v.tensor_tensor(d, o, rinv.ap(), ALU.mult),
                              waits=[("VE", rcnt)], inc=("VE", 1))
                mark(f"obank{pidx % 2}_free", "VE", mcnt)
            pending_tails.append(tail)

    for gi, (ctx, s) in enumerate(slots):
        emit_S_exp(ctx, s, gi)
        if s == 4 and pending_tails:
            pending_tails.pop(0)()
        if gi >= 2:
            emit_RPV(*slots[gi - 2])
    emit_RPV(*slots[-2])
    emit_RPV(*slots[-1])
    for t_ in pending_tails:
        t_()
    pending_tails = []
    mark("attnT", "VE", P.at("VE"))
    p3_te = P.at("TE")

    if debug:
        P.emit("g", lambda g: g.dma_start(taps["tap_attn"], QA.ap()),
               waits=need("attnT") + need("tapq"), inc=("DT", 16))
        mark("tapattn", "DT", P.at("DT"))

    ybank = {0: [PB[0], PB[1]], 1: [PB[2], PB[3]]}
    statb = [PB[4], PB[5]]
    PTBf32 = PTB.bitcast(F32)
    lnbf2 = PTD
    lnsq2 = PTt[3]
    pending_stats = pending_stats2 = None
    for oc in range(NC):
        for half in range(2):
            bi = (oc * 2 + half) % 2
            buf = WBt[bi]
            dsem = f"DWB{bi}"
            w = need(f"wb{bi}_free")
            src = wo_r[oc * 128:(oc + 1) * 128, half * 2048:(half + 1) * 2048]
            P.emit("g", lambda g, d=buf.ap(), s=src: g.dma_start(d, s), waits=w,
                   inc=(dsem, 16))
            dtot = P.at(dsem)
            for g2 in range(2):
                pb = ybank[g2][oc % 2]
                w = [(dsem, dtot)] + need("attnT") + need(f"yb{g2}{oc % 2}_free")
                for ci in range(16):
                    c = half * 16 + ci
                    lhs = buf.ap()[:, ci * 128:(ci + 1) * 128]
                    rhs = QA.ap()[:, c * 1024 + g2 * GT:c * 1024 + g2 * GT + GT]
                    cnt = P.emit("t", lambda t, o=pb.ap(), l=lhs, r=rhs,
                                 st=(c == 0), sp=(c == NC - 1):
                                 t.matmul(o, l, r, start=st, stop=sp),
                                 waits=(w if ci == 0 else ()), inc=("TE", 1))
            mark(f"wb{bi}_free", "TE", cnt)
        if pending_stats is not None:
            pending_stats()
            pending_stats = None
        if pending_stats2 is not None:
            pending_stats2()
            pending_stats2 = None
        for g2 in range(2):
            rbuf = RESt[g2]
            rsem = f"DR{g2}"
            w = need(f"res{g2}_free")
            P.emit("y", lambda sy, d=rbuf.ap(),
                   s=xq[:, oc * 1024 + g2 * GT:oc * 1024 + (g2 + 1) * GT]:
                   sy.dma_start(d, s), waits=w, inc=(rsem, 16))
            rtot = P.at(rsem)
            pb = ybank[g2][oc % 2]
            if g2 == 0:
                dst = AR2_f.ap()[:, oc * GT:(oc + 1) * GT]
            else:
                dst = R1St[oc % 2].ap()
            w = [("TE", cnt), (rsem, rtot)] + \
                (need(f"r1s{oc % 2}_free") + need(f"r1s{oc % 2}_read") if g2 == 1 else [])
            vcnt = P.emit("v", lambda v, d=dst, o=pb.ap(), r=rbuf.ap():
                          v.tensor_tensor(d, o, r, ALU.add), waits=w, inc=("VE", 1))
            mark(f"res{g2}_free", "VE", vcnt)
            mark(f"yb{g2}{oc % 2}_free", "VE", vcnt)
            if g2 == 0:
                scnt = P.emit("s", lambda s, a=dst: s.activation(lnsq.ap(), a, AF.Square),
                              waits=[("VE", vcnt)] + need("lnsq_free"), inc=("SC", 1))
                vcnt2 = P.emit("v", lambda v, a=dst: v.tensor_copy(lnbf.ap(), a),
                               waits=need("lnbf_free"), inc=("VE", 1))

                def stats(oc=oc, scnt=scnt, vcnt2=vcnt2):
                    c1 = P.emit("t", lambda t, st=(oc == 0), sp=(oc == NC - 1):
                                t.matmul(statb[0].ap()[0:1, :], onescol.ap(), lnbf.ap(),
                                         start=st, stop=sp),
                                waits=[("VE", vcnt2)] + need("consts_g"), inc=("TE", 1))
                    mark("lnbf_free", "TE", c1)
                    c2 = P.emit("t", lambda t, st=(oc == 0), sp=(oc == NC - 1):
                                t.matmul(statb[1].ap()[0:1, :], onescol.ap(), lnsq.ap(),
                                         start=st, stop=sp), waits=[("SC", scnt)],
                                inc=("TE", 1))
                    mark("lnsq_free", "TE", c2)
                    mark("ln1stats", "TE", c2)
                pending_stats = stats
            else:
                ycnt = P.emit("y", lambda sy, d=spill[oc * 128:(oc + 1) * 128, :], s=dst:
                              sy.dma_start(d, s), waits=[("VE", vcnt)], inc=("DSP", 16))
                mark(f"r1s{oc % 2}_free", "DSP", ycnt)
                scnt2 = P.emit("s", lambda s, a=dst: s.activation(lnsq2.ap(), a, AF.Square),
                               waits=[("VE", vcnt)] + need("lnsq2_free"), inc=("SC", 1))
                mark(f"r1s{oc % 2}_read", "SC", scnt2)
                vcnt3 = P.emit("v", lambda v, a=dst: v.tensor_copy(lnbf2.ap(), a),
                               waits=need("lnbf2_free"), inc=("VE", 1))

                def statsb(oc=oc, scnt2=scnt2, vcnt3=vcnt3):
                    c1 = P.emit("t", lambda t, st=(oc == 0), sp=(oc == NC - 1):
                                t.matmul(PB[6].ap()[0:1, :], onescol.ap(), lnbf2.ap(),
                                         start=st, stop=sp),
                                waits=[("VE", vcnt3)], inc=("TE", 1))
                    mark("lnbf2_free", "TE", c1)
                    c2 = P.emit("t", lambda t, st=(oc == 0), sp=(oc == NC - 1):
                                t.matmul(PTBf32.ap()[0:1, 0:512], onescol.ap(), lnsq2.ap(),
                                         start=st, stop=sp), waits=[("SC", scnt2)],
                                inc=("TE", 1))
                    mark("lnsq2_free", "TE", c2)
                    mark("ln1bstats", "TE", c2)
                pending_stats2 = statsb
    for ps in (pending_stats, pending_stats2):
        if ps is not None:
            ps()
    pending_stats = pending_stats2 = None
    mark("p4_te", "TE", P.at("TE"))
    mark("p4", "VE", P.at("VE"))

    for i, (dstw, srcw) in enumerate([(a_sb, a_in), (bm_sb, bm_in), (rw_sb, rw_in)]):
        w = [("TE", p3_te)] if i == 0 else []
        P.emit("g", lambda g, d=dstw, s=srcw: g.dma_start(d, s), waits=w, inc=("DF", 16))
    mark("ffnw", "DF", P.at("DF"))

    if debug:
        P.emit("g", lambda g: g.dma_start(taps["tap_r1"], AR2_f.ap()),
               waits=need("p4") + need("tapattn"), inc=("DT", 16))
        mark("tapr1", "DT", P.at("DT"))
        P.emit("y", lambda sy: sy.dma_start(
               taps["tap_r1b"].rearrange("p (c n) -> p c n", n=GT),
               spill.rearrange("(c p) n -> p c n", p=128)),
               waits=[("DSP", P.at("DSP"))], inc=("DT2", 16))

    def ln_finish(tag, sA, sB, Mdst, Rdst, r0, r1_, r2_, extra=()):
        sem_stats = need(f"{tag}stats")
        c = P.emit("v", lambda v: v.tensor_scalar(r0.ap(), sA[0:1, :],
                   1.0 / D, None, ALU.mult), waits=sem_stats + list(extra), inc=("VE", 1))
        c = P.emit("v", lambda v: v.tensor_scalar(r1_.ap(), sB[0:1, :],
                   1.0 / D, EPS, ALU.mult, ALU.add), inc=("VE", 1))
        c = P.emit("v", lambda v: v.tensor_tensor(r2_.ap(), r0.ap(),
                   r0.ap(), ALU.mult), inc=("VE", 1))
        c = P.emit("v", lambda v: v.tensor_tensor(r1_.ap(), r1_.ap(),
                   r2_.ap(), ALU.subtract), inc=("VE", 1))
        mark(f"{tag}statdone", "VE", c)
        sc = P.emit("s", lambda s: s.activation(r2_.ap(), r1_.ap(), AF.Sqrt),
                    waits=[("VE", c)], inc=("SC", 1))
        rc = P.emit("v", lambda v: v.reciprocal(r1_.ap(), r2_.ap()),
                    waits=[("SC", sc)], inc=("VE", 1))
        b1 = P.emit("t", lambda t: t.matmul(PB[0].ap(), onesrow_f.ap(), r0.ap(),
                    start=True, stop=True),
                    waits=[("VE", c)] + need("consts_g") + need("bc_free"), inc=("TE", 1))
        s1 = P.emit("s", lambda s: s.activation(Mdst.ap(), PB[0].ap(), AF.Copy),
                    waits=[("TE", b1)], inc=("SC", 1))
        b2 = P.emit("t", lambda t: t.matmul(PB[0].ap(), onesrow_f.ap(), r1_.ap(),
                    start=True, stop=True), waits=[("VE", rc), ("SC", s1)], inc=("TE", 1))
        s2 = P.emit("s", lambda s: s.activation(Rdst.ap(), PB[0].ap(), AF.Copy),
                    waits=[("TE", b2)], inc=("SC", 1))
        mark("bc_free", "SC", s2)
        return s1, s2

    def ln_apply(x1_to_QA, g2, Msb, Rsb, s1, s2, extra=()):
        for c_ in range(NC):
            ap = AR2_f.ap()[:, c_ * GT:(c_ + 1) * GT]
            P.emit("v", lambda v, a=ap: v.tensor_tensor(a, a, Msb.ap(), ALU.subtract),
                   waits=[("SC", s1), ("SC", s2)] + list(extra), inc=("VE", 1))
            v2 = P.emit("v", lambda v, a=ap: v.tensor_tensor(a, a, Rsb.ap(), ALU.mult),
                        inc=("VE", 1))
            mark(f"lnapply_{c_}", "VE", v2)
            if x1_to_QA:
                dst = QA.ap()[:, g2 * (H * 512) + c_ * GT: g2 * (H * 512) + (c_ + 1) * GT]
                sc2 = P.emit("s", lambda s, d=dst, a=ap: s.activation(d, a, AF.Copy),
                             waits=[("VE", v2)] + need("p4_te"), inc=("SC", 1))
                mark(f"x1b_{c_}", "SC", sc2)
        mark("lnapplied", "VE", P.at("VE"))

    def ln_finish_apply(tag, x1_to_QA, g2):
        s1, s2 = ln_finish(tag, statb[0].ap(), statb[1].ap(), M_sb, Rs_sb,
                           srow[0], srow[1], srow[2])
        ln_apply(x1_to_QA, g2, M_sb, Rs_sb, s1, s2)

    def ffn_and_ln2_and_out(g2):
        x1b = QA.ap()[:, g2 * (H * 512):(g2 + 1) * (H * 512)]
        for ts in range(4):
            for c in range(NC):
                w = need("ffnw") + [rd[f"x1b_{c}"]] + (need("lg_free") if c == 0 else [])
                lhs = x1b[:, c * GT + ts * 128: c * GT + ts * 128 + 128]
                rhs = rw_sb[:, c * E:(c + 1) * E]
                P.emit("t", lambda t, o=PB[0].ap()[:, 0:E], l=lhs, r=rhs, st=(c == 0):
                       t.matmul(o, l, r, start=st, stop=False),
                       waits=w, inc=("TE", 1))
            cnt = P.emit("t", lambda t: t.matmul(PB[0].ap()[:, 0:E], onesrow_f.ap(),
                         b_sb.ap(), start=False, stop=True),
                         waits=need("consts") + need("consts_g"), inc=("TE", 1))
            sc = P.emit("s", lambda s: s.activation(lgt.ap(), PB[0].ap()[:, 0:E], AF.Copy),
                        waits=[("TE", cnt)], inc=("SC", 1))
            mark("lg_free", "SC", sc)
            vm = P.emit("v", lambda v: v.max(out=top8.ap(), in_=lgt.ap()),
                        waits=[("SC", sc)], inc=("VE", 1))
            vd = P.emit("v", lambda v: v.tensor_tensor(m1d.ap(), top8.ap()[:, 0:1],
                        top8.ap()[:, 1:2], ALU.subtract), waits=[("VE", vm)], inc=("VE", 1))
            sg = P.emit("s", lambda s: s.activation(g1t.ap(), m1d.ap(), AF.Sigmoid),
                        waits=[("VE", vd)], inc=("SC", 1))
            vg = P.emit("v", lambda v: v.tensor_scalar(g2t.ap(), g1t.ap(), -1.0, 1.0,
                        ALU.mult, ALU.add), waits=[("SC", sg)], inc=("VE", 1))
            P.emit("v", lambda v: v.tensor_scalar(msk1.ap(), lgt.ap(), top8.ap()[:, 0:1],
                   None, ALU.is_equal), waits=[("VE", vm)], inc=("VE", 1))
            P.emit("v", lambda v: v.tensor_scalar(msk2.ap(), lgt.ap(), top8.ap()[:, 1:2],
                   None, ALU.is_equal), inc=("VE", 1))
            P.emit("v", lambda v: v.tensor_scalar(msk1.ap(), msk1.ap(), g1t.ap(),
                   None, ALU.mult), inc=("VE", 1))
            v4 = P.emit("v", lambda v, d=wful.ap()[:, ts * E:(ts + 1) * E]:
                        v.scalar_tensor_tensor(d, msk2.ap(), g2t.ap(), msk1.ap(),
                        ALU.mult, ALU.add), waits=[("VE", vg)], inc=("VE", 1))
            tc = P.emit("t", lambda t, i=wful.ap()[:, ts * E:(ts + 1) * E]:
                        t.transpose(PB[1].ap()[0:E, 0:128], i, eye_f.ap()),
                        waits=[("VE", v4)] + need("wt_free") + need("consts"),
                        inc=("TE", 1))
            sc2 = P.emit("s", lambda s, d=wtrow_full.ap()[0:8, ts * 128:(ts + 1) * 128]:
                         s.activation(d, PB[1].ap()[0:E, 0:128], AF.Copy),
                         waits=[("TE", tc)], inc=("SC", 1))
            mark("wt_free", "SC", sc2)
            mark(f"wtr_{ts}", "SC", sc2)
        if debug and g2 == 0:
            P.emit("g", lambda g: g.dma_start(taps["tap_w"], wful.ap()),
                   waits=[rd["wtr_3"]], inc=("DT", 16))
        wt = [rd[f"wtr_{ts}"] for ts in range(4)]
        tc = P.emit("t", lambda t: t.matmul(PB[2].ap(), eexp_sb.ap(), wtrow_full.ap()[0:8, 0:512],
                    start=True, stop=True), waits=wt + need("consts"), inc=("TE", 1))
        wsc = P.emit("s", lambda s: s.activation(wexp_sb.ap(), PB[2].ap(), AF.Copy),
                     waits=[("TE", tc)], inc=("SC", 1))
        for c in range(NC):
            lhs = a_sb[:, c * 128:(c + 1) * 128]
            rhs = x1b[:, c * GT:(c + 1) * GT]
            w = [rd[f"x1b_{c}"]] + (need("ffnw") + need("h_free") if c == 0 else [])
            cnt = P.emit("t", lambda t, l=lhs, r=rhs, st=(c == 0), sp=(c == NC - 1):
                         t.matmul(PB[1].ap(), l, r, start=st, stop=sp),
                         waits=w, inc=("TE", 1))
        hv = P.emit("v", lambda v: v.tensor_tensor(hwb.ap(), PB[1].ap(), wexp_sb.ap(),
                    ALU.mult), waits=[("TE", cnt), ("SC", wsc)], inc=("VE", 1))
        mark("h_free", "VE", hv)
        if debug and g2 == 0:
            P.emit("g", lambda g: g.dma_start(taps["tap_hw"], hwb.ap()),
                   waits=[("VE", hv)], inc=("DT", 16))
        ob2 = [PB[3], PB[1], PB[2], PB[0]]
        pend2 = None
        for oc in range(NC):
            pb = ob2[oc % 4]
            w = [("VE", hv)] + need(f"ob{oc % 4}_free") + \
                (need("bc_free") + need("lg_free") if oc % 4 == 3 else [])
            cnt = P.emit("t", lambda t, o=pb.ap(), l=bm_sb[:, oc * 128:(oc + 1) * 128]:
                         t.matmul(o, l, hwb.ap(), start=True, stop=True),
                         waits=w, inc=("TE", 1))
            if pend2 is not None:
                pend2()
                pend2 = None
            ap = AR2_f.ap()[:, oc * GT:(oc + 1) * GT]
            vcnt = P.emit("v", lambda v, a=ap, o=pb.ap():
                          v.scalar_tensor_tensor(a, o, FFN_SCALE, a, ALU.mult, ALU.add),
                          waits=[("TE", cnt), rd[f"lnapply_{oc}"]], inc=("VE", 1))
            mark(f"ob{oc % 4}_free", "VE", vcnt)
            scnt = P.emit("s", lambda s, a=ap: s.activation(lnsq.ap(), a, AF.Square),
                          waits=[("VE", vcnt)] + need("lnsq_free"), inc=("SC", 1))
            vcnt2 = P.emit("v", lambda v, a=ap: v.tensor_copy(lnbf.ap(), a),
                           waits=need("lnbf_free"), inc=("VE", 1))

            def stats2(oc=oc, scnt=scnt, vcnt2=vcnt2):
                c1 = P.emit("t", lambda t, st=(oc == 0), sp=(oc == NC - 1):
                            t.matmul(statb[0].ap()[0:1, :], onescol.ap(), lnbf.ap(),
                                     start=st, stop=sp), waits=[("VE", vcnt2)],
                            inc=("TE", 1))
                mark("lnbf_free", "TE", c1)
                c2 = P.emit("t", lambda t, st=(oc == 0), sp=(oc == NC - 1):
                            t.matmul(statb[1].ap()[0:1, :], onescol.ap(), lnsq.ap(),
                                     start=st, stop=sp), waits=[("SC", scnt)],
                            inc=("TE", 1))
                mark("lnsq_free", "TE", c2)
                mark("ln2stats", "TE", c2)
            pend2 = stats2
        pend2()
        pend2 = None
        ln_finish_apply("ln2", False, g2)
        for oc in range(NC):
            src = AR2_f.ap()[:, oc * GT:(oc + 1) * GT]
            dst = y_out[oc * 128:(oc + 1) * 128, g2 * GT:(g2 + 1) * GT]
            cnt = P.emit("y", lambda sy, d=dst, s=src: sy.dma_start(d, s),
                         waits=[rd[f"lnapply_{oc}"]], inc=(f"DY{oc % 4}", 16))
            ycnts[(g2, oc)] = cnt

    ycnts = {}
    ln_finish_apply("ln1", True, 0)
    if debug:
        P.emit("g", lambda g: g.dma_start(taps["tap_x1"], AR2_f.ap()),
               waits=need("lnapplied") + need("tapr1"), inc=("DT", 16))
        mark("tapx1", "DT", P.at("DT"))
        P.emit("v", lambda v: v.tensor_copy(srow[2].ap(), srow[0].ap()),
               waits=need("tapx1"), inc=("VE", 1))
    ffn_and_ln2_and_out(0)

    s1b, s2b = ln_finish("ln1b", PB[6].ap(), PTBf32.ap()[:, 0:512], M_sb, Rs_sb,
                         srow[0], srow[1], srow[2])
    dspall = ("DSP", P.at("DSP"))
    dyall = [(f"DY{k}", P.at(f"DY{k}")) for k in range(4)]
    for oc in range(NC):
        relw = [dspall] + dyall + need("lnapplied")
        P.emit("y", lambda sy, d=AR2_f.ap()[:, oc * GT:(oc + 1) * GT],
               s=spill[oc * 128:(oc + 1) * 128, :]:
               sy.dma_start(d, s), waits=relw, inc=("DRL", 16))
    reload_done = ("DRL", P.at("DRL"))
    ln_apply(True, 1, M_sb, Rs_sb, s1b, s2b, extra=[reload_done])
    ffn_and_ln2_and_out(1)

    fin = [(f"DY{k}", P.at(f"DY{k}")) for k in range(4)]
    if debug:
        fin += [("DT", P.at("DT")), ("DT2", P.at("DT2"))]
    P.emit("y", lambda sy: None, waits=fin)

    with contextlib.ExitStack() as st:
        semh = {name: st.enter_context(nc.semaphore(name)) for name in P.cnt}
        block = st.enter_context(nc.Block())
        engmap = {"t": block.tensor, "s": block.scalar, "v": block.vector,
                  "g": block.gpsimd, "y": block.sync}

        def run_queue(e):
            def body(eng):
                for fn, waits, inc in P.q[e]:
                    for sem, tgt in waits:
                        eng.wait_ge(semh[sem], tgt)
                    ins = fn(eng)
                    if inc is not None:
                        assert ins is not None
                        ins.then_inc(semh[inc[0]], inc[1])
            return body

        for e in P.ENGINES:
            engmap[e](run_queue(e))

    return nc


_NC_CACHE = {}

def _get_nc(debug=False):
    if debug not in _NC_CACHE:
        _NC_CACHE[debug] = build(debug)
    return _NC_CACHE[debug]


def make_inputs(x, wq, wk, wv, wo, router_w, router_b, A, Bm):
    f32 = np.float32
    wq_r = np.ascontiguousarray(
        np.asarray(wq).reshape(32, 128, 32, 128).transpose(2, 1, 0, 3)).reshape(NC * 128, NC * 128)
    wo_r = np.ascontiguousarray(
        np.asarray(wo).reshape(32, 128, 32, 128).transpose(2, 1, 0, 3)).reshape(NC * 128, NC * 128)
    wkv = np.ascontiguousarray(
        np.concatenate([wk, wv], axis=1).reshape(32, 128, 256).transpose(1, 0, 2)).reshape(128, NC * 256)
    rw_in = np.ascontiguousarray(
        np.asarray(router_w).T.reshape(32, 128, 8).transpose(1, 0, 2).reshape(128, 256))
    a_in = np.ascontiguousarray(
        np.asarray(A).reshape(128, 4096).T.reshape(32, 128, 128).transpose(1, 0, 2).reshape(128, 4096))
    bm_in = np.ascontiguousarray(np.asarray(Bm).transpose(0, 2, 1).reshape(128, 4096))
    eexp = np.zeros((8, 128), f32)
    for e in range(8):
        eexp[e, e * 16:(e + 1) * 16] = 1.0
    eye_b = np.eye(128, dtype=f32).astype(BF)
    eye_f = np.eye(128, dtype=f32)
    kt_i = np.arange(128)[:, None]
    qt_i = np.arange(GT)[None, :]
    masks = np.concatenate(
        [(kt_i + 128 * m <= qt_i).astype(f32) for m in range(4)], axis=1).astype(BF)

    shared = dict(wq_r=wq_r, wo_r=wo_r, wkv=wkv, rw_in=rw_in,
                  b_row=np.asarray(router_b).reshape(1, 8).astype(f32),
                  a_in=a_in, bm_in=bm_in, eexp=eexp, eye_b=eye_b, eye_f=eye_f,
                  masks=masks)

    in_maps = []
    xT = [np.ascontiguousarray(np.asarray(x)[b].T) for b in range(2)]
    for core in range(8):
        b, j = core // 4, core % 4
        qsA, qsB = 512 * j, 512 * (7 - j)
        xt = xT[b]
        xq = np.concatenate([xt[:, qsA:qsA + 512], xt[:, qsB:qsB + 512]], axis=1)
        xq = np.ascontiguousarray(
            xq.reshape(32, 128, 1024).transpose(1, 0, 2)).reshape(128, NC * 1024)
        za = np.zeros((4096, NA * 128 - 512 - qsA), f32)
        xkA = np.concatenate([xt[:, qsA:qsA + 512], xt[:, 0:qsA], za], axis=1)
        zb = np.zeros((4096, NB * 128 - 512 - qsB), f32)
        xkB = np.concatenate([xt[:, qsB:qsB + 512], xt[:, 0:qsB], zb], axis=1)
        def xk_r(xk, ntiles):
            a = xk.reshape(4, 8, 128, ntiles, 512).transpose(3, 0, 2, 1, 4)
            return np.ascontiguousarray(a).reshape(ntiles * 4 * 128, 4096)
        xkA = xk_r(xkA, NA // 4)
        xkB = xk_r(xkB, NB // 4)
        sb_ = np.full((NA + NB,), DEADB, f32)
        sb_[:4 * j + 4] = CBIAS
        sb_[NA:NA + 32 - 4 * j] = CBIAS
        sbias = np.ascontiguousarray(np.tile(sb_[None, :], (128, 1)))
        m = dict(shared)
        m.update(xq=xq, xkA=xkA, xkB=xkB, sbias=sbias)
        in_maps.append(m)
    return in_maps


def kernel(L0_attn_x, L0_attn_wq, L0_attn_wk, L0_attn_wv, L0_attn_wo,
           L0_ffn_router_w, L0_ffn_router_b, L0_ffn_A, L0_ffn_Bm, _debug=False):
    in_maps = make_inputs(L0_attn_x, L0_attn_wq, L0_attn_wk, L0_attn_wv, L0_attn_wo,
                          L0_ffn_router_w, L0_ffn_router_b, L0_ffn_A, L0_ffn_Bm)
    nc = _get_nc(_debug)
    res = run_bass_kernel_spmd(nc, in_maps, core_ids=list(range(8)))
    out = np.empty((2, 4096, 4096), np.float32)
    for core in range(8):
        b, j = core // 4, core % 4
        y = res.results[core]["y"]
        out[b, 512 * j:512 * j + 512] = y[:, 0:512].T
        out[b, 512 * (7 - j):512 * (7 - j) + 512] = y[:, 512:1024].T
    if _debug:
        return out, res
    return out


# version 13
# speedup vs baseline: 1.1831x; 1.1831x over previous
import contextlib
import numpy as np
import ml_dtypes
import concourse.bass as bass
import concourse.mybir as mybir
from concourse.bass_utils import run_bass_kernel_spmd

F32 = mybir.dt.float32
BF16 = mybir.dt.bfloat16
AF = mybir.ActivationFunctionType
ALU = mybir.AluOpType
BF = ml_dtypes.bfloat16

D = 4096
NC = 32
H = 32
GT = 512
NA, NB = 16, 32
SCALE = float(1.0 / np.sqrt(128.0))
CBIAS = -12.0
DEADB = -1e30
EPS = 1e-5
E = 8
FFN_SCALE = 0.5


class Prog:
    ENGINES = ("t", "s", "v", "g", "y")

    def __init__(self):
        self.q = {e: [] for e in self.ENGINES}
        self.cnt = {}
        self.waited = {e: {} for e in self.ENGINES}

    def inc(self, sem, n=1):
        self.cnt[sem] = self.cnt.get(sem, 0) + n
        return self.cnt[sem]

    def at(self, sem):
        return self.cnt.get(sem, 0)

    def emit(self, e, fn, waits=(), inc=None):
        w = []
        for sem, tgt in waits:
            if tgt <= 0:
                continue
            if self.waited[e].get(sem, -1) >= tgt:
                continue
            self.waited[e][sem] = tgt
            w.append((sem, tgt))
        cnt = None
        if inc is not None:
            cnt = self.inc(*inc)
        self.q[e].append((fn, w, inc))
        return cnt


def build(debug=False):
    nc = bass.Bass("TRN2", target_bir_lowering=False)

    xq = nc.dram_tensor("xq", [128, NC * 1024], F32, kind="ExternalInput").ap()
    xkA = nc.dram_tensor("xkA", [NA * 128, 4096], F32, kind="ExternalInput").ap()
    xkB = nc.dram_tensor("xkB", [NB * 128, 4096], F32, kind="ExternalInput").ap()
    sbias = nc.dram_tensor("sbias", [128, NA + NB], F32, kind="ExternalInput").ap()
    wq_r = nc.dram_tensor("wq_r", [NC * 128, NC * 128], F32, kind="ExternalInput").ap()
    wo_r = nc.dram_tensor("wo_r", [NC * 128, NC * 128], F32, kind="ExternalInput").ap()
    wkv = nc.dram_tensor("wkv", [128, NC * 256], F32, kind="ExternalInput").ap()
    rw_in = nc.dram_tensor("rw_in", [128, NC * E], F32, kind="ExternalInput").ap()
    b_row = nc.dram_tensor("b_row", [1, E], F32, kind="ExternalInput").ap()
    a_in = nc.dram_tensor("a_in", [128, NC * 128], F32, kind="ExternalInput").ap()
    bm_in = nc.dram_tensor("bm_in", [128, NC * 128], F32, kind="ExternalInput").ap()
    eexp_in = nc.dram_tensor("eexp", [E, 128], F32, kind="ExternalInput").ap()
    eye_b_in = nc.dram_tensor("eye_b", [128, 128], BF16, kind="ExternalInput").ap()
    eye_f_in = nc.dram_tensor("eye_f", [128, 128], F32, kind="ExternalInput").ap()
    masks_in = nc.dram_tensor("masks", [128, 4 * GT], BF16, kind="ExternalInput").ap()

    y_out = nc.dram_tensor("y", [D, 1024], F32, kind="ExternalOutput").ap()
    spill = nc.dram_tensor("r1spill", [NC * 128, GT], F32, kind="Internal").ap()

    taps = {}
    if debug:
        for name, shape, dt in [
            ("tap_kA", [128, NA * 128], F32), ("tap_vA", [128, NA * 128], F32),
            ("tap_kB", [128, NB * 128], F32), ("tap_vB", [128, NB * 128], F32),
            ("tap_q", [128, H * 1024], F32), ("tap_attn", [128, H * 1024], F32),
            ("tap_r1", [128, NC * GT], F32), ("tap_x1", [128, NC * GT], F32),
            ("tap_w", [128, 4 * E], F32), ("tap_hw", [128, GT], F32),
            ("tap_r1b", [128, NC * GT], F32),
        ]:
            taps[name] = nc.dram_tensor(name, shape, dt, kind="ExternalOutput").ap()

    QA = nc.alloc_sbuf_tensor("QA", [128, H * 1024], BF16)
    AR2_f = nc.alloc_sbuf_tensor("AR2", [128, NC * GT], F32)
    XO = AR2_f.bitcast(BF16)
    KV = nc.alloc_sbuf_tensor("KV", [128, (NA + NB) * 128 * 2], BF16)
    kTA = KV.ap()[:, 0:NA * 128]
    kTB = KV.ap()[:, NA * 128:(NA + NB) * 128]
    VA = KV.ap()[:, (NA + NB) * 128:(2 * NA + NB) * 128]
    VB = KV.ap()[:, (2 * NA + NB) * 128:]
    WBt = [nc.alloc_sbuf_tensor(f"WB{i}", [128, 16 * 128], BF16) for i in range(2)]
    XKBt = [nc.alloc_sbuf_tensor(f"XKB{i}", [128, 8 * GT], BF16) for i in range(2)]
    PTt = [nc.alloc_sbuf_tensor(f"PT{i}", [128, GT], BF16) for i in range(4)]
    PTD = nc.alloc_sbuf_tensor("PTD", [128, GT], BF16)
    vstage = PTD
    masks = nc.alloc_sbuf_tensor("masks_sb", [128, 4 * GT], BF16)
    sbias_sb = nc.alloc_sbuf_tensor("sbias_sb", [128, NA + NB], F32)
    ones128 = nc.alloc_sbuf_tensor("ones128", [128, 128], BF16)
    onescol = nc.alloc_sbuf_tensor("onescol", [128, 1], BF16)
    onesrow_f = nc.alloc_sbuf_tensor("onesrow_f", [1, 128], F32)
    eye_b = nc.alloc_sbuf_tensor("eye_b_sb", [128, 128], BF16)
    eye_f = nc.alloc_sbuf_tensor("eye_f_sb", [128, 128], F32)
    wkv_sb = QA.ap()[:, 0:NC * 256]
    a_sb = KV.ap()[:, 0:NC * 128]
    bm_sb = KV.ap()[:, NC * 128:2 * NC * 128]
    rw_sb = KV.ap()[:, 2 * NC * 128:2 * NC * 128 + NC * E]
    b_sb = nc.alloc_sbuf_tensor("b_sb", [1, E], F32)
    eexp_sb = nc.alloc_sbuf_tensor("eexp_sb", [E, 128], F32)
    RESt = [nc.alloc_sbuf_tensor(f"RES{i}", [128, GT], F32) for i in range(2)]
    R1St = [nc.alloc_sbuf_tensor(f"R1S{i}", [128, GT], F32) for i in range(2)]
    lnbf = PTt[1]
    lnsq = PTt[2]
    M_sb = nc.alloc_sbuf_tensor("M_sb", [128, GT], F32)
    Rs_sb = nc.alloc_sbuf_tensor("Rs_sb", [128, GT], F32)
    rinv_ap = XKBt[1].bitcast(F32).ap()[:, 0:GT]
    wexp_sb = nc.alloc_sbuf_tensor("wexp_sb", [128, GT], F32)
    hwb = PTt[0]
    lgt = nc.alloc_sbuf_tensor("lgt", [128, E], F32)
    top8 = nc.alloc_sbuf_tensor("top8", [128, E], F32)
    m1d = nc.alloc_sbuf_tensor("m1d", [128, 1], F32)
    g1t = nc.alloc_sbuf_tensor("g1t", [128, 1], F32)
    g2t = nc.alloc_sbuf_tensor("g2t", [128, 1], F32)
    msk1 = nc.alloc_sbuf_tensor("msk1", [128, E], F32)
    msk2 = nc.alloc_sbuf_tensor("msk2", [128, E], F32)
    wful = nc.alloc_sbuf_tensor("wful", [128, 4 * E], F32)
    wtrow_full = masks.bitcast(F32)
    srow = [nc.alloc_sbuf_tensor(f"srow{i}", [1, GT], F32) for i in range(3)]

    assert nc.sbuf_bytes_remaining >= 0, f"SBUF over: {nc.sbuf_bytes_remaining}"

    PB = [nc.alloc_psum_tensor(f"PB{i}", [128, 512], F32) for i in range(7)]
    PTB = nc.alloc_psum_tensor("PTB", [128, 1024], BF16)
    assert nc.psum_banks_remaining == 0

    P = Prog()
    rd = {}

    def mark(name, sem, cnt):
        rd[name] = (sem, cnt)

    def need(name):
        return [rd[name]] if name in rd else []

    P.emit("g", lambda g: g.memset(ones128.ap(), 1.0))
    P.emit("g", lambda g: g.memset(onescol.ap(), 1.0))
    P.emit("g", lambda g: g.memset(onesrow_f.ap(), 1.0), inc=("GE", 1))
    mark("consts_g", "GE", P.at("GE"))

    for dst, src in [(masks.ap(), masks_in), (sbias_sb.ap(), sbias),
                     (eye_b.ap(), eye_b_in), (eye_f.ap(), eye_f_in),
                     (eexp_sb.ap(), eexp_in), (b_sb.ap(), b_row)]:
        P.emit("y", lambda sy, d=dst, s=src: sy.dma_start(d, s), inc=("DC", 16))
    mark("consts", "DC", P.at("DC"))

    P.emit("g", lambda g: g.dma_start(wkv_sb, wkv), inc=("DKV", 16))
    mark("wkv", "DKV", P.at("DKV"))

    for hf in range(4):
        src = xq[:, hf * 8 * 1024:(hf + 1) * 8 * 1024]
        dst = XO.ap()[:, hf * 8 * 1024:(hf + 1) * 8 * 1024]
        P.emit("g", lambda g, d=dst, s=src: g.dma_start(d, s), inc=("DXO", 16))
    mark("xo", "DXO", P.at("DXO"))

    def p2_quarter(xk, kT, V, ti, qtr, tag, ntiles):
        qi = ti * 4 + qtr
        bi = qi % 2
        buf = XKBt[bi]
        src = xk[qi * 128:(qi + 1) * 128, :]
        dsem = f"DXK{bi}"
        w = need(f"xkb{bi}_free")
        P.emit("g", lambda g, d=buf.ap(), s=src: g.dma_start(d, s), waits=w,
               inc=(dsem, 16))
        dtot = P.at(dsem)
        for which, pb, wofs in (("k", PB[4], 0), ("v", PB[5], 128)):
            w = [(dsem, dtot)] + need("wkv") + need(f"pb{which}_free")
            for ci in range(8):
                c = qtr * 8 + ci
                lhs = wkv_sb[:, c * 256 + wofs:c * 256 + wofs + 128]
                rhs = buf.ap()[:, ci * GT:(ci + 1) * GT]
                cnt = P.emit("t", lambda t, o=pb.ap(), l=lhs, r=rhs,
                             st=(c == 0), sp=(c == NC - 1):
                             t.matmul(o, l, r, start=st, stop=sp),
                             waits=(w if ci == 0 else ()), inc=("TE", 1))
            mark(f"p2{which}", "TE", cnt)
        mark(f"xkb{bi}_free", "TE", P.at("TE"))
        if qtr == 3:
            cnt = P.emit("s", lambda s, d=kT[:, ti * GT:(ti + 1) * GT], o=PB[4].ap():
                         s.activation(d, o, AF.Copy), waits=need("p2k"), inc=("SC", 1))
            mark("pbk_free", "SC", cnt)
            vst = P.emit("s", lambda s, d=vstage.ap(), o=PB[5].ap():
                         s.activation(d, o, AF.Copy), waits=need("p2v"), inc=("SC", 1))
            mark("pbv_free", "SC", vst)
            for cs in range(4):
                w = [("SC", vst)] + need("consts") + (need("ptb_free") if cs == 0 else [])
                cnt = P.emit("t", lambda t, o=PTB.ap()[:, cs * 128:(cs + 1) * 128],
                             i=vstage.ap()[:, cs * 128:(cs + 1) * 128], e=eye_b.ap():
                             t.transpose(o, i, e), waits=w, inc=("TE", 1))
            mark("vtr", "TE", cnt)
            cnt = P.emit("s", lambda s, d=V[:, ti * GT:(ti + 1) * GT],
                         o=PTB.ap()[:, 0:512]:
                         s.activation(d, o, AF.Copy), waits=need("vtr"), inc=("SC", 1))
            mark("ptb_free", "SC", cnt)
            if ti == ntiles - 1:
                mark(f"kv{tag}", "SC", P.at("SC"))

    qbank = {0: [PB[0], PB[2]], 1: [PB[1], PB[3]]}

    def p1_half(dq, half):
        bi = (dq * 2 + half) % 2
        buf = WBt[bi]
        dsem = f"DWB{bi}"
        w = need(f"wb{bi}_free")
        src = wq_r[dq * 128:(dq + 1) * 128, half * 2048:(half + 1) * 2048]
        P.emit("g", lambda g, d=buf.ap(), s=src: g.dma_start(d, s), waits=w,
               inc=(dsem, 16))
        dtot = P.at(dsem)
        for g2 in range(2):
            pb = qbank[g2][dq % 2]
            w = [(dsem, dtot)] + need("xo") + need(f"qb{g2}{dq % 2}_free")
            for ci in range(16):
                c = half * 16 + ci
                lhs = buf.ap()[:, ci * 128:(ci + 1) * 128]
                rhs = XO.ap()[:, c * 1024 + g2 * GT:c * 1024 + g2 * GT + GT]
                cnt = P.emit("t", lambda t, o=pb.ap(), l=lhs, r=rhs,
                             st=(c == 0), sp=(c == NC - 1):
                             t.matmul(o, l, r, start=st, stop=sp),
                             waits=(w if ci == 0 else ()), inc=("TE", 1))
        mark(f"wb{bi}_free", "TE", cnt)
        if half == 1:
            for g2 in range(2):
                pb = qbank[g2][dq % 2]
                dst = QA.ap()[:, dq * 1024 + g2 * GT: dq * 1024 + g2 * GT + GT]
                scnt = P.emit("s", lambda s, d=dst, o=pb.ap():
                              s.activation(d, o, AF.Copy), waits=[("TE", cnt)],
                              inc=("SC", 1))
                mark(f"qb{g2}{dq % 2}_free", "SC", scnt)

    p2_units = [("A", xkA, kTA, VA, NA // 4, ti, qtr)
                for ti in range(NA // 4) for qtr in range(4)]
    p2_units += [("B", xkB, kTB, VB, NB // 4, ti, qtr)
                 for ti in range(NB // 4) for qtr in range(4)]
    p1_order = list(range(8, NC)) + list(range(8))
    p1_units = [(dq, half) for dq in p1_order for half in range(2)]
    ip2 = ip1 = 0
    while ip2 < len(p2_units) or ip1 < len(p1_units):
        if ip2 < len(p2_units):
            tag, xk_, kT_, V_, nt_, ti_, qtr_ = p2_units[ip2]
            p2_quarter(xk_, kT_, V_, ti_, qtr_, tag, nt_)
            ip2 += 1
        if ip1 < len(p1_units):
            p1_half(*p1_units[ip1])
            ip1 += 1
    mark("qT", "SC", P.at("SC"))

    if debug:
        P.emit("g", lambda g: g.dma_start(taps["tap_kA"], kTA), waits=need("kvA"), inc=("DT", 16))
        P.emit("g", lambda g: g.dma_start(taps["tap_vA"], VA), waits=need("kvA"), inc=("DT", 16))
        P.emit("g", lambda g: g.dma_start(taps["tap_kB"], kTB), waits=need("kvB"), inc=("DT", 16))
        P.emit("g", lambda g: g.dma_start(taps["tap_vB"], VB), waits=need("kvB"), inc=("DT", 16))
        P.emit("g", lambda g: g.dma_start(taps["tap_q"], QA.ap()), waits=need("qT"), inc=("DT", 16))
        mark("tapq", "DT", P.at("DT"))

    srot = [PB[0], PB[1], PB[2]]
    obank = [PB[3], PB[4]]
    rbank = [PB[5], PB[6]]
    for k in range(2):
        mark(f"obank{k}_free", "SC", P.at("SC"))
        mark(f"rbank{k}_free", "SC", P.at("SC"))
    slots = []
    pairidx = 0
    for (Ng, kT, V, gbase, g2) in [(NA, kTA, VA, 0, 0), (NB, kTB, VB, NA, 1)]:
        for h in range(H):
            ctx = dict(Ng=Ng, kT=kT, V=V, gbase=gbase, pidx=pairidx,
                       qsl=QA.ap()[:, h * 1024 + g2 * GT: h * 1024 + g2 * GT + GT],
                       ob=obank[pairidx % 2], rb=rbank[pairidx % 2], pready={})
            for s in range(Ng):
                slots.append((ctx, s))
            pairidx += 1

    pending_tails = []

    def emit_S_exp(ctx, s, gi):
        bank = srot[gi % 3]
        w = need(f"srot{gi % 3}_free") + (need("qT") if ctx["pidx"] == 0 else [])
        scnt = P.emit("t", lambda t, o=bank.ap(),
                      l=ctx["kT"][:, s * 128:(s + 1) * 128], r=ctx["qsl"]:
                      t.matmul(o, l, r, start=True, stop=True),
                      waits=w, inc=("TE", 1))
        dstP = PTD.ap() if s < 4 else PTt[s % 4].ap()
        w = [("TE", scnt)] + need("consts") + \
            (need("ptd_free") if s < 4 else need(f"pt{s % 4}_free"))
        ecnt = P.emit("s", lambda ss, d=dstP, o=bank.ap(),
                      b=sbias_sb.ap()[:, ctx["gbase"] + s:ctx["gbase"] + s + 1]:
                      ss.activation(d, o, AF.Exp, bias=b, scale=SCALE),
                      waits=w, inc=("SC", 1))
        mark(f"srot{gi % 3}_free", "SC", ecnt)
        if s < 4:
            w = [("SC", ecnt)] + need(f"pt{s % 4}_free") + need("consts")
            vcnt = P.emit("v", lambda v, d=PTt[s % 4].ap(), a=PTD.ap(),
                          m=masks.ap()[:, s * GT:(s + 1) * GT]:
                          v.tensor_tensor(d, a, m, ALU.mult),
                          waits=w, inc=("VE", 1))
            mark("ptd_free", "VE", vcnt)
            ctx["pready"][s] = ("VE", vcnt)
        else:
            ctx["pready"][s] = ("SC", ecnt)

    def emit_RPV(ctx, s):
        Ng, pidx, rb, ob = ctx["Ng"], ctx["pidx"], ctx["rb"], ctx["ob"]
        w = [ctx["pready"][s]] + ([need("consts_g")[0]] if pidx == 0 and s == 0 else []) \
            + (need(f"rbank{pidx % 2}_free") if s == 0 else [])
        P.emit("t", lambda t, o=rb.ap(), l=ones128.ap(), r=PTt[s % 4].ap(),
               st=(s == 0), sp=(s == Ng - 1):
               t.matmul(o, l, r, start=st, stop=sp), waits=w, inc=("TE", 1))
        w = need(f"obank{pidx % 2}_free") if s == 0 else []
        cnt = P.emit("t", lambda t, o=ob.ap(), l=ctx["V"][:, s * 128:(s + 1) * 128],
                     r=PTt[s % 4].ap(), st=(s == 0), sp=(s == Ng - 1):
                     t.matmul(o, l, r, start=st, stop=sp), waits=w, inc=("TE", 1))
        mark(f"pt{s % 4}_free", "TE", cnt)
        if s == Ng - 1:
            ostop = P.at("TE")

            def tail(ostop=ostop, rb=rb, ob=ob, qsl=ctx["qsl"], pidx=pidx):
                rcnt = P.emit("v", lambda v, o=rb.ap(): v.reciprocal(rinv_ap, o),
                              waits=[("TE", ostop)], inc=("VE", 1))
                mark(f"rbank{pidx % 2}_free", "VE", rcnt)
                mcnt = P.emit("v", lambda v, d=qsl, o=ob.ap():
                              v.tensor_tensor(d, o, rinv_ap, ALU.mult),
                              waits=[("VE", rcnt)], inc=("VE", 1))
                mark(f"obank{pidx % 2}_free", "VE", mcnt)
            pending_tails.append(tail)

    for gi, (ctx, s) in enumerate(slots):
        emit_S_exp(ctx, s, gi)
        if s == 4 and pending_tails:
            pending_tails.pop(0)()
        if gi >= 2:
            emit_RPV(*slots[gi - 2])
    emit_RPV(*slots[-2])
    emit_RPV(*slots[-1])
    for t_ in pending_tails:
        t_()
    pending_tails = []
    mark("attnT", "VE", P.at("VE"))
    p3_te = P.at("TE")

    if debug:
        P.emit("g", lambda g: g.dma_start(taps["tap_attn"], QA.ap()),
               waits=need("attnT") + need("tapq"), inc=("DT", 16))
        mark("tapattn", "DT", P.at("DT"))

    ybank = {0: [PB[0], PB[1]], 1: [PB[2], PB[3]]}
    statb = [PB[4], PB[5]]
    PTBf32 = PTB.bitcast(F32)
    lnbf2 = PTD
    lnsq2 = PTt[3]
    pending_stats = pending_stats2 = None
    for oc in range(NC):
        for half in range(2):
            bi = (oc * 2 + half) % 2
            buf = WBt[bi]
            dsem = f"DWB{bi}"
            w = need(f"wb{bi}_free")
            src = wo_r[oc * 128:(oc + 1) * 128, half * 2048:(half + 1) * 2048]
            P.emit("g", lambda g, d=buf.ap(), s=src: g.dma_start(d, s), waits=w,
                   inc=(dsem, 16))
            dtot = P.at(dsem)
            for g2 in range(2):
                pb = ybank[g2][oc % 2]
                w = [(dsem, dtot)] + need("attnT") + need(f"yb{g2}{oc % 2}_free")
                for ci in range(16):
                    c = half * 16 + ci
                    lhs = buf.ap()[:, ci * 128:(ci + 1) * 128]
                    rhs = QA.ap()[:, c * 1024 + g2 * GT:c * 1024 + g2 * GT + GT]
                    cnt = P.emit("t", lambda t, o=pb.ap(), l=lhs, r=rhs,
                                 st=(c == 0), sp=(c == NC - 1):
                                 t.matmul(o, l, r, start=st, stop=sp),
                                 waits=(w if ci == 0 else ()), inc=("TE", 1))
            mark(f"wb{bi}_free", "TE", cnt)
        if pending_stats is not None:
            pending_stats()
            pending_stats = None
        if pending_stats2 is not None:
            pending_stats2()
            pending_stats2 = None
        for g2 in range(2):
            rbuf = RESt[g2]
            rsem = f"DR{g2}"
            w = need(f"res{g2}_free")
            P.emit("y", lambda sy, d=rbuf.ap(),
                   s=xq[:, oc * 1024 + g2 * GT:oc * 1024 + (g2 + 1) * GT]:
                   sy.dma_start(d, s), waits=w, inc=(rsem, 16))
            rtot = P.at(rsem)
            pb = ybank[g2][oc % 2]
            if g2 == 0:
                dst = AR2_f.ap()[:, oc * GT:(oc + 1) * GT]
            else:
                dst = R1St[oc % 2].ap()
            w = [("TE", cnt), (rsem, rtot)] + \
                (need(f"r1s{oc % 2}_free") + need(f"r1s{oc % 2}_read") if g2 == 1 else [])
            vcnt = P.emit("v", lambda v, d=dst, o=pb.ap(), r=rbuf.ap():
                          v.tensor_tensor(d, o, r, ALU.add), waits=w, inc=("VE", 1))
            mark(f"res{g2}_free", "VE", vcnt)
            mark(f"yb{g2}{oc % 2}_free", "VE", vcnt)
            if g2 == 0:
                scnt = P.emit("s", lambda s, a=dst: s.activation(lnsq.ap(), a, AF.Square),
                              waits=[("VE", vcnt)] + need("lnsq_free"), inc=("SC", 1))
                vcnt2 = P.emit("v", lambda v, a=dst: v.tensor_copy(lnbf.ap(), a),
                               waits=need("lnbf_free"), inc=("VE", 1))

                def stats(oc=oc, scnt=scnt, vcnt2=vcnt2):
                    c1 = P.emit("t", lambda t, st=(oc == 0), sp=(oc == NC - 1):
                                t.matmul(statb[0].ap()[0:1, :], onescol.ap(), lnbf.ap(),
                                         start=st, stop=sp),
                                waits=[("VE", vcnt2)] + need("consts_g"), inc=("TE", 1))
                    mark("lnbf_free", "TE", c1)
                    c2 = P.emit("t", lambda t, st=(oc == 0), sp=(oc == NC - 1):
                                t.matmul(statb[1].ap()[0:1, :], onescol.ap(), lnsq.ap(),
                                         start=st, stop=sp), waits=[("SC", scnt)],
                                inc=("TE", 1))
                    mark("lnsq_free", "TE", c2)
                    mark("ln1stats", "TE", c2)
                pending_stats = stats
            else:
                ycnt = P.emit("y", lambda sy, d=spill[oc * 128:(oc + 1) * 128, :], s=dst:
                              sy.dma_start(d, s), waits=[("VE", vcnt)], inc=("DSP", 16))
                mark(f"r1s{oc % 2}_free", "DSP", ycnt)
                scnt2 = P.emit("s", lambda s, a=dst: s.activation(lnsq2.ap(), a, AF.Square),
                               waits=[("VE", vcnt)] + need("lnsq2_free"), inc=("SC", 1))
                mark(f"r1s{oc % 2}_read", "SC", scnt2)
                vcnt3 = P.emit("v", lambda v, a=dst: v.tensor_copy(lnbf2.ap(), a),
                               waits=need("lnbf2_free"), inc=("VE", 1))

                def statsb(oc=oc, scnt2=scnt2, vcnt3=vcnt3):
                    c1 = P.emit("t", lambda t, st=(oc == 0), sp=(oc == NC - 1):
                                t.matmul(PB[6].ap()[0:1, :], onescol.ap(), lnbf2.ap(),
                                         start=st, stop=sp),
                                waits=[("VE", vcnt3)], inc=("TE", 1))
                    mark("lnbf2_free", "TE", c1)
                    c2 = P.emit("t", lambda t, st=(oc == 0), sp=(oc == NC - 1):
                                t.matmul(PTBf32.ap()[0:1, 0:512], onescol.ap(), lnsq2.ap(),
                                         start=st, stop=sp), waits=[("SC", scnt2)],
                                inc=("TE", 1))
                    mark("lnsq2_free", "TE", c2)
                    mark("ln1bstats", "TE", c2)
                pending_stats2 = statsb
    for ps in (pending_stats, pending_stats2):
        if ps is not None:
            ps()
    pending_stats = pending_stats2 = None
    mark("p4_te", "TE", P.at("TE"))
    mark("p4", "VE", P.at("VE"))

    for i, (dstw, srcw) in enumerate([(a_sb, a_in), (bm_sb, bm_in), (rw_sb, rw_in)]):
        w = [("TE", p3_te)] if i == 0 else []
        P.emit("g", lambda g, d=dstw, s=srcw: g.dma_start(d, s), waits=w, inc=("DF", 16))
    mark("ffnw", "DF", P.at("DF"))

    if debug:
        P.emit("g", lambda g: g.dma_start(taps["tap_r1"], AR2_f.ap()),
               waits=need("p4") + need("tapattn"), inc=("DT", 16))
        mark("tapr1", "DT", P.at("DT"))
        P.emit("y", lambda sy: sy.dma_start(
               taps["tap_r1b"].rearrange("p (c n) -> p c n", n=GT),
               spill.rearrange("(c p) n -> p c n", p=128)),
               waits=[("DSP", P.at("DSP"))], inc=("DT2", 16))

    def ln_finish(tag, sA, sB, Mdst, Rdst, r0, r1_, r2_, extra=()):
        sem_stats = need(f"{tag}stats")
        c = P.emit("v", lambda v: v.tensor_scalar(r0.ap(), sA[0:1, :],
                   1.0 / D, None, ALU.mult), waits=sem_stats + list(extra), inc=("VE", 1))
        c = P.emit("v", lambda v: v.tensor_scalar(r1_.ap(), sB[0:1, :],
                   1.0 / D, EPS, ALU.mult, ALU.add), inc=("VE", 1))
        c = P.emit("v", lambda v: v.tensor_tensor(r2_.ap(), r0.ap(),
                   r0.ap(), ALU.mult), inc=("VE", 1))
        c = P.emit("v", lambda v: v.tensor_tensor(r1_.ap(), r1_.ap(),
                   r2_.ap(), ALU.subtract), inc=("VE", 1))
        mark(f"{tag}statdone", "VE", c)
        sc = P.emit("s", lambda s: s.activation(r2_.ap(), r1_.ap(), AF.Sqrt),
                    waits=[("VE", c)], inc=("SC", 1))
        rc = P.emit("v", lambda v: v.reciprocal(r1_.ap(), r2_.ap()),
                    waits=[("SC", sc)], inc=("VE", 1))
        b1 = P.emit("t", lambda t: t.matmul(PB[0].ap(), onesrow_f.ap(), r0.ap(),
                    start=True, stop=True),
                    waits=[("VE", c)] + need("consts_g") + need("bc_free"), inc=("TE", 1))
        s1 = P.emit("s", lambda s: s.activation(Mdst.ap(), PB[0].ap(), AF.Copy),
                    waits=[("TE", b1)], inc=("SC", 1))
        b2 = P.emit("t", lambda t: t.matmul(PB[0].ap(), onesrow_f.ap(), r1_.ap(),
                    start=True, stop=True), waits=[("VE", rc), ("SC", s1)], inc=("TE", 1))
        s2 = P.emit("s", lambda s: s.activation(Rdst.ap(), PB[0].ap(), AF.Copy),
                    waits=[("TE", b2)], inc=("SC", 1))
        mark("bc_free", "SC", s2)
        return s1, s2

    def ln_apply(x1_to_QA, g2, Msb, Rsb, s1, s2, extra=()):
        for c_ in range(NC):
            ap = AR2_f.ap()[:, c_ * GT:(c_ + 1) * GT]
            P.emit("v", lambda v, a=ap: v.tensor_tensor(a, a, Msb.ap(), ALU.subtract),
                   waits=[("SC", s1), ("SC", s2)] + list(extra), inc=("VE", 1))
            v2 = P.emit("v", lambda v, a=ap: v.tensor_tensor(a, a, Rsb.ap(), ALU.mult),
                        inc=("VE", 1))
            mark(f"lnapply_{c_}", "VE", v2)
            if x1_to_QA:
                dst = QA.ap()[:, g2 * (H * 512) + c_ * GT: g2 * (H * 512) + (c_ + 1) * GT]
                sc2 = P.emit("s", lambda s, d=dst, a=ap: s.activation(d, a, AF.Copy),
                             waits=[("VE", v2)] + need("p4_te"), inc=("SC", 1))
                mark(f"x1b_{c_}", "SC", sc2)
        mark("lnapplied", "VE", P.at("VE"))

    def ln_finish_apply(tag, x1_to_QA, g2):
        s1, s2 = ln_finish(tag, statb[0].ap(), statb[1].ap(), M_sb, Rs_sb,
                           srow[0], srow[1], srow[2])
        ln_apply(x1_to_QA, g2, M_sb, Rs_sb, s1, s2)

    def ffn_and_ln2_and_out(g2):
        x1b = QA.ap()[:, g2 * (H * 512):(g2 + 1) * (H * 512)]
        for ts in range(4):
            for c in range(NC):
                w = need("ffnw") + [rd[f"x1b_{c}"]] + (need("lg_free") if c == 0 else [])
                lhs = x1b[:, c * GT + ts * 128: c * GT + ts * 128 + 128]
                rhs = rw_sb[:, c * E:(c + 1) * E]
                P.emit("t", lambda t, o=PB[0].ap()[:, 0:E], l=lhs, r=rhs, st=(c == 0):
                       t.matmul(o, l, r, start=st, stop=False),
                       waits=w, inc=("TE", 1))
            cnt = P.emit("t", lambda t: t.matmul(PB[0].ap()[:, 0:E], onesrow_f.ap(),
                         b_sb.ap(), start=False, stop=True),
                         waits=need("consts") + need("consts_g"), inc=("TE", 1))
            sc = P.emit("s", lambda s: s.activation(lgt.ap(), PB[0].ap()[:, 0:E], AF.Copy),
                        waits=[("TE", cnt)], inc=("SC", 1))
            mark("lg_free", "SC", sc)
            vm = P.emit("v", lambda v: v.max(out=top8.ap(), in_=lgt.ap()),
                        waits=[("SC", sc)], inc=("VE", 1))
            vd = P.emit("v", lambda v: v.tensor_tensor(m1d.ap(), top8.ap()[:, 0:1],
                        top8.ap()[:, 1:2], ALU.subtract), waits=[("VE", vm)], inc=("VE", 1))
            sg = P.emit("s", lambda s: s.activation(g1t.ap(), m1d.ap(), AF.Sigmoid),
                        waits=[("VE", vd)], inc=("SC", 1))
            vg = P.emit("v", lambda v: v.tensor_scalar(g2t.ap(), g1t.ap(), -1.0, 1.0,
                        ALU.mult, ALU.add), waits=[("SC", sg)], inc=("VE", 1))
            P.emit("v", lambda v: v.tensor_scalar(msk1.ap(), lgt.ap(), top8.ap()[:, 0:1],
                   None, ALU.is_equal), waits=[("VE", vm)], inc=("VE", 1))
            P.emit("v", lambda v: v.tensor_scalar(msk2.ap(), lgt.ap(), top8.ap()[:, 1:2],
                   None, ALU.is_equal), inc=("VE", 1))
            P.emit("v", lambda v: v.tensor_scalar(msk1.ap(), msk1.ap(), g1t.ap(),
                   None, ALU.mult), inc=("VE", 1))
            v4 = P.emit("v", lambda v, d=wful.ap()[:, ts * E:(ts + 1) * E]:
                        v.scalar_tensor_tensor(d, msk2.ap(), g2t.ap(), msk1.ap(),
                        ALU.mult, ALU.add), waits=[("VE", vg)], inc=("VE", 1))
            tc = P.emit("t", lambda t, i=wful.ap()[:, ts * E:(ts + 1) * E]:
                        t.transpose(PB[1].ap()[0:E, 0:128], i, eye_f.ap()),
                        waits=[("VE", v4)] + need("wt_free") + need("consts"),
                        inc=("TE", 1))
            sc2 = P.emit("s", lambda s, d=wtrow_full.ap()[0:8, ts * 128:(ts + 1) * 128]:
                         s.activation(d, PB[1].ap()[0:E, 0:128], AF.Copy),
                         waits=[("TE", tc)], inc=("SC", 1))
            mark("wt_free", "SC", sc2)
            mark(f"wtr_{ts}", "SC", sc2)
        if debug and g2 == 0:
            P.emit("g", lambda g: g.dma_start(taps["tap_w"], wful.ap()),
                   waits=[rd["wtr_3"]], inc=("DT", 16))
        wt = [rd[f"wtr_{ts}"] for ts in range(4)]
        tc = P.emit("t", lambda t: t.matmul(PB[2].ap(), eexp_sb.ap(), wtrow_full.ap()[0:8, 0:512],
                    start=True, stop=True), waits=wt + need("consts"), inc=("TE", 1))
        wsc = P.emit("s", lambda s: s.activation(wexp_sb.ap(), PB[2].ap(), AF.Copy),
                     waits=[("TE", tc)], inc=("SC", 1))
        for c in range(NC):
            lhs = a_sb[:, c * 128:(c + 1) * 128]
            rhs = x1b[:, c * GT:(c + 1) * GT]
            w = [rd[f"x1b_{c}"]] + (need("ffnw") + need("h_free") if c == 0 else [])
            cnt = P.emit("t", lambda t, l=lhs, r=rhs, st=(c == 0), sp=(c == NC - 1):
                         t.matmul(PB[1].ap(), l, r, start=st, stop=sp),
                         waits=w, inc=("TE", 1))
        hv = P.emit("v", lambda v: v.tensor_tensor(hwb.ap(), PB[1].ap(), wexp_sb.ap(),
                    ALU.mult), waits=[("TE", cnt), ("SC", wsc)], inc=("VE", 1))
        mark("h_free", "VE", hv)
        if debug and g2 == 0:
            P.emit("g", lambda g: g.dma_start(taps["tap_hw"], hwb.ap()),
                   waits=[("VE", hv)], inc=("DT", 16))
        ob2 = [PB[3], PB[1], PB[2], PB[0]]
        pend2 = None
        for oc in range(NC):
            pb = ob2[oc % 4]
            w = [("VE", hv)] + need(f"ob{oc % 4}_free") + \
                (need("bc_free") + need("lg_free") if oc % 4 == 3 else [])
            cnt = P.emit("t", lambda t, o=pb.ap(), l=bm_sb[:, oc * 128:(oc + 1) * 128]:
                         t.matmul(o, l, hwb.ap(), start=True, stop=True),
                         waits=w, inc=("TE", 1))
            if pend2 is not None:
                pend2()
                pend2 = None
            ap = AR2_f.ap()[:, oc * GT:(oc + 1) * GT]
            vcnt = P.emit("v", lambda v, a=ap, o=pb.ap():
                          v.scalar_tensor_tensor(a, o, FFN_SCALE, a, ALU.mult, ALU.add),
                          waits=[("TE", cnt), rd[f"lnapply_{oc}"]], inc=("VE", 1))
            mark(f"ob{oc % 4}_free", "VE", vcnt)
            scnt = P.emit("s", lambda s, a=ap: s.activation(lnsq.ap(), a, AF.Square),
                          waits=[("VE", vcnt)] + need("lnsq_free"), inc=("SC", 1))
            vcnt2 = P.emit("v", lambda v, a=ap: v.tensor_copy(lnbf.ap(), a),
                           waits=need("lnbf_free"), inc=("VE", 1))

            def stats2(oc=oc, scnt=scnt, vcnt2=vcnt2):
                c1 = P.emit("t", lambda t, st=(oc == 0), sp=(oc == NC - 1):
                            t.matmul(statb[0].ap()[0:1, :], onescol.ap(), lnbf.ap(),
                                     start=st, stop=sp), waits=[("VE", vcnt2)],
                            inc=("TE", 1))
                mark("lnbf_free", "TE", c1)
                c2 = P.emit("t", lambda t, st=(oc == 0), sp=(oc == NC - 1):
                            t.matmul(statb[1].ap()[0:1, :], onescol.ap(), lnsq.ap(),
                                     start=st, stop=sp), waits=[("SC", scnt)],
                            inc=("TE", 1))
                mark("lnsq_free", "TE", c2)
                mark("ln2stats", "TE", c2)
            pend2 = stats2
        pend2()
        pend2 = None
        ln_finish_apply("ln2", False, g2)
        for oc in range(NC):
            src = AR2_f.ap()[:, oc * GT:(oc + 1) * GT]
            dst = y_out[oc * 128:(oc + 1) * 128, g2 * GT:(g2 + 1) * GT]
            cnt = P.emit("y", lambda sy, d=dst, s=src: sy.dma_start(d, s),
                         waits=[rd[f"lnapply_{oc}"]], inc=(f"DY{oc % 4}", 16))
            ycnts[(g2, oc)] = cnt

    ycnts = {}
    ln_finish_apply("ln1", True, 0)
    if debug:
        P.emit("g", lambda g: g.dma_start(taps["tap_x1"], AR2_f.ap()),
               waits=need("lnapplied") + need("tapr1"), inc=("DT", 16))
        mark("tapx1", "DT", P.at("DT"))
        P.emit("v", lambda v: v.tensor_copy(srow[2].ap(), srow[0].ap()),
               waits=need("tapx1"), inc=("VE", 1))
    ffn_and_ln2_and_out(0)

    s1b, s2b = ln_finish("ln1b", PB[6].ap(), PTBf32.ap()[:, 0:512], M_sb, Rs_sb,
                         srow[0], srow[1], srow[2])
    dspall = ("DSP", P.at("DSP"))
    dyall = [(f"DY{k}", P.at(f"DY{k}")) for k in range(4)]
    for oc in range(NC):
        relw = [dspall] + dyall + need("lnapplied")
        P.emit("y", lambda sy, d=AR2_f.ap()[:, oc * GT:(oc + 1) * GT],
               s=spill[oc * 128:(oc + 1) * 128, :]:
               sy.dma_start(d, s), waits=relw, inc=("DRL", 16))
    reload_done = ("DRL", P.at("DRL"))
    ln_apply(True, 1, M_sb, Rs_sb, s1b, s2b, extra=[reload_done])
    ffn_and_ln2_and_out(1)

    fin = [(f"DY{k}", P.at(f"DY{k}")) for k in range(4)]
    if debug:
        fin += [("DT", P.at("DT")), ("DT2", P.at("DT2"))]
    P.emit("y", lambda sy: None, waits=fin)

    with contextlib.ExitStack() as st:
        semh = {name: st.enter_context(nc.semaphore(name)) for name in P.cnt}
        block = st.enter_context(nc.Block())
        engmap = {"t": block.tensor, "s": block.scalar, "v": block.vector,
                  "g": block.gpsimd, "y": block.sync}

        def run_queue(e):
            def body(eng):
                for fn, waits, inc in P.q[e]:
                    for sem, tgt in waits:
                        eng.wait_ge(semh[sem], tgt)
                    ins = fn(eng)
                    if inc is not None:
                        assert ins is not None
                        ins.then_inc(semh[inc[0]], inc[1])
            return body

        for e in P.ENGINES:
            engmap[e](run_queue(e))

    return nc


_NC_CACHE = {}

def _get_nc(debug=False):
    if debug not in _NC_CACHE:
        _NC_CACHE[debug] = build(debug)
    return _NC_CACHE[debug]


def make_inputs(x, wq, wk, wv, wo, router_w, router_b, A, Bm):
    f32 = np.float32
    wq_r = np.ascontiguousarray(
        np.asarray(wq).reshape(32, 128, 32, 128).transpose(2, 1, 0, 3)).reshape(NC * 128, NC * 128)
    wo_r = np.ascontiguousarray(
        np.asarray(wo).reshape(32, 128, 32, 128).transpose(2, 1, 0, 3)).reshape(NC * 128, NC * 128)
    wkv = np.ascontiguousarray(
        np.concatenate([wk, wv], axis=1).reshape(32, 128, 256).transpose(1, 0, 2)).reshape(128, NC * 256)
    rw_in = np.ascontiguousarray(
        np.asarray(router_w).T.reshape(32, 128, 8).transpose(1, 0, 2).reshape(128, 256))
    a_in = np.ascontiguousarray(
        np.asarray(A).reshape(128, 4096).T.reshape(32, 128, 128).transpose(1, 0, 2).reshape(128, 4096))
    bm_in = np.ascontiguousarray(np.asarray(Bm).transpose(0, 2, 1).reshape(128, 4096))
    eexp = np.zeros((8, 128), f32)
    for e in range(8):
        eexp[e, e * 16:(e + 1) * 16] = 1.0
    eye_b = np.eye(128, dtype=f32).astype(BF)
    eye_f = np.eye(128, dtype=f32)
    kt_i = np.arange(128)[:, None]
    qt_i = np.arange(GT)[None, :]
    masks = np.concatenate(
        [(kt_i + 128 * m <= qt_i).astype(f32) for m in range(4)], axis=1).astype(BF)

    shared = dict(wq_r=wq_r, wo_r=wo_r, wkv=wkv, rw_in=rw_in,
                  b_row=np.asarray(router_b).reshape(1, 8).astype(f32),
                  a_in=a_in, bm_in=bm_in, eexp=eexp, eye_b=eye_b, eye_f=eye_f,
                  masks=masks)

    in_maps = []
    xT = [np.ascontiguousarray(np.asarray(x)[b].T) for b in range(2)]
    for core in range(8):
        b, j = core // 4, core % 4
        qsA, qsB = 512 * j, 512 * (7 - j)
        xt = xT[b]
        xq = np.concatenate([xt[:, qsA:qsA + 512], xt[:, qsB:qsB + 512]], axis=1)
        xq = np.ascontiguousarray(
            xq.reshape(32, 128, 1024).transpose(1, 0, 2)).reshape(128, NC * 1024)
        za = np.zeros((4096, NA * 128 - 512 - qsA), f32)
        xkA = np.concatenate([xt[:, qsA:qsA + 512], xt[:, 0:qsA], za], axis=1)
        zb = np.zeros((4096, NB * 128 - 512 - qsB), f32)
        xkB = np.concatenate([xt[:, qsB:qsB + 512], xt[:, 0:qsB], zb], axis=1)
        def xk_r(xk, ntiles):
            a = xk.reshape(4, 8, 128, ntiles, 512).transpose(3, 0, 2, 1, 4)
            return np.ascontiguousarray(a).reshape(ntiles * 4 * 128, 4096)
        xkA = xk_r(xkA, NA // 4)
        xkB = xk_r(xkB, NB // 4)
        sb_ = np.full((NA + NB,), DEADB, f32)
        sb_[:4 * j + 4] = CBIAS
        sb_[NA:NA + 32 - 4 * j] = CBIAS
        sbias = np.ascontiguousarray(np.tile(sb_[None, :], (128, 1)))
        m = dict(shared)
        m.update(xq=xq, xkA=xkA, xkB=xkB, sbias=sbias)
        in_maps.append(m)
    return in_maps


def kernel(L0_attn_x, L0_attn_wq, L0_attn_wk, L0_attn_wv, L0_attn_wo,
           L0_ffn_router_w, L0_ffn_router_b, L0_ffn_A, L0_ffn_Bm, _debug=False):
    in_maps = make_inputs(L0_attn_x, L0_attn_wq, L0_attn_wk, L0_attn_wv, L0_attn_wo,
                          L0_ffn_router_w, L0_ffn_router_b, L0_ffn_A, L0_ffn_Bm)
    nc = _get_nc(_debug)
    res = run_bass_kernel_spmd(nc, in_maps, core_ids=list(range(8)))
    out = np.empty((2, 4096, 4096), np.float32)
    for core in range(8):
        b, j = core // 4, core % 4
        y = res.results[core]["y"]
        out[b, 512 * j:512 * j + 512] = y[:, 0:512].T
        out[b, 512 * (7 - j):512 * (7 - j) + 512] = y[:, 512:1024].T
    if _debug:
        return out, res
    return out


# version 14
# speedup vs baseline: 1.1850x; 1.0016x over previous
import contextlib
import numpy as np
import ml_dtypes
import concourse.bass as bass
import concourse.mybir as mybir
from concourse.bass_utils import run_bass_kernel_spmd

F32 = mybir.dt.float32
BF16 = mybir.dt.bfloat16
AF = mybir.ActivationFunctionType
ALU = mybir.AluOpType
BF = ml_dtypes.bfloat16

D = 4096
NC = 32
H = 32
GT = 512
NA, NB = 16, 32
SCALE = float(1.0 / np.sqrt(128.0))
CBIAS = -12.0
DEADB = -1e30
EPS = 1e-5
E = 8
FFN_SCALE = 0.5


class Prog:
    ENGINES = ("t", "s", "v", "g", "y")

    def __init__(self):
        self.q = {e: [] for e in self.ENGINES}
        self.cnt = {}
        self.waited = {e: {} for e in self.ENGINES}

    def inc(self, sem, n=1):
        self.cnt[sem] = self.cnt.get(sem, 0) + n
        return self.cnt[sem]

    def at(self, sem):
        return self.cnt.get(sem, 0)

    def emit(self, e, fn, waits=(), inc=None):
        w = []
        for sem, tgt in waits:
            if tgt <= 0:
                continue
            if self.waited[e].get(sem, -1) >= tgt:
                continue
            self.waited[e][sem] = tgt
            w.append((sem, tgt))
        cnt = None
        if inc is not None:
            cnt = self.inc(*inc)
        self.q[e].append((fn, w, inc))
        return cnt


def build(debug=False):
    nc = bass.Bass("TRN2", target_bir_lowering=False)

    xq = nc.dram_tensor("xq", [128, NC * 1024], F32, kind="ExternalInput").ap()
    xkA = nc.dram_tensor("xkA", [NA * 128, 4096], F32, kind="ExternalInput").ap()
    xkB = nc.dram_tensor("xkB", [NB * 128, 4096], F32, kind="ExternalInput").ap()
    sbias = nc.dram_tensor("sbias", [128, NA + NB], F32, kind="ExternalInput").ap()
    wq_r = nc.dram_tensor("wq_r", [NC * 128, NC * 128], F32, kind="ExternalInput").ap()
    wo_r = nc.dram_tensor("wo_r", [NC * 128, NC * 128], F32, kind="ExternalInput").ap()
    wkv = nc.dram_tensor("wkv", [128, NC * 256], F32, kind="ExternalInput").ap()
    rw_in = nc.dram_tensor("rw_in", [128, NC * E], F32, kind="ExternalInput").ap()
    b_row = nc.dram_tensor("b_row", [1, E], F32, kind="ExternalInput").ap()
    a_in = nc.dram_tensor("a_in", [128, NC * 128], F32, kind="ExternalInput").ap()
    bm_in = nc.dram_tensor("bm_in", [128, NC * 128], F32, kind="ExternalInput").ap()
    eexp_in = nc.dram_tensor("eexp", [E, 128], F32, kind="ExternalInput").ap()
    eye_b_in = nc.dram_tensor("eye_b", [128, 128], BF16, kind="ExternalInput").ap()
    eye_f_in = nc.dram_tensor("eye_f", [128, 128], F32, kind="ExternalInput").ap()
    masks_in = nc.dram_tensor("masks", [128, 4 * GT], BF16, kind="ExternalInput").ap()

    y_out = nc.dram_tensor("y", [D, 1024], F32, kind="ExternalOutput").ap()
    spill = nc.dram_tensor("r1spill", [NC * 128, GT], F32, kind="Internal").ap()

    taps = {}
    if debug:
        for name, shape, dt in [
            ("tap_kA", [128, NA * 128], F32), ("tap_vA", [128, NA * 128], F32),
            ("tap_kB", [128, NB * 128], F32), ("tap_vB", [128, NB * 128], F32),
            ("tap_q", [128, H * 1024], F32), ("tap_attn", [128, H * 1024], F32),
            ("tap_r1", [128, NC * GT], F32), ("tap_x1", [128, NC * GT], F32),
            ("tap_w", [128, 4 * E], F32), ("tap_hw", [128, GT], F32),
            ("tap_r1b", [128, NC * GT], F32),
        ]:
            taps[name] = nc.dram_tensor(name, shape, dt, kind="ExternalOutput").ap()

    QA = nc.alloc_sbuf_tensor("QA", [128, H * 1024], BF16)
    AR2_f = nc.alloc_sbuf_tensor("AR2", [128, NC * GT], F32)
    XO = AR2_f.bitcast(BF16)
    KV = nc.alloc_sbuf_tensor("KV", [128, (NA + NB) * 128 * 2], BF16)
    kTA = KV.ap()[:, 0:NA * 128]
    kTB = KV.ap()[:, NA * 128:(NA + NB) * 128]
    VA = KV.ap()[:, (NA + NB) * 128:(2 * NA + NB) * 128]
    VB = KV.ap()[:, (2 * NA + NB) * 128:]
    WBt = [nc.alloc_sbuf_tensor(f"WB{i}", [128, 16 * 128], BF16) for i in range(2)]
    XKBt = [nc.alloc_sbuf_tensor(f"XKB{i}", [128, 8 * GT], BF16) for i in range(2)]
    PTt = [nc.alloc_sbuf_tensor(f"PT{i}", [128, GT], BF16) for i in range(4)]
    PTD = nc.alloc_sbuf_tensor("PTD", [128, GT], BF16)
    vstage = PTD
    masks = nc.alloc_sbuf_tensor("masks_sb", [128, 4 * GT], BF16)
    sbias_sb = nc.alloc_sbuf_tensor("sbias_sb", [128, NA + NB], F32)
    ones128 = nc.alloc_sbuf_tensor("ones128", [128, 128], BF16)
    onescol = nc.alloc_sbuf_tensor("onescol", [128, 1], BF16)
    onesrow_f = nc.alloc_sbuf_tensor("onesrow_f", [1, 128], F32)
    eye_b = nc.alloc_sbuf_tensor("eye_b_sb", [128, 128], BF16)
    eye_f = nc.alloc_sbuf_tensor("eye_f_sb", [128, 128], F32)
    wkv_sb = QA.ap()[:, 0:NC * 256]
    a_sb = KV.ap()[:, 0:NC * 128]
    bm_sb = KV.ap()[:, NC * 128:2 * NC * 128]
    rw_sb = KV.ap()[:, 2 * NC * 128:2 * NC * 128 + NC * E]
    b_sb = nc.alloc_sbuf_tensor("b_sb", [1, E], F32)
    eexp_sb = nc.alloc_sbuf_tensor("eexp_sb", [E, 128], F32)
    RESt = [nc.alloc_sbuf_tensor(f"RES{i}", [128, GT], F32) for i in range(2)]
    R1St = [nc.alloc_sbuf_tensor(f"R1S{i}", [128, GT], F32) for i in range(2)]
    lnbf = PTt[1]
    lnsq = PTt[2]
    M_sb = nc.alloc_sbuf_tensor("M_sb", [128, GT], F32)
    Rs_sb = nc.alloc_sbuf_tensor("Rs_sb", [128, GT], F32)
    rinv_ap = XKBt[1].bitcast(F32).ap()[:, 0:GT]
    wexp_sb = nc.alloc_sbuf_tensor("wexp_sb", [128, GT], F32)
    hwb = PTt[0]
    lgt = nc.alloc_sbuf_tensor("lgt", [128, E], F32)
    top8 = nc.alloc_sbuf_tensor("top8", [128, E], F32)
    m1d = nc.alloc_sbuf_tensor("m1d", [128, 1], F32)
    g1t = nc.alloc_sbuf_tensor("g1t", [128, 1], F32)
    g2t = nc.alloc_sbuf_tensor("g2t", [128, 1], F32)
    msk1 = nc.alloc_sbuf_tensor("msk1", [128, E], F32)
    msk2 = nc.alloc_sbuf_tensor("msk2", [128, E], F32)
    wful = nc.alloc_sbuf_tensor("wful", [128, 4 * E], F32)
    wtrow_full = masks.bitcast(F32)
    srow = [nc.alloc_sbuf_tensor(f"srow{i}", [1, GT], F32) for i in range(3)]

    assert nc.sbuf_bytes_remaining >= 0, f"SBUF over: {nc.sbuf_bytes_remaining}"

    PB = [nc.alloc_psum_tensor(f"PB{i}", [128, 512], F32) for i in range(7)]
    PTB = nc.alloc_psum_tensor("PTB", [128, 1024], BF16)
    assert nc.psum_banks_remaining == 0

    P = Prog()
    rd = {}

    def mark(name, sem, cnt):
        rd[name] = (sem, cnt)

    def need(name):
        return [rd[name]] if name in rd else []

    P.emit("g", lambda g: g.memset(ones128.ap(), 1.0))
    P.emit("g", lambda g: g.memset(onescol.ap(), 1.0))
    P.emit("g", lambda g: g.memset(onesrow_f.ap(), 1.0), inc=("GE", 1))
    mark("consts_g", "GE", P.at("GE"))

    for dst, src in [(masks.ap(), masks_in), (sbias_sb.ap(), sbias),
                     (eye_b.ap(), eye_b_in), (eye_f.ap(), eye_f_in),
                     (eexp_sb.ap(), eexp_in), (b_sb.ap(), b_row)]:
        P.emit("y", lambda sy, d=dst, s=src: sy.dma_start(d, s), inc=("DC", 16))
    mark("consts", "DC", P.at("DC"))

    P.emit("g", lambda g: g.dma_start(wkv_sb, wkv), inc=("DKV", 16))
    mark("wkv", "DKV", P.at("DKV"))

    for hf in range(4):
        src = xq[:, hf * 8 * 1024:(hf + 1) * 8 * 1024]
        dst = XO.ap()[:, hf * 8 * 1024:(hf + 1) * 8 * 1024]
        P.emit("g", lambda g, d=dst, s=src: g.dma_start(d, s), inc=("DXO", 16))
    mark("xo", "DXO", P.at("DXO"))

    def p2_quarter(xk, kT, V, ti, qtr, tag, ntiles):
        qi = ti * 4 + qtr
        bi = qi % 2
        buf = XKBt[bi]
        src = xk[qi * 128:(qi + 1) * 128, :]
        dsem = f"DXK{bi}"
        w = need(f"xkb{bi}_free")
        P.emit("g", lambda g, d=buf.ap(), s=src: g.dma_start(d, s), waits=w,
               inc=(dsem, 16))
        dtot = P.at(dsem)
        for which, pb, wofs in (("k", PB[4], 0), ("v", PB[5], 128)):
            w = [(dsem, dtot)] + need("wkv") + need(f"pb{which}_free")
            for ci in range(8):
                c = qtr * 8 + ci
                lhs = wkv_sb[:, c * 256 + wofs:c * 256 + wofs + 128]
                rhs = buf.ap()[:, ci * GT:(ci + 1) * GT]
                cnt = P.emit("t", lambda t, o=pb.ap(), l=lhs, r=rhs,
                             st=(c == 0), sp=(c == NC - 1):
                             t.matmul(o, l, r, start=st, stop=sp),
                             waits=(w if ci == 0 else ()), inc=("TE", 1))
            mark(f"p2{which}", "TE", cnt)
        mark(f"xkb{bi}_free", "TE", P.at("TE"))
        if qtr == 3:
            cnt = P.emit("s", lambda s, d=kT[:, ti * GT:(ti + 1) * GT], o=PB[4].ap():
                         s.activation(d, o, AF.Copy), waits=need("p2k"), inc=("SC", 1))
            mark("pbk_free", "SC", cnt)
            vst = P.emit("s", lambda s, d=vstage.ap(), o=PB[5].ap():
                         s.activation(d, o, AF.Copy), waits=need("p2v"), inc=("SC", 1))
            mark("pbv_free", "SC", vst)
            for cs in range(4):
                w = [("SC", vst)] + need("consts") + (need("ptb_free") if cs == 0 else [])
                cnt = P.emit("t", lambda t, o=PTB.ap()[:, cs * 128:(cs + 1) * 128],
                             i=vstage.ap()[:, cs * 128:(cs + 1) * 128], e=eye_b.ap():
                             t.transpose(o, i, e), waits=w, inc=("TE", 1))
            mark("vtr", "TE", cnt)
            cnt = P.emit("s", lambda s, d=V[:, ti * GT:(ti + 1) * GT],
                         o=PTB.ap()[:, 0:512]:
                         s.activation(d, o, AF.Copy), waits=need("vtr"), inc=("SC", 1))
            mark("ptb_free", "SC", cnt)
            if ti == ntiles - 1:
                mark(f"kv{tag}", "SC", P.at("SC"))

    qbank = {0: [PB[0], PB[2]], 1: [PB[1], PB[3]]}

    def p1_half(dq, half):
        bi = (dq * 2 + half) % 2
        buf = WBt[bi]
        dsem = f"DWB{bi}"
        w = need(f"wb{bi}_free")
        src = wq_r[dq * 128:(dq + 1) * 128, half * 2048:(half + 1) * 2048]
        P.emit("g", lambda g, d=buf.ap(), s=src: g.dma_start(d, s), waits=w,
               inc=(dsem, 16))
        dtot = P.at(dsem)
        for g2 in range(2):
            pb = qbank[g2][dq % 2]
            w = [(dsem, dtot)] + need("xo") + need(f"qb{g2}{dq % 2}_free")
            for ci in range(16):
                c = half * 16 + ci
                lhs = buf.ap()[:, ci * 128:(ci + 1) * 128]
                rhs = XO.ap()[:, c * 1024 + g2 * GT:c * 1024 + g2 * GT + GT]
                cnt = P.emit("t", lambda t, o=pb.ap(), l=lhs, r=rhs,
                             st=(c == 0), sp=(c == NC - 1):
                             t.matmul(o, l, r, start=st, stop=sp),
                             waits=(w if ci == 0 else ()), inc=("TE", 1))
        mark(f"wb{bi}_free", "TE", cnt)
        if half == 1:
            for g2 in range(2):
                pb = qbank[g2][dq % 2]
                dst = QA.ap()[:, dq * 1024 + g2 * GT: dq * 1024 + g2 * GT + GT]
                scnt = P.emit("s", lambda s, d=dst, o=pb.ap():
                              s.activation(d, o, AF.Copy), waits=[("TE", cnt)],
                              inc=("SC", 1))
                mark(f"qb{g2}{dq % 2}_free", "SC", scnt)

    p2_units = [("A", xkA, kTA, VA, NA // 4, ti, qtr)
                for ti in range(NA // 4) for qtr in range(4)]
    p2_units += [("B", xkB, kTB, VB, NB // 4, ti, qtr)
                 for ti in range(NB // 4) for qtr in range(4)]
    p1_order = list(range(8, NC)) + list(range(8))
    p1_units = [(dq, half) for dq in p1_order for half in range(2)]
    ip2 = ip1 = 0
    while ip2 < len(p2_units) or ip1 < len(p1_units):
        if ip2 < len(p2_units):
            tag, xk_, kT_, V_, nt_, ti_, qtr_ = p2_units[ip2]
            p2_quarter(xk_, kT_, V_, ti_, qtr_, tag, nt_)
            ip2 += 1
        if ip1 < len(p1_units):
            p1_half(*p1_units[ip1])
            ip1 += 1
    mark("qT", "SC", P.at("SC"))

    if debug:
        P.emit("g", lambda g: g.dma_start(taps["tap_kA"], kTA), waits=need("kvA"), inc=("DT", 16))
        P.emit("g", lambda g: g.dma_start(taps["tap_vA"], VA), waits=need("kvA"), inc=("DT", 16))
        P.emit("g", lambda g: g.dma_start(taps["tap_kB"], kTB), waits=need("kvB"), inc=("DT", 16))
        P.emit("g", lambda g: g.dma_start(taps["tap_vB"], VB), waits=need("kvB"), inc=("DT", 16))
        P.emit("g", lambda g: g.dma_start(taps["tap_q"], QA.ap()), waits=need("qT"), inc=("DT", 16))
        mark("tapq", "DT", P.at("DT"))

    srot = [PB[0], PB[1], PB[2]]
    obank = [PB[3], PB[4]]
    rbank = [PB[5], PB[6]]
    for k in range(2):
        mark(f"obank{k}_free", "SC", P.at("SC"))
        mark(f"rbank{k}_free", "SC", P.at("SC"))
    slots = []
    pairidx = 0
    for (Ng, kT, V, gbase, g2) in [(NA, kTA, VA, 0, 0), (NB, kTB, VB, NA, 1)]:
        for h in range(H):
            ctx = dict(Ng=Ng, kT=kT, V=V, gbase=gbase, pidx=pairidx,
                       qsl=QA.ap()[:, h * 1024 + g2 * GT: h * 1024 + g2 * GT + GT],
                       ob=obank[pairidx % 2], rb=rbank[pairidx % 2], pready={})
            for s in range(Ng):
                slots.append((ctx, s))
            pairidx += 1

    pending_tails = []

    def emit_S_exp(ctx, s, gi):
        bank = srot[gi % 3]
        w = need(f"srot{gi % 3}_free") + (need("qT") if ctx["pidx"] == 0 else [])
        scnt = P.emit("t", lambda t, o=bank.ap(),
                      l=ctx["kT"][:, s * 128:(s + 1) * 128], r=ctx["qsl"]:
                      t.matmul(o, l, r, start=True, stop=True),
                      waits=w, inc=("TE", 1))
        dstP = PTD.ap() if s < 4 else PTt[s % 4].ap()
        w = [("TE", scnt)] + need("consts") + \
            (need("ptd_free") if s < 4 else need(f"pt{s % 4}_free"))
        ecnt = P.emit("s", lambda ss, d=dstP, o=bank.ap(),
                      b=sbias_sb.ap()[:, ctx["gbase"] + s:ctx["gbase"] + s + 1]:
                      ss.activation(d, o, AF.Exp, bias=b, scale=SCALE),
                      waits=w, inc=("SC", 1))
        mark(f"srot{gi % 3}_free", "SC", ecnt)
        if s < 4:
            w = [("SC", ecnt)] + need(f"pt{s % 4}_free") + need("consts")
            vcnt = P.emit("v", lambda v, d=PTt[s % 4].ap(), a=PTD.ap(),
                          m=masks.ap()[:, s * GT:(s + 1) * GT]:
                          v.tensor_tensor(d, a, m, ALU.mult),
                          waits=w, inc=("VE", 1))
            mark("ptd_free", "VE", vcnt)
            ctx["pready"][s] = ("VE", vcnt)
        else:
            ctx["pready"][s] = ("SC", ecnt)

    def emit_RPV(ctx, s):
        Ng, pidx, rb, ob = ctx["Ng"], ctx["pidx"], ctx["rb"], ctx["ob"]
        w = [ctx["pready"][s]] + ([need("consts_g")[0]] if pidx == 0 and s == 0 else []) \
            + (need(f"rbank{pidx % 2}_free") if s == 0 else [])
        P.emit("t", lambda t, o=rb.ap(), l=ones128.ap(), r=PTt[s % 4].ap(),
               st=(s == 0), sp=(s == Ng - 1):
               t.matmul(o, l, r, start=st, stop=sp), waits=w, inc=("TE", 1))
        w = need(f"obank{pidx % 2}_free") if s == 0 else []
        cnt = P.emit("t", lambda t, o=ob.ap(), l=ctx["V"][:, s * 128:(s + 1) * 128],
                     r=PTt[s % 4].ap(), st=(s == 0), sp=(s == Ng - 1):
                     t.matmul(o, l, r, start=st, stop=sp), waits=w, inc=("TE", 1))
        mark(f"pt{s % 4}_free", "TE", cnt)
        if s == Ng - 1:
            ostop = P.at("TE")

            def tail(ostop=ostop, rb=rb, ob=ob, qsl=ctx["qsl"], pidx=pidx):
                rcnt = P.emit("v", lambda v, o=rb.ap(): v.reciprocal(rinv_ap, o),
                              waits=[("TE", ostop)] + need("tail_done"), inc=("VE", 1))
                mark(f"rbank{pidx % 2}_free", "VE", rcnt)
                mcnt = P.emit("v", lambda v, d=qsl, o=ob.ap():
                              v.tensor_tensor(d, o, rinv_ap, ALU.mult),
                              waits=[("VE", rcnt)], inc=("VE", 1))
                mark(f"obank{pidx % 2}_free", "VE", mcnt)
                mark("tail_done", "VE", mcnt)
            pending_tails.append(tail)

    for gi, (ctx, s) in enumerate(slots):
        emit_S_exp(ctx, s, gi)
        if s == 4 and pending_tails:
            pending_tails.pop(0)()
        if gi >= 2:
            emit_RPV(*slots[gi - 2])
    emit_RPV(*slots[-2])
    emit_RPV(*slots[-1])
    for t_ in pending_tails:
        t_()
    pending_tails = []
    mark("attnT", "VE", P.at("VE"))
    p3_te = P.at("TE")

    if debug:
        P.emit("g", lambda g: g.dma_start(taps["tap_attn"], QA.ap()),
               waits=need("attnT") + need("tapq"), inc=("DT", 16))
        mark("tapattn", "DT", P.at("DT"))

    ybank = {0: [PB[0], PB[1]], 1: [PB[2], PB[3]]}
    statb = [PB[4], PB[5]]
    PTBf32 = PTB.bitcast(F32)
    lnbf2 = PTD
    lnsq2 = PTt[3]
    pending_stats = pending_stats2 = None
    for oc in range(NC):
        for half in range(2):
            bi = (oc * 2 + half) % 2
            buf = WBt[bi]
            dsem = f"DWB{bi}"
            w = need(f"wb{bi}_free")
            src = wo_r[oc * 128:(oc + 1) * 128, half * 2048:(half + 1) * 2048]
            P.emit("g", lambda g, d=buf.ap(), s=src: g.dma_start(d, s), waits=w,
                   inc=(dsem, 16))
            dtot = P.at(dsem)
            for g2 in range(2):
                pb = ybank[g2][oc % 2]
                w = [(dsem, dtot)] + need("attnT") + need(f"yb{g2}{oc % 2}_free")
                for ci in range(16):
                    c = half * 16 + ci
                    lhs = buf.ap()[:, ci * 128:(ci + 1) * 128]
                    rhs = QA.ap()[:, c * 1024 + g2 * GT:c * 1024 + g2 * GT + GT]
                    cnt = P.emit("t", lambda t, o=pb.ap(), l=lhs, r=rhs,
                                 st=(c == 0), sp=(c == NC - 1):
                                 t.matmul(o, l, r, start=st, stop=sp),
                                 waits=(w if ci == 0 else ()), inc=("TE", 1))
            mark(f"wb{bi}_free", "TE", cnt)
        if pending_stats is not None:
            pending_stats()
            pending_stats = None
        if pending_stats2 is not None:
            pending_stats2()
            pending_stats2 = None
        for g2 in range(2):
            rbuf = RESt[g2]
            rsem = f"DR{g2}"
            w = need(f"res{g2}_free")
            P.emit("y", lambda sy, d=rbuf.ap(),
                   s=xq[:, oc * 1024 + g2 * GT:oc * 1024 + (g2 + 1) * GT]:
                   sy.dma_start(d, s), waits=w, inc=(rsem, 16))
            rtot = P.at(rsem)
            pb = ybank[g2][oc % 2]
            if g2 == 0:
                dst = AR2_f.ap()[:, oc * GT:(oc + 1) * GT]
            else:
                dst = R1St[oc % 2].ap()
            w = [("TE", cnt), (rsem, rtot)] + \
                (need(f"r1s{oc % 2}_free") + need(f"r1s{oc % 2}_read") if g2 == 1 else [])
            vcnt = P.emit("v", lambda v, d=dst, o=pb.ap(), r=rbuf.ap():
                          v.tensor_tensor(d, o, r, ALU.add), waits=w, inc=("VE", 1))
            mark(f"res{g2}_free", "VE", vcnt)
            mark(f"yb{g2}{oc % 2}_free", "VE", vcnt)
            if g2 == 0:
                scnt = P.emit("s", lambda s, a=dst: s.activation(lnsq.ap(), a, AF.Square),
                              waits=[("VE", vcnt)] + need("lnsq_free"), inc=("SC", 1))
                vcnt2 = P.emit("v", lambda v, a=dst: v.tensor_copy(lnbf.ap(), a),
                               waits=need("lnbf_free"), inc=("VE", 1))

                def stats(oc=oc, scnt=scnt, vcnt2=vcnt2):
                    c1 = P.emit("t", lambda t, st=(oc == 0), sp=(oc == NC - 1):
                                t.matmul(statb[0].ap()[0:1, :], onescol.ap(), lnbf.ap(),
                                         start=st, stop=sp),
                                waits=[("VE", vcnt2)] + need("consts_g"), inc=("TE", 1))
                    mark("lnbf_free", "TE", c1)
                    c2 = P.emit("t", lambda t, st=(oc == 0), sp=(oc == NC - 1):
                                t.matmul(statb[1].ap()[0:1, :], onescol.ap(), lnsq.ap(),
                                         start=st, stop=sp), waits=[("SC", scnt)],
                                inc=("TE", 1))
                    mark("lnsq_free", "TE", c2)
                    mark("ln1stats", "TE", c2)
                pending_stats = stats
            else:
                ycnt = P.emit("y", lambda sy, d=spill[oc * 128:(oc + 1) * 128, :], s=dst:
                              sy.dma_start(d, s), waits=[("VE", vcnt)], inc=("DSP", 16))
                mark(f"r1s{oc % 2}_free", "DSP", ycnt)
                scnt2 = P.emit("s", lambda s, a=dst: s.activation(lnsq2.ap(), a, AF.Square),
                               waits=[("VE", vcnt)] + need("lnsq2_free"), inc=("SC", 1))
                mark(f"r1s{oc % 2}_read", "SC", scnt2)
                vcnt3 = P.emit("v", lambda v, a=dst: v.tensor_copy(lnbf2.ap(), a),
                               waits=need("lnbf2_free"), inc=("VE", 1))

                def statsb(oc=oc, scnt2=scnt2, vcnt3=vcnt3):
                    c1 = P.emit("t", lambda t, st=(oc == 0), sp=(oc == NC - 1):
                                t.matmul(PB[6].ap()[0:1, :], onescol.ap(), lnbf2.ap(),
                                         start=st, stop=sp),
                                waits=[("VE", vcnt3)], inc=("TE", 1))
                    mark("lnbf2_free", "TE", c1)
                    c2 = P.emit("t", lambda t, st=(oc == 0), sp=(oc == NC - 1):
                                t.matmul(PTBf32.ap()[0:1, 0:512], onescol.ap(), lnsq2.ap(),
                                         start=st, stop=sp), waits=[("SC", scnt2)],
                                inc=("TE", 1))
                    mark("lnsq2_free", "TE", c2)
                    mark("ln1bstats", "TE", c2)
                pending_stats2 = statsb
    for ps in (pending_stats, pending_stats2):
        if ps is not None:
            ps()
    pending_stats = pending_stats2 = None
    mark("p4_te", "TE", P.at("TE"))
    mark("p4", "VE", P.at("VE"))

    for i, (dstw, srcw) in enumerate([(a_sb, a_in), (bm_sb, bm_in), (rw_sb, rw_in)]):
        w = [("TE", p3_te)] if i == 0 else []
        P.emit("g", lambda g, d=dstw, s=srcw: g.dma_start(d, s), waits=w, inc=("DF", 16))
    mark("ffnw", "DF", P.at("DF"))

    if debug:
        P.emit("g", lambda g: g.dma_start(taps["tap_r1"], AR2_f.ap()),
               waits=need("p4") + need("tapattn"), inc=("DT", 16))
        mark("tapr1", "DT", P.at("DT"))
        P.emit("y", lambda sy: sy.dma_start(
               taps["tap_r1b"].rearrange("p (c n) -> p c n", n=GT),
               spill.rearrange("(c p) n -> p c n", p=128)),
               waits=[("DSP", P.at("DSP"))], inc=("DT2", 16))

    def ln_finish(tag, sA, sB, Mdst, Rdst, r0, r1_, r2_, extra=()):
        sem_stats = need(f"{tag}stats")
        c = P.emit("v", lambda v: v.tensor_scalar(r0.ap(), sA[0:1, :],
                   1.0 / D, None, ALU.mult), waits=sem_stats + list(extra), inc=("VE", 1))
        c = P.emit("v", lambda v: v.tensor_scalar(r1_.ap(), sB[0:1, :],
                   1.0 / D, EPS, ALU.mult, ALU.add), inc=("VE", 1))
        c = P.emit("v", lambda v: v.tensor_tensor(r2_.ap(), r0.ap(),
                   r0.ap(), ALU.mult), inc=("VE", 1))
        c = P.emit("v", lambda v: v.tensor_tensor(r1_.ap(), r1_.ap(),
                   r2_.ap(), ALU.subtract), inc=("VE", 1))
        mark(f"{tag}statdone", "VE", c)
        sc = P.emit("s", lambda s: s.activation(r2_.ap(), r1_.ap(), AF.Sqrt),
                    waits=[("VE", c)], inc=("SC", 1))
        rc = P.emit("v", lambda v: v.reciprocal(r1_.ap(), r2_.ap()),
                    waits=[("SC", sc)], inc=("VE", 1))
        b1 = P.emit("t", lambda t: t.matmul(PB[0].ap(), onesrow_f.ap(), r0.ap(),
                    start=True, stop=True),
                    waits=[("VE", c)] + need("consts_g") + need("bc_free"), inc=("TE", 1))
        s1 = P.emit("s", lambda s: s.activation(Mdst.ap(), PB[0].ap(), AF.Copy),
                    waits=[("TE", b1)], inc=("SC", 1))
        b2 = P.emit("t", lambda t: t.matmul(PB[0].ap(), onesrow_f.ap(), r1_.ap(),
                    start=True, stop=True), waits=[("VE", rc), ("SC", s1)], inc=("TE", 1))
        s2 = P.emit("s", lambda s: s.activation(Rdst.ap(), PB[0].ap(), AF.Copy),
                    waits=[("TE", b2)], inc=("SC", 1))
        mark("bc_free", "SC", s2)
        return s1, s2

    def ln_apply(x1_to_QA, g2, Msb, Rsb, s1, s2, extra=()):
        for c_ in range(NC):
            ap = AR2_f.ap()[:, c_ * GT:(c_ + 1) * GT]
            P.emit("v", lambda v, a=ap: v.tensor_tensor(a, a, Msb.ap(), ALU.subtract),
                   waits=[("SC", s1), ("SC", s2)] + list(extra), inc=("VE", 1))
            v2 = P.emit("v", lambda v, a=ap: v.tensor_tensor(a, a, Rsb.ap(), ALU.mult),
                        inc=("VE", 1))
            mark(f"lnapply_{c_}", "VE", v2)
            if x1_to_QA:
                dst = QA.ap()[:, g2 * (H * 512) + c_ * GT: g2 * (H * 512) + (c_ + 1) * GT]
                sc2 = P.emit("s", lambda s, d=dst, a=ap: s.activation(d, a, AF.Copy),
                             waits=[("VE", v2)] + need("p4_te"), inc=("SC", 1))
                mark(f"x1b_{c_}", "SC", sc2)
        mark("lnapplied", "VE", P.at("VE"))

    def ln_finish_apply(tag, x1_to_QA, g2):
        s1, s2 = ln_finish(tag, statb[0].ap(), statb[1].ap(), M_sb, Rs_sb,
                           srow[0], srow[1], srow[2])
        ln_apply(x1_to_QA, g2, M_sb, Rs_sb, s1, s2)

    def ffn_and_ln2_and_out(g2):
        x1b = QA.ap()[:, g2 * (H * 512):(g2 + 1) * (H * 512)]
        for ts in range(4):
            for c in range(NC):
                w = need("ffnw") + [rd[f"x1b_{c}"]] + (need("lg_free") if c == 0 else [])
                lhs = x1b[:, c * GT + ts * 128: c * GT + ts * 128 + 128]
                rhs = rw_sb[:, c * E:(c + 1) * E]
                P.emit("t", lambda t, o=PB[0].ap()[:, 0:E], l=lhs, r=rhs, st=(c == 0):
                       t.matmul(o, l, r, start=st, stop=False),
                       waits=w, inc=("TE", 1))
            cnt = P.emit("t", lambda t: t.matmul(PB[0].ap()[:, 0:E], onesrow_f.ap(),
                         b_sb.ap(), start=False, stop=True),
                         waits=need("consts") + need("consts_g"), inc=("TE", 1))
            sc = P.emit("s", lambda s: s.activation(lgt.ap(), PB[0].ap()[:, 0:E], AF.Copy),
                        waits=[("TE", cnt)], inc=("SC", 1))
            mark("lg_free", "SC", sc)
            vm = P.emit("v", lambda v: v.max(out=top8.ap(), in_=lgt.ap()),
                        waits=[("SC", sc)], inc=("VE", 1))
            vd = P.emit("v", lambda v: v.tensor_tensor(m1d.ap(), top8.ap()[:, 0:1],
                        top8.ap()[:, 1:2], ALU.subtract), waits=[("VE", vm)], inc=("VE", 1))
            sg = P.emit("s", lambda s: s.activation(g1t.ap(), m1d.ap(), AF.Sigmoid),
                        waits=[("VE", vd)], inc=("SC", 1))
            vg = P.emit("v", lambda v: v.tensor_scalar(g2t.ap(), g1t.ap(), -1.0, 1.0,
                        ALU.mult, ALU.add), waits=[("SC", sg)], inc=("VE", 1))
            P.emit("v", lambda v: v.tensor_scalar(msk1.ap(), lgt.ap(), top8.ap()[:, 0:1],
                   None, ALU.is_equal), waits=[("VE", vm)], inc=("VE", 1))
            P.emit("v", lambda v: v.tensor_scalar(msk2.ap(), lgt.ap(), top8.ap()[:, 1:2],
                   None, ALU.is_equal), inc=("VE", 1))
            P.emit("v", lambda v: v.tensor_scalar(msk1.ap(), msk1.ap(), g1t.ap(),
                   None, ALU.mult), inc=("VE", 1))
            v4 = P.emit("v", lambda v, d=wful.ap()[:, ts * E:(ts + 1) * E]:
                        v.scalar_tensor_tensor(d, msk2.ap(), g2t.ap(), msk1.ap(),
                        ALU.mult, ALU.add), waits=[("VE", vg)], inc=("VE", 1))
            tc = P.emit("t", lambda t, i=wful.ap()[:, ts * E:(ts + 1) * E]:
                        t.transpose(PB[1].ap()[0:E, 0:128], i, eye_f.ap()),
                        waits=[("VE", v4)] + need("wt_free") + need("consts"),
                        inc=("TE", 1))
            sc2 = P.emit("s", lambda s, d=wtrow_full.ap()[0:8, ts * 128:(ts + 1) * 128]:
                         s.activation(d, PB[1].ap()[0:E, 0:128], AF.Copy),
                         waits=[("TE", tc)], inc=("SC", 1))
            mark("wt_free", "SC", sc2)
            mark(f"wtr_{ts}", "SC", sc2)
        if debug and g2 == 0:
            P.emit("g", lambda g: g.dma_start(taps["tap_w"], wful.ap()),
                   waits=[rd["wtr_3"]], inc=("DT", 16))
        wt = [rd[f"wtr_{ts}"] for ts in range(4)]
        tc = P.emit("t", lambda t: t.matmul(PB[2].ap(), eexp_sb.ap(), wtrow_full.ap()[0:8, 0:512],
                    start=True, stop=True), waits=wt + need("consts"), inc=("TE", 1))
        wsc = P.emit("s", lambda s: s.activation(wexp_sb.ap(), PB[2].ap(), AF.Copy),
                     waits=[("TE", tc)], inc=("SC", 1))
        for c in range(NC):
            lhs = a_sb[:, c * 128:(c + 1) * 128]
            rhs = x1b[:, c * GT:(c + 1) * GT]
            w = [rd[f"x1b_{c}"]] + (need("ffnw") + need("h_free") if c == 0 else [])
            cnt = P.emit("t", lambda t, l=lhs, r=rhs, st=(c == 0), sp=(c == NC - 1):
                         t.matmul(PB[1].ap(), l, r, start=st, stop=sp),
                         waits=w, inc=("TE", 1))
        hv = P.emit("v", lambda v: v.tensor_tensor(hwb.ap(), PB[1].ap(), wexp_sb.ap(),
                    ALU.mult), waits=[("TE", cnt), ("SC", wsc)], inc=("VE", 1))
        mark("h_free", "VE", hv)
        if debug and g2 == 0:
            P.emit("g", lambda g: g.dma_start(taps["tap_hw"], hwb.ap()),
                   waits=[("VE", hv)], inc=("DT", 16))
        ob2 = [PB[3], PB[1], PB[2], PB[0]]
        pend2 = None
        for oc in range(NC):
            pb = ob2[oc % 4]
            w = [("VE", hv)] + need(f"ob{oc % 4}_free") + \
                (need("bc_free") + need("lg_free") if oc % 4 == 3 else [])
            cnt = P.emit("t", lambda t, o=pb.ap(), l=bm_sb[:, oc * 128:(oc + 1) * 128]:
                         t.matmul(o, l, hwb.ap(), start=True, stop=True),
                         waits=w, inc=("TE", 1))
            if pend2 is not None:
                pend2()
                pend2 = None
            ap = AR2_f.ap()[:, oc * GT:(oc + 1) * GT]
            vcnt = P.emit("v", lambda v, a=ap, o=pb.ap():
                          v.scalar_tensor_tensor(a, o, FFN_SCALE, a, ALU.mult, ALU.add),
                          waits=[("TE", cnt), rd[f"lnapply_{oc}"]], inc=("VE", 1))
            mark(f"ob{oc % 4}_free", "VE", vcnt)
            scnt = P.emit("s", lambda s, a=ap: s.activation(lnsq.ap(), a, AF.Square),
                          waits=[("VE", vcnt)] + need("lnsq_free"), inc=("SC", 1))
            vcnt2 = P.emit("v", lambda v, a=ap: v.tensor_copy(lnbf.ap(), a),
                           waits=need("lnbf_free"), inc=("VE", 1))

            def stats2(oc=oc, scnt=scnt, vcnt2=vcnt2):
                c1 = P.emit("t", lambda t, st=(oc == 0), sp=(oc == NC - 1):
                            t.matmul(statb[0].ap()[0:1, :], onescol.ap(), lnbf.ap(),
                                     start=st, stop=sp), waits=[("VE", vcnt2)],
                            inc=("TE", 1))
                mark("lnbf_free", "TE", c1)
                c2 = P.emit("t", lambda t, st=(oc == 0), sp=(oc == NC - 1):
                            t.matmul(statb[1].ap()[0:1, :], onescol.ap(), lnsq.ap(),
                                     start=st, stop=sp), waits=[("SC", scnt)],
                            inc=("TE", 1))
                mark("lnsq_free", "TE", c2)
                mark("ln2stats", "TE", c2)
            pend2 = stats2
        pend2()
        pend2 = None
        ln_finish_apply("ln2", False, g2)
        for oc in range(NC):
            src = AR2_f.ap()[:, oc * GT:(oc + 1) * GT]
            dst = y_out[oc * 128:(oc + 1) * 128, g2 * GT:(g2 + 1) * GT]
            cnt = P.emit("y", lambda sy, d=dst, s=src: sy.dma_start(d, s),
                         waits=[rd[f"lnapply_{oc}"]], inc=(f"DY{oc % 4}", 16))
            ycnts[(g2, oc)] = cnt

    ycnts = {}
    ln_finish_apply("ln1", True, 0)
    if debug:
        P.emit("g", lambda g: g.dma_start(taps["tap_x1"], AR2_f.ap()),
               waits=need("lnapplied") + need("tapr1"), inc=("DT", 16))
        mark("tapx1", "DT", P.at("DT"))
        P.emit("v", lambda v: v.tensor_copy(srow[2].ap(), srow[0].ap()),
               waits=need("tapx1"), inc=("VE", 1))
    ffn_and_ln2_and_out(0)

    s1b, s2b = ln_finish("ln1b", PB[6].ap(), PTBf32.ap()[:, 0:512], M_sb, Rs_sb,
                         srow[0], srow[1], srow[2])
    dspall = ("DSP", P.at("DSP"))
    dyall = [(f"DY{k}", P.at(f"DY{k}")) for k in range(4)]
    for oc in range(NC):
        relw = [dspall] + dyall + need("lnapplied")
        P.emit("y", lambda sy, d=AR2_f.ap()[:, oc * GT:(oc + 1) * GT],
               s=spill[oc * 128:(oc + 1) * 128, :]:
               sy.dma_start(d, s), waits=relw, inc=("DRL", 16))
    reload_done = ("DRL", P.at("DRL"))
    ln_apply(True, 1, M_sb, Rs_sb, s1b, s2b, extra=[reload_done])
    ffn_and_ln2_and_out(1)

    fin = [(f"DY{k}", P.at(f"DY{k}")) for k in range(4)]
    if debug:
        fin += [("DT", P.at("DT")), ("DT2", P.at("DT2"))]
    P.emit("y", lambda sy: None, waits=fin)

    with contextlib.ExitStack() as st:
        semh = {name: st.enter_context(nc.semaphore(name)) for name in P.cnt}
        block = st.enter_context(nc.Block())
        engmap = {"t": block.tensor, "s": block.scalar, "v": block.vector,
                  "g": block.gpsimd, "y": block.sync}

        def run_queue(e):
            def body(eng):
                for fn, waits, inc in P.q[e]:
                    for sem, tgt in waits:
                        eng.wait_ge(semh[sem], tgt)
                    ins = fn(eng)
                    if inc is not None:
                        assert ins is not None
                        ins.then_inc(semh[inc[0]], inc[1])
            return body

        for e in P.ENGINES:
            engmap[e](run_queue(e))

    return nc


_NC_CACHE = {}

def _get_nc(debug=False):
    if debug not in _NC_CACHE:
        _NC_CACHE[debug] = build(debug)
    return _NC_CACHE[debug]


def make_inputs(x, wq, wk, wv, wo, router_w, router_b, A, Bm):
    f32 = np.float32
    wq_r = np.ascontiguousarray(
        np.asarray(wq).reshape(32, 128, 32, 128).transpose(2, 1, 0, 3)).reshape(NC * 128, NC * 128)
    wo_r = np.ascontiguousarray(
        np.asarray(wo).reshape(32, 128, 32, 128).transpose(2, 1, 0, 3)).reshape(NC * 128, NC * 128)
    wkv = np.ascontiguousarray(
        np.concatenate([wk, wv], axis=1).reshape(32, 128, 256).transpose(1, 0, 2)).reshape(128, NC * 256)
    rw_in = np.ascontiguousarray(
        np.asarray(router_w).T.reshape(32, 128, 8).transpose(1, 0, 2).reshape(128, 256))
    a_in = np.ascontiguousarray(
        np.asarray(A).reshape(128, 4096).T.reshape(32, 128, 128).transpose(1, 0, 2).reshape(128, 4096))
    bm_in = np.ascontiguousarray(np.asarray(Bm).transpose(0, 2, 1).reshape(128, 4096))
    eexp = np.zeros((8, 128), f32)
    for e in range(8):
        eexp[e, e * 16:(e + 1) * 16] = 1.0
    eye_b = np.eye(128, dtype=f32).astype(BF)
    eye_f = np.eye(128, dtype=f32)
    kt_i = np.arange(128)[:, None]
    qt_i = np.arange(GT)[None, :]
    masks = np.concatenate(
        [(kt_i + 128 * m <= qt_i).astype(f32) for m in range(4)], axis=1).astype(BF)

    shared = dict(wq_r=wq_r, wo_r=wo_r, wkv=wkv, rw_in=rw_in,
                  b_row=np.asarray(router_b).reshape(1, 8).astype(f32),
                  a_in=a_in, bm_in=bm_in, eexp=eexp, eye_b=eye_b, eye_f=eye_f,
                  masks=masks)

    in_maps = []
    xT = [np.ascontiguousarray(np.asarray(x)[b].T) for b in range(2)]
    for core in range(8):
        b, j = core // 4, core % 4
        qsA, qsB = 512 * j, 512 * (7 - j)
        xt = xT[b]
        xq = np.concatenate([xt[:, qsA:qsA + 512], xt[:, qsB:qsB + 512]], axis=1)
        xq = np.ascontiguousarray(
            xq.reshape(32, 128, 1024).transpose(1, 0, 2)).reshape(128, NC * 1024)
        za = np.zeros((4096, NA * 128 - 512 - qsA), f32)
        xkA = np.concatenate([xt[:, qsA:qsA + 512], xt[:, 0:qsA], za], axis=1)
        zb = np.zeros((4096, NB * 128 - 512 - qsB), f32)
        xkB = np.concatenate([xt[:, qsB:qsB + 512], xt[:, 0:qsB], zb], axis=1)
        def xk_r(xk, ntiles):
            a = xk.reshape(4, 8, 128, ntiles, 512).transpose(3, 0, 2, 1, 4)
            return np.ascontiguousarray(a).reshape(ntiles * 4 * 128, 4096)
        xkA = xk_r(xkA, NA // 4)
        xkB = xk_r(xkB, NB // 4)
        sb_ = np.full((NA + NB,), DEADB, f32)
        sb_[:4 * j + 4] = CBIAS
        sb_[NA:NA + 32 - 4 * j] = CBIAS
        sbias = np.ascontiguousarray(np.tile(sb_[None, :], (128, 1)))
        m = dict(shared)
        m.update(xq=xq, xkA=xkA, xkB=xkB, sbias=sbias)
        in_maps.append(m)
    return in_maps


def kernel(L0_attn_x, L0_attn_wq, L0_attn_wk, L0_attn_wv, L0_attn_wo,
           L0_ffn_router_w, L0_ffn_router_b, L0_ffn_A, L0_ffn_Bm, _debug=False):
    in_maps = make_inputs(L0_attn_x, L0_attn_wq, L0_attn_wk, L0_attn_wv, L0_attn_wo,
                          L0_ffn_router_w, L0_ffn_router_b, L0_ffn_A, L0_ffn_Bm)
    nc = _get_nc(_debug)
    res = run_bass_kernel_spmd(nc, in_maps, core_ids=list(range(8)))
    out = np.empty((2, 4096, 4096), np.float32)
    for core in range(8):
        b, j = core // 4, core % 4
        y = res.results[core]["y"]
        out[b, 512 * j:512 * j + 512] = y[:, 0:512].T
        out[b, 512 * (7 - j):512 * (7 - j) + 512] = y[:, 512:1024].T
    if _debug:
        return out, res
    return out


# version 15
# speedup vs baseline: 1.1861x; 1.0010x over previous
import contextlib
import numpy as np
import ml_dtypes
import concourse.bass as bass
import concourse.mybir as mybir
from concourse.bass_utils import run_bass_kernel_spmd

F32 = mybir.dt.float32
BF16 = mybir.dt.bfloat16
AF = mybir.ActivationFunctionType
ALU = mybir.AluOpType
BF = ml_dtypes.bfloat16

D = 4096
NC = 32
H = 32
GT = 512
NA, NB = 16, 32
SCALE = float(1.0 / np.sqrt(128.0))
CBIAS = -12.0
DEADB = -1e30
EPS = 1e-5
E = 8
FFN_SCALE = 0.5


class Prog:
    ENGINES = ("t", "s", "v", "g", "y")

    def __init__(self):
        self.q = {e: [] for e in self.ENGINES}
        self.cnt = {}
        self.waited = {e: {} for e in self.ENGINES}

    def inc(self, sem, n=1):
        self.cnt[sem] = self.cnt.get(sem, 0) + n
        return self.cnt[sem]

    def at(self, sem):
        return self.cnt.get(sem, 0)

    def emit(self, e, fn, waits=(), inc=None):
        w = []
        for sem, tgt in waits:
            if tgt <= 0:
                continue
            if self.waited[e].get(sem, -1) >= tgt:
                continue
            self.waited[e][sem] = tgt
            w.append((sem, tgt))
        cnt = None
        if inc is not None:
            cnt = self.inc(*inc)
        self.q[e].append((fn, w, inc))
        return cnt


def build(debug=False):
    nc = bass.Bass("TRN2", target_bir_lowering=False)

    xq = nc.dram_tensor("xq", [128, NC * 1024], F32, kind="ExternalInput").ap()
    xkA = nc.dram_tensor("xkA", [NA * 128, 4096], F32, kind="ExternalInput").ap()
    xkB = nc.dram_tensor("xkB", [NB * 128, 4096], F32, kind="ExternalInput").ap()
    sbias = nc.dram_tensor("sbias", [128, NA + NB], F32, kind="ExternalInput").ap()
    wq_r = nc.dram_tensor("wq_r", [NC * 128, NC * 128], F32, kind="ExternalInput").ap()
    wo_r = nc.dram_tensor("wo_r", [NC * 128, NC * 128], F32, kind="ExternalInput").ap()
    wkv = nc.dram_tensor("wkv", [128, NC * 256], F32, kind="ExternalInput").ap()
    rw_in = nc.dram_tensor("rw_in", [128, NC * E], F32, kind="ExternalInput").ap()
    b_row = nc.dram_tensor("b_row", [1, E], F32, kind="ExternalInput").ap()
    a_in = nc.dram_tensor("a_in", [128, NC * 128], F32, kind="ExternalInput").ap()
    bm_in = nc.dram_tensor("bm_in", [128, NC * 128], F32, kind="ExternalInput").ap()
    eexp_in = nc.dram_tensor("eexp", [E, 128], F32, kind="ExternalInput").ap()
    eye_b_in = nc.dram_tensor("eye_b", [128, 128], BF16, kind="ExternalInput").ap()
    eye_f_in = nc.dram_tensor("eye_f", [128, 128], F32, kind="ExternalInput").ap()
    masks_in = nc.dram_tensor("masks", [128, 4 * GT], BF16, kind="ExternalInput").ap()

    y_out = nc.dram_tensor("y", [D, 1024], F32, kind="ExternalOutput").ap()
    spill = nc.dram_tensor("r1spill", [NC * 128, GT], F32, kind="Internal").ap()

    taps = {}
    if debug:
        for name, shape, dt in [
            ("tap_kA", [128, NA * 128], F32), ("tap_vA", [128, NA * 128], F32),
            ("tap_kB", [128, NB * 128], F32), ("tap_vB", [128, NB * 128], F32),
            ("tap_q", [128, H * 1024], F32), ("tap_attn", [128, H * 1024], F32),
            ("tap_r1", [128, NC * GT], F32), ("tap_x1", [128, NC * GT], F32),
            ("tap_w", [128, 4 * E], F32), ("tap_hw", [128, GT], F32),
            ("tap_r1b", [128, NC * GT], F32),
        ]:
            taps[name] = nc.dram_tensor(name, shape, dt, kind="ExternalOutput").ap()

    QA = nc.alloc_sbuf_tensor("QA", [128, H * 1024], BF16)
    AR2_f = nc.alloc_sbuf_tensor("AR2", [128, NC * GT], F32)
    XO = AR2_f.bitcast(BF16)
    KV = nc.alloc_sbuf_tensor("KV", [128, (NA + NB) * 128 * 2], BF16)
    kTA = KV.ap()[:, 0:NA * 128]
    kTB = KV.ap()[:, NA * 128:(NA + NB) * 128]
    VA = KV.ap()[:, (NA + NB) * 128:(2 * NA + NB) * 128]
    VB = KV.ap()[:, (2 * NA + NB) * 128:]
    WBt = [nc.alloc_sbuf_tensor(f"WB{i}", [128, 16 * 128], BF16) for i in range(2)]
    XKBt = [nc.alloc_sbuf_tensor(f"XKB{i}", [128, 8 * GT], BF16) for i in range(2)]
    PTt = [nc.alloc_sbuf_tensor(f"PT{i}", [128, GT], BF16) for i in range(4)]
    PTD = nc.alloc_sbuf_tensor("PTD", [128, GT], BF16)
    vstage = PTD
    masks = nc.alloc_sbuf_tensor("masks_sb", [128, 4 * GT], BF16)
    sbias_sb = nc.alloc_sbuf_tensor("sbias_sb", [128, NA + NB], F32)
    ones128 = nc.alloc_sbuf_tensor("ones128", [128, 128], BF16)
    onescol = nc.alloc_sbuf_tensor("onescol", [128, 1], BF16)
    onesrow_f = nc.alloc_sbuf_tensor("onesrow_f", [1, 128], F32)
    eye_b = nc.alloc_sbuf_tensor("eye_b_sb", [128, 128], BF16)
    eye_f = nc.alloc_sbuf_tensor("eye_f_sb", [128, 128], F32)
    wkv_sb = QA.ap()[:, 0:NC * 256]
    a_sb = KV.ap()[:, 0:NC * 128]
    bm_sb = KV.ap()[:, NC * 128:2 * NC * 128]
    rw_sb = KV.ap()[:, 2 * NC * 128:2 * NC * 128 + NC * E]
    b_sb = nc.alloc_sbuf_tensor("b_sb", [1, E], F32)
    eexp_sb = nc.alloc_sbuf_tensor("eexp_sb", [E, 128], F32)
    RESt = [nc.alloc_sbuf_tensor(f"RES{i}", [128, GT], F32) for i in range(2)]
    R1St = [nc.alloc_sbuf_tensor(f"R1S{i}", [128, GT], F32) for i in range(2)]
    lnbf = PTt[1]
    lnsq = PTt[2]
    M_sb = nc.alloc_sbuf_tensor("M_sb", [128, GT], F32)
    Rs_sb = nc.alloc_sbuf_tensor("Rs_sb", [128, GT], F32)
    rinv_ap = XKBt[1].bitcast(F32).ap()[:, 0:GT]
    wexp_sb = nc.alloc_sbuf_tensor("wexp_sb", [128, GT], F32)
    hwb = PTt[0]
    lgt = nc.alloc_sbuf_tensor("lgt", [128, E], F32)
    top8 = nc.alloc_sbuf_tensor("top8", [128, E], F32)
    m1d = nc.alloc_sbuf_tensor("m1d", [128, 1], F32)
    g1t = nc.alloc_sbuf_tensor("g1t", [128, 1], F32)
    g2t = nc.alloc_sbuf_tensor("g2t", [128, 1], F32)
    msk1 = nc.alloc_sbuf_tensor("msk1", [128, E], F32)
    msk2 = nc.alloc_sbuf_tensor("msk2", [128, E], F32)
    wful = nc.alloc_sbuf_tensor("wful", [128, 4 * E], F32)
    wtrow_full = masks.bitcast(F32)
    srow = [nc.alloc_sbuf_tensor(f"srow{i}", [1, GT], F32) for i in range(3)]

    assert nc.sbuf_bytes_remaining >= 0, f"SBUF over: {nc.sbuf_bytes_remaining}"

    PB = [nc.alloc_psum_tensor(f"PB{i}", [128, 512], F32) for i in range(7)]
    PTB = nc.alloc_psum_tensor("PTB", [128, 1024], BF16)
    assert nc.psum_banks_remaining == 0

    P = Prog()
    rd = {}

    def mark(name, sem, cnt):
        rd[name] = (sem, cnt)

    def need(name):
        return [rd[name]] if name in rd else []

    P.emit("g", lambda g: g.memset(ones128.ap(), 1.0))
    P.emit("g", lambda g: g.memset(onescol.ap(), 1.0))
    P.emit("g", lambda g: g.memset(onesrow_f.ap(), 1.0), inc=("GE", 1))
    mark("consts_g", "GE", P.at("GE"))

    for dst, src in [(masks.ap(), masks_in), (sbias_sb.ap(), sbias),
                     (eye_b.ap(), eye_b_in), (eye_f.ap(), eye_f_in),
                     (eexp_sb.ap(), eexp_in), (b_sb.ap(), b_row)]:
        P.emit("y", lambda sy, d=dst, s=src: sy.dma_start(d, s), inc=("DC", 16))
    mark("consts", "DC", P.at("DC"))

    P.emit("g", lambda g: g.dma_start(wkv_sb, wkv), inc=("DKV", 16))
    mark("wkv", "DKV", P.at("DKV"))

    for hf in range(4):
        src = xq[:, hf * 8 * 1024:(hf + 1) * 8 * 1024]
        dst = XO.ap()[:, hf * 8 * 1024:(hf + 1) * 8 * 1024]
        P.emit("g", lambda g, d=dst, s=src: g.dma_start(d, s), inc=("DXO", 16))
    mark("xo", "DXO", P.at("DXO"))

    def p2_quarter(xk, kT, V, ti, qtr, tag, ntiles):
        qi = ti * 4 + qtr
        bi = qi % 2
        buf = XKBt[bi]
        src = xk[qi * 128:(qi + 1) * 128, :]
        dsem = f"DXK{bi}"
        w = need(f"xkb{bi}_free")
        P.emit("g", lambda g, d=buf.ap(), s=src: g.dma_start(d, s), waits=w,
               inc=(dsem, 16))
        dtot = P.at(dsem)
        for which, pb, wofs in (("k", PB[4], 0), ("v", PB[5], 128)):
            w = [(dsem, dtot)] + need("wkv") + need(f"pb{which}_free")
            for ci in range(8):
                c = qtr * 8 + ci
                lhs = wkv_sb[:, c * 256 + wofs:c * 256 + wofs + 128]
                rhs = buf.ap()[:, ci * GT:(ci + 1) * GT]
                cnt = P.emit("t", lambda t, o=pb.ap(), l=lhs, r=rhs,
                             st=(c == 0), sp=(c == NC - 1):
                             t.matmul(o, l, r, start=st, stop=sp),
                             waits=(w if ci == 0 else ()), inc=("TE", 1))
            mark(f"p2{which}", "TE", cnt)
        mark(f"xkb{bi}_free", "TE", P.at("TE"))
        if qtr == 3:
            cnt = P.emit("s", lambda s, d=kT[:, ti * GT:(ti + 1) * GT], o=PB[4].ap():
                         s.activation(d, o, AF.Copy), waits=need("p2k"), inc=("SC", 1))
            mark("pbk_free", "SC", cnt)
            vst = P.emit("s", lambda s, d=vstage.ap(), o=PB[5].ap():
                         s.activation(d, o, AF.Copy), waits=need("p2v"), inc=("SC", 1))
            mark("pbv_free", "SC", vst)
            for cs in range(4):
                w = [("SC", vst)] + need("consts") + (need("ptb_free") if cs == 0 else [])
                cnt = P.emit("t", lambda t, o=PTB.ap()[:, cs * 128:(cs + 1) * 128],
                             i=vstage.ap()[:, cs * 128:(cs + 1) * 128], e=eye_b.ap():
                             t.transpose(o, i, e), waits=w, inc=("TE", 1))
            mark("vtr", "TE", cnt)
            cnt = P.emit("s", lambda s, d=V[:, ti * GT:(ti + 1) * GT],
                         o=PTB.ap()[:, 0:512]:
                         s.activation(d, o, AF.Copy), waits=need("vtr"), inc=("SC", 1))
            mark("ptb_free", "SC", cnt)
            if ti == ntiles - 1:
                mark(f"kv{tag}", "SC", P.at("SC"))

    qbank = {0: [PB[0], PB[2]], 1: [PB[1], PB[3]]}

    def p1_half(dq, half):
        bi = (dq * 2 + half) % 2
        buf = WBt[bi]
        dsem = f"DWB{bi}"
        w = need(f"wb{bi}_free")
        src = wq_r[dq * 128:(dq + 1) * 128, half * 2048:(half + 1) * 2048]
        P.emit("g", lambda g, d=buf.ap(), s=src: g.dma_start(d, s), waits=w,
               inc=(dsem, 16))
        dtot = P.at(dsem)
        for g2 in range(2):
            pb = qbank[g2][dq % 2]
            w = [(dsem, dtot)] + need("xo") + need(f"qb{g2}{dq % 2}_free")
            for ci in range(16):
                c = half * 16 + ci
                lhs = buf.ap()[:, ci * 128:(ci + 1) * 128]
                rhs = XO.ap()[:, c * 1024 + g2 * GT:c * 1024 + g2 * GT + GT]
                cnt = P.emit("t", lambda t, o=pb.ap(), l=lhs, r=rhs,
                             st=(c == 0), sp=(c == NC - 1):
                             t.matmul(o, l, r, start=st, stop=sp),
                             waits=(w if ci == 0 else ()), inc=("TE", 1))
        mark(f"wb{bi}_free", "TE", cnt)
        if half == 1:
            for g2 in range(2):
                pb = qbank[g2][dq % 2]
                dst = QA.ap()[:, dq * 1024 + g2 * GT: dq * 1024 + g2 * GT + GT]
                scnt = P.emit("s", lambda s, d=dst, o=pb.ap():
                              s.activation(d, o, AF.Copy), waits=[("TE", cnt)],
                              inc=("SC", 1))
                mark(f"qb{g2}{dq % 2}_free", "SC", scnt)

    p2_units = [("A", xkA, kTA, VA, NA // 4, ti, qtr)
                for ti in range(NA // 4) for qtr in range(4)]
    p2_units += [("B", xkB, kTB, VB, NB // 4, ti, qtr)
                 for ti in range(NB // 4) for qtr in range(4)]
    p1_order = list(range(8, NC)) + list(range(8))
    p1_units = [(dq, half) for dq in p1_order for half in range(2)]
    ip2 = ip1 = 0
    while ip2 < len(p2_units) or ip1 < len(p1_units):
        if ip2 < len(p2_units):
            tag, xk_, kT_, V_, nt_, ti_, qtr_ = p2_units[ip2]
            p2_quarter(xk_, kT_, V_, ti_, qtr_, tag, nt_)
            ip2 += 1
        if ip1 < len(p1_units):
            p1_half(*p1_units[ip1])
            ip1 += 1
    mark("qT", "SC", P.at("SC"))

    if debug:
        P.emit("g", lambda g: g.dma_start(taps["tap_kA"], kTA), waits=need("kvA"), inc=("DT", 16))
        P.emit("g", lambda g: g.dma_start(taps["tap_vA"], VA), waits=need("kvA"), inc=("DT", 16))
        P.emit("g", lambda g: g.dma_start(taps["tap_kB"], kTB), waits=need("kvB"), inc=("DT", 16))
        P.emit("g", lambda g: g.dma_start(taps["tap_vB"], VB), waits=need("kvB"), inc=("DT", 16))
        P.emit("g", lambda g: g.dma_start(taps["tap_q"], QA.ap()), waits=need("qT"), inc=("DT", 16))
        mark("tapq", "DT", P.at("DT"))

    srot = [PB[0], PB[1], PB[2]]
    obank = [PB[3], PB[4]]
    rbank = [PB[5], PB[6]]
    for k in range(2):
        mark(f"obank{k}_free", "SC", P.at("SC"))
        mark(f"rbank{k}_free", "SC", P.at("SC"))
    slots = []
    pairidx = 0
    for (Ng, kT, V, gbase, g2) in [(NA, kTA, VA, 0, 0), (NB, kTB, VB, NA, 1)]:
        for h in range(H):
            ctx = dict(Ng=Ng, kT=kT, V=V, gbase=gbase, pidx=pairidx,
                       qsl=QA.ap()[:, h * 1024 + g2 * GT: h * 1024 + g2 * GT + GT],
                       ob=obank[pairidx % 2], rb=rbank[pairidx % 2], pready={})
            for s in range(Ng):
                slots.append((ctx, s))
            pairidx += 1

    pending_tails = []

    def emit_S_exp(ctx, s, gi):
        bank = srot[gi % 3]
        w = need(f"srot{gi % 3}_free") + (need("qT") if ctx["pidx"] == 0 else [])
        scnt = P.emit("t", lambda t, o=bank.ap(),
                      l=ctx["kT"][:, s * 128:(s + 1) * 128], r=ctx["qsl"]:
                      t.matmul(o, l, r, start=True, stop=True),
                      waits=w, inc=("TE", 1))
        dstP = PTD.ap() if s < 4 else PTt[s % 4].ap()
        w = [("TE", scnt)] + need("consts") + \
            (need("ptd_free") if s < 4 else need(f"pt{s % 4}_free"))
        ecnt = P.emit("s", lambda ss, d=dstP, o=bank.ap(),
                      b=sbias_sb.ap()[:, ctx["gbase"] + s:ctx["gbase"] + s + 1]:
                      ss.activation(d, o, AF.Exp, bias=b, scale=SCALE),
                      waits=w, inc=("SC", 1))
        mark(f"srot{gi % 3}_free", "SC", ecnt)
        if s < 4:
            w = [("SC", ecnt)] + need(f"pt{s % 4}_free") + need("consts")
            vcnt = P.emit("v", lambda v, d=PTt[s % 4].ap(), a=PTD.ap(),
                          m=masks.ap()[:, s * GT:(s + 1) * GT]:
                          v.tensor_tensor(d, a, m, ALU.mult),
                          waits=w, inc=("VE", 1))
            mark("ptd_free", "VE", vcnt)
            ctx["pready"][s] = ("VE", vcnt)
        else:
            ctx["pready"][s] = ("SC", ecnt)

    def emit_RPV(ctx, s):
        Ng, pidx, rb, ob = ctx["Ng"], ctx["pidx"], ctx["rb"], ctx["ob"]
        w = [ctx["pready"][s]] + ([need("consts_g")[0]] if pidx == 0 and s == 0 else []) \
            + (need(f"rbank{pidx % 2}_free") if s == 0 else [])
        P.emit("t", lambda t, o=rb.ap(), l=ones128.ap(), r=PTt[s % 4].ap(),
               st=(s == 0), sp=(s == Ng - 1):
               t.matmul(o, l, r, start=st, stop=sp), waits=w, inc=("TE", 1))
        w = need(f"obank{pidx % 2}_free") if s == 0 else []
        cnt = P.emit("t", lambda t, o=ob.ap(), l=ctx["V"][:, s * 128:(s + 1) * 128],
                     r=PTt[s % 4].ap(), st=(s == 0), sp=(s == Ng - 1):
                     t.matmul(o, l, r, start=st, stop=sp), waits=w, inc=("TE", 1))
        mark(f"pt{s % 4}_free", "TE", cnt)
        if s == Ng - 1:
            ostop = P.at("TE")

            def tail(ostop=ostop, rb=rb, ob=ob, qsl=ctx["qsl"], pidx=pidx):
                rcnt = P.emit("v", lambda v, o=rb.ap(): v.reciprocal(rinv_ap, o),
                              waits=[("TE", ostop)] + need("tail_done"), inc=("VE", 1))
                mark(f"rbank{pidx % 2}_free", "VE", rcnt)
                mcnt = P.emit("v", lambda v, d=qsl, o=ob.ap():
                              v.tensor_tensor(d, o, rinv_ap, ALU.mult),
                              waits=[("VE", rcnt)], inc=("VE", 1))
                mark(f"obank{pidx % 2}_free", "VE", mcnt)
                mark("tail_done", "VE", mcnt)
            pending_tails.append(tail)

    for gi, (ctx, s) in enumerate(slots):
        emit_S_exp(ctx, s, gi)
        if s == 4 and pending_tails:
            pending_tails.pop(0)()
        if gi >= 2:
            emit_RPV(*slots[gi - 2])
    emit_RPV(*slots[-2])
    emit_RPV(*slots[-1])
    for t_ in pending_tails:
        t_()
    pending_tails = []
    mark("attnT", "VE", P.at("VE"))
    p3_te = P.at("TE")

    if debug:
        P.emit("g", lambda g: g.dma_start(taps["tap_attn"], QA.ap()),
               waits=need("attnT") + need("tapq"), inc=("DT", 16))
        mark("tapattn", "DT", P.at("DT"))

    ybank = {0: [PB[0], PB[1]], 1: [PB[2], PB[3]]}
    statb = [PB[4], PB[5]]
    PTBf32 = PTB.bitcast(F32)
    lnbf2 = PTD
    lnsq2 = PTt[3]
    pending_stats = pending_stats2 = None
    for oc in range(NC):
        for half in range(2):
            bi = (oc * 2 + half) % 2
            buf = WBt[bi]
            dsem = f"DWB{bi}"
            w = need(f"wb{bi}_free")
            src = wo_r[oc * 128:(oc + 1) * 128, half * 2048:(half + 1) * 2048]
            P.emit("g", lambda g, d=buf.ap(), s=src: g.dma_start(d, s), waits=w,
                   inc=(dsem, 16))
            dtot = P.at(dsem)
            for g2 in range(2):
                pb = ybank[g2][oc % 2]
                w = [(dsem, dtot)] + need("attnT") + need(f"yb{g2}{oc % 2}_free")
                for ci in range(16):
                    c = half * 16 + ci
                    lhs = buf.ap()[:, ci * 128:(ci + 1) * 128]
                    rhs = QA.ap()[:, c * 1024 + g2 * GT:c * 1024 + g2 * GT + GT]
                    cnt = P.emit("t", lambda t, o=pb.ap(), l=lhs, r=rhs,
                                 st=(c == 0), sp=(c == NC - 1):
                                 t.matmul(o, l, r, start=st, stop=sp),
                                 waits=(w if ci == 0 else ()), inc=("TE", 1))
            mark(f"wb{bi}_free", "TE", cnt)
        if pending_stats is not None:
            pending_stats()
            pending_stats = None
        if pending_stats2 is not None:
            pending_stats2()
            pending_stats2 = None
        for g2 in range(2):
            rbuf = RESt[g2]
            rsem = f"DR{g2}"
            w = need(f"res{g2}_free")
            P.emit("y", lambda sy, d=rbuf.ap(),
                   s=xq[:, oc * 1024 + g2 * GT:oc * 1024 + (g2 + 1) * GT]:
                   sy.dma_start(d, s), waits=w, inc=(rsem, 16))
            rtot = P.at(rsem)
            pb = ybank[g2][oc % 2]
            if g2 == 0:
                dst = AR2_f.ap()[:, oc * GT:(oc + 1) * GT]
            else:
                dst = R1St[oc % 2].ap()
            w = [("TE", cnt), (rsem, rtot)] + \
                (need(f"r1s{oc % 2}_free") + need(f"r1s{oc % 2}_read") if g2 == 1 else [])
            vcnt = P.emit("v", lambda v, d=dst, o=pb.ap(), r=rbuf.ap():
                          v.tensor_tensor(d, o, r, ALU.add), waits=w, inc=("VE", 1))
            mark(f"res{g2}_free", "VE", vcnt)
            mark(f"yb{g2}{oc % 2}_free", "VE", vcnt)
            if g2 == 0:
                scnt = P.emit("s", lambda s, a=dst: s.activation(lnsq.ap(), a, AF.Square),
                              waits=[("VE", vcnt)] + need("lnsq_free"), inc=("SC", 1))
                vcnt2 = P.emit("v", lambda v, a=dst: v.tensor_copy(lnbf.ap(), a),
                               waits=[("VE", vcnt)] + need("lnbf_free"), inc=("VE", 1))

                def stats(oc=oc, scnt=scnt, vcnt2=vcnt2):
                    c1 = P.emit("t", lambda t, st=(oc == 0), sp=(oc == NC - 1):
                                t.matmul(statb[0].ap()[0:1, :], onescol.ap(), lnbf.ap(),
                                         start=st, stop=sp),
                                waits=[("VE", vcnt2)] + need("consts_g"), inc=("TE", 1))
                    mark("lnbf_free", "TE", c1)
                    c2 = P.emit("t", lambda t, st=(oc == 0), sp=(oc == NC - 1):
                                t.matmul(statb[1].ap()[0:1, :], onescol.ap(), lnsq.ap(),
                                         start=st, stop=sp), waits=[("SC", scnt)],
                                inc=("TE", 1))
                    mark("lnsq_free", "TE", c2)
                    mark("ln1stats", "TE", c2)
                pending_stats = stats
            else:
                ycnt = P.emit("y", lambda sy, d=spill[oc * 128:(oc + 1) * 128, :], s=dst:
                              sy.dma_start(d, s), waits=[("VE", vcnt)], inc=("DSP", 16))
                mark(f"r1s{oc % 2}_free", "DSP", ycnt)
                scnt2 = P.emit("s", lambda s, a=dst: s.activation(lnsq2.ap(), a, AF.Square),
                               waits=[("VE", vcnt)] + need("lnsq2_free"), inc=("SC", 1))
                mark(f"r1s{oc % 2}_read", "SC", scnt2)
                vcnt3 = P.emit("v", lambda v, a=dst: v.tensor_copy(lnbf2.ap(), a),
                               waits=[("VE", vcnt)] + need("lnbf2_free"), inc=("VE", 1))

                def statsb(oc=oc, scnt2=scnt2, vcnt3=vcnt3):
                    c1 = P.emit("t", lambda t, st=(oc == 0), sp=(oc == NC - 1):
                                t.matmul(PB[6].ap()[0:1, :], onescol.ap(), lnbf2.ap(),
                                         start=st, stop=sp),
                                waits=[("VE", vcnt3)], inc=("TE", 1))
                    mark("lnbf2_free", "TE", c1)
                    c2 = P.emit("t", lambda t, st=(oc == 0), sp=(oc == NC - 1):
                                t.matmul(PTBf32.ap()[0:1, 0:512], onescol.ap(), lnsq2.ap(),
                                         start=st, stop=sp), waits=[("SC", scnt2)],
                                inc=("TE", 1))
                    mark("lnsq2_free", "TE", c2)
                    mark("ln1bstats", "TE", c2)
                pending_stats2 = statsb
    for ps in (pending_stats, pending_stats2):
        if ps is not None:
            ps()
    pending_stats = pending_stats2 = None
    mark("p4_te", "TE", P.at("TE"))
    mark("p4", "VE", P.at("VE"))

    for i, (dstw, srcw) in enumerate([(a_sb, a_in), (bm_sb, bm_in), (rw_sb, rw_in)]):
        w = [("TE", p3_te)] if i == 0 else []
        P.emit("g", lambda g, d=dstw, s=srcw: g.dma_start(d, s), waits=w, inc=("DF", 16))
    mark("ffnw", "DF", P.at("DF"))

    if debug:
        P.emit("g", lambda g: g.dma_start(taps["tap_r1"], AR2_f.ap()),
               waits=need("p4") + need("tapattn"), inc=("DT", 16))
        mark("tapr1", "DT", P.at("DT"))
        P.emit("y", lambda sy: sy.dma_start(
               taps["tap_r1b"].rearrange("p (c n) -> p c n", n=GT),
               spill.rearrange("(c p) n -> p c n", p=128)),
               waits=[("DSP", P.at("DSP"))], inc=("DT2", 16))

    def ln_finish(tag, sA, sB, Mdst, Rdst, r0, r1_, r2_, extra=()):
        sem_stats = need(f"{tag}stats")
        c = P.emit("v", lambda v: v.tensor_scalar(r0.ap(), sA[0:1, :],
                   1.0 / D, None, ALU.mult), waits=sem_stats + list(extra), inc=("VE", 1))
        c = P.emit("v", lambda v: v.tensor_scalar(r1_.ap(), sB[0:1, :],
                   1.0 / D, EPS, ALU.mult, ALU.add), inc=("VE", 1))
        c = P.emit("v", lambda v: v.tensor_tensor(r2_.ap(), r0.ap(),
                   r0.ap(), ALU.mult), inc=("VE", 1))
        c = P.emit("v", lambda v: v.tensor_tensor(r1_.ap(), r1_.ap(),
                   r2_.ap(), ALU.subtract), inc=("VE", 1))
        mark(f"{tag}statdone", "VE", c)
        sc = P.emit("s", lambda s: s.activation(r2_.ap(), r1_.ap(), AF.Sqrt),
                    waits=[("VE", c)], inc=("SC", 1))
        rc = P.emit("v", lambda v: v.reciprocal(r1_.ap(), r2_.ap()),
                    waits=[("SC", sc)], inc=("VE", 1))
        b1 = P.emit("t", lambda t: t.matmul(PB[0].ap(), onesrow_f.ap(), r0.ap(),
                    start=True, stop=True),
                    waits=[("VE", c)] + need("consts_g") + need("bc_free"), inc=("TE", 1))
        s1 = P.emit("s", lambda s: s.activation(Mdst.ap(), PB[0].ap(), AF.Copy),
                    waits=[("TE", b1)], inc=("SC", 1))
        b2 = P.emit("t", lambda t: t.matmul(PB[0].ap(), onesrow_f.ap(), r1_.ap(),
                    start=True, stop=True), waits=[("VE", rc), ("SC", s1)], inc=("TE", 1))
        s2 = P.emit("s", lambda s: s.activation(Rdst.ap(), PB[0].ap(), AF.Copy),
                    waits=[("TE", b2)], inc=("SC", 1))
        mark("bc_free", "SC", s2)
        return s1, s2

    def ln_apply(x1_to_QA, g2, Msb, Rsb, s1, s2, extra=()):
        for c_ in range(NC):
            ap = AR2_f.ap()[:, c_ * GT:(c_ + 1) * GT]
            P.emit("v", lambda v, a=ap: v.tensor_tensor(a, a, Msb.ap(), ALU.subtract),
                   waits=[("SC", s1), ("SC", s2)] + list(extra), inc=("VE", 1))
            v2 = P.emit("v", lambda v, a=ap: v.tensor_tensor(a, a, Rsb.ap(), ALU.mult),
                        inc=("VE", 1))
            mark(f"lnapply_{c_}", "VE", v2)
            if x1_to_QA:
                dst = QA.ap()[:, g2 * (H * 512) + c_ * GT: g2 * (H * 512) + (c_ + 1) * GT]
                sc2 = P.emit("s", lambda s, d=dst, a=ap: s.activation(d, a, AF.Copy),
                             waits=[("VE", v2)] + need("p4_te"), inc=("SC", 1))
                mark(f"x1b_{c_}", "SC", sc2)
        mark("lnapplied", "VE", P.at("VE"))

    def ln_finish_apply(tag, x1_to_QA, g2):
        s1, s2 = ln_finish(tag, statb[0].ap(), statb[1].ap(), M_sb, Rs_sb,
                           srow[0], srow[1], srow[2])
        ln_apply(x1_to_QA, g2, M_sb, Rs_sb, s1, s2)

    def ffn_and_ln2_and_out(g2):
        x1b = QA.ap()[:, g2 * (H * 512):(g2 + 1) * (H * 512)]
        for ts in range(4):
            for c in range(NC):
                w = need("ffnw") + [rd[f"x1b_{c}"]] + (need("lg_free") if c == 0 else [])
                lhs = x1b[:, c * GT + ts * 128: c * GT + ts * 128 + 128]
                rhs = rw_sb[:, c * E:(c + 1) * E]
                P.emit("t", lambda t, o=PB[0].ap()[:, 0:E], l=lhs, r=rhs, st=(c == 0):
                       t.matmul(o, l, r, start=st, stop=False),
                       waits=w, inc=("TE", 1))
            cnt = P.emit("t", lambda t: t.matmul(PB[0].ap()[:, 0:E], onesrow_f.ap(),
                         b_sb.ap(), start=False, stop=True),
                         waits=need("consts") + need("consts_g"), inc=("TE", 1))
            sc = P.emit("s", lambda s: s.activation(lgt.ap(), PB[0].ap()[:, 0:E], AF.Copy),
                        waits=[("TE", cnt)], inc=("SC", 1))
            mark("lg_free", "SC", sc)
            vm = P.emit("v", lambda v: v.max(out=top8.ap(), in_=lgt.ap()),
                        waits=[("SC", sc)], inc=("VE", 1))
            vd = P.emit("v", lambda v: v.tensor_tensor(m1d.ap(), top8.ap()[:, 0:1],
                        top8.ap()[:, 1:2], ALU.subtract), waits=[("VE", vm)], inc=("VE", 1))
            sg = P.emit("s", lambda s: s.activation(g1t.ap(), m1d.ap(), AF.Sigmoid),
                        waits=[("VE", vd)], inc=("SC", 1))
            vg = P.emit("v", lambda v: v.tensor_scalar(g2t.ap(), g1t.ap(), -1.0, 1.0,
                        ALU.mult, ALU.add), waits=[("SC", sg)], inc=("VE", 1))
            P.emit("v", lambda v: v.tensor_scalar(msk1.ap(), lgt.ap(), top8.ap()[:, 0:1],
                   None, ALU.is_equal), waits=[("VE", vm)], inc=("VE", 1))
            P.emit("v", lambda v: v.tensor_scalar(msk2.ap(), lgt.ap(), top8.ap()[:, 1:2],
                   None, ALU.is_equal), inc=("VE", 1))
            P.emit("v", lambda v: v.tensor_scalar(msk1.ap(), msk1.ap(), g1t.ap(),
                   None, ALU.mult), inc=("VE", 1))
            v4 = P.emit("v", lambda v, d=wful.ap()[:, ts * E:(ts + 1) * E]:
                        v.scalar_tensor_tensor(d, msk2.ap(), g2t.ap(), msk1.ap(),
                        ALU.mult, ALU.add), waits=[("VE", vg)], inc=("VE", 1))
            tc = P.emit("t", lambda t, i=wful.ap()[:, ts * E:(ts + 1) * E]:
                        t.transpose(PB[1].ap()[0:E, 0:128], i, eye_f.ap()),
                        waits=[("VE", v4)] + need("wt_free") + need("consts"),
                        inc=("TE", 1))
            sc2 = P.emit("s", lambda s, d=wtrow_full.ap()[0:8, ts * 128:(ts + 1) * 128]:
                         s.activation(d, PB[1].ap()[0:E, 0:128], AF.Copy),
                         waits=[("TE", tc)], inc=("SC", 1))
            mark("wt_free", "SC", sc2)
            mark(f"wtr_{ts}", "SC", sc2)
        if debug and g2 == 0:
            P.emit("g", lambda g: g.dma_start(taps["tap_w"], wful.ap()),
                   waits=[rd["wtr_3"]], inc=("DT", 16))
        wt = [rd[f"wtr_{ts}"] for ts in range(4)]
        tc = P.emit("t", lambda t: t.matmul(PB[2].ap(), eexp_sb.ap(), wtrow_full.ap()[0:8, 0:512],
                    start=True, stop=True), waits=wt + need("consts"), inc=("TE", 1))
        wsc = P.emit("s", lambda s: s.activation(wexp_sb.ap(), PB[2].ap(), AF.Copy),
                     waits=[("TE", tc)], inc=("SC", 1))
        for c in range(NC):
            lhs = a_sb[:, c * 128:(c + 1) * 128]
            rhs = x1b[:, c * GT:(c + 1) * GT]
            w = [rd[f"x1b_{c}"]] + (need("ffnw") + need("h_free") if c == 0 else [])
            cnt = P.emit("t", lambda t, l=lhs, r=rhs, st=(c == 0), sp=(c == NC - 1):
                         t.matmul(PB[1].ap(), l, r, start=st, stop=sp),
                         waits=w, inc=("TE", 1))
        hv = P.emit("v", lambda v: v.tensor_tensor(hwb.ap(), PB[1].ap(), wexp_sb.ap(),
                    ALU.mult), waits=[("TE", cnt), ("SC", wsc)], inc=("VE", 1))
        mark("h_free", "VE", hv)
        if debug and g2 == 0:
            P.emit("g", lambda g: g.dma_start(taps["tap_hw"], hwb.ap()),
                   waits=[("VE", hv)], inc=("DT", 16))
        ob2 = [PB[3], PB[1], PB[2], PB[0]]
        pend2 = None
        for oc in range(NC):
            pb = ob2[oc % 4]
            w = [("VE", hv)] + need(f"ob{oc % 4}_free") + \
                (need("bc_free") + need("lg_free") if oc % 4 == 3 else [])
            cnt = P.emit("t", lambda t, o=pb.ap(), l=bm_sb[:, oc * 128:(oc + 1) * 128]:
                         t.matmul(o, l, hwb.ap(), start=True, stop=True),
                         waits=w, inc=("TE", 1))
            if pend2 is not None:
                pend2()
                pend2 = None
            ap = AR2_f.ap()[:, oc * GT:(oc + 1) * GT]
            vcnt = P.emit("v", lambda v, a=ap, o=pb.ap():
                          v.scalar_tensor_tensor(a, o, FFN_SCALE, a, ALU.mult, ALU.add),
                          waits=[("TE", cnt), rd[f"lnapply_{oc}"]], inc=("VE", 1))
            mark(f"ob{oc % 4}_free", "VE", vcnt)
            scnt = P.emit("s", lambda s, a=ap: s.activation(lnsq.ap(), a, AF.Square),
                          waits=[("VE", vcnt)] + need("lnsq_free"), inc=("SC", 1))
            vcnt2 = P.emit("v", lambda v, a=ap: v.tensor_copy(lnbf.ap(), a),
                           waits=[("VE", vcnt)] + need("lnbf_free"), inc=("VE", 1))

            def stats2(oc=oc, scnt=scnt, vcnt2=vcnt2):
                c1 = P.emit("t", lambda t, st=(oc == 0), sp=(oc == NC - 1):
                            t.matmul(statb[0].ap()[0:1, :], onescol.ap(), lnbf.ap(),
                                     start=st, stop=sp), waits=[("VE", vcnt2)],
                            inc=("TE", 1))
                mark("lnbf_free", "TE", c1)
                c2 = P.emit("t", lambda t, st=(oc == 0), sp=(oc == NC - 1):
                            t.matmul(statb[1].ap()[0:1, :], onescol.ap(), lnsq.ap(),
                                     start=st, stop=sp), waits=[("SC", scnt)],
                            inc=("TE", 1))
                mark("lnsq_free", "TE", c2)
                mark("ln2stats", "TE", c2)
            pend2 = stats2
        pend2()
        pend2 = None
        ln_finish_apply("ln2", False, g2)
        for oc in range(NC):
            src = AR2_f.ap()[:, oc * GT:(oc + 1) * GT]
            dst = y_out[oc * 128:(oc + 1) * 128, g2 * GT:(g2 + 1) * GT]
            cnt = P.emit("y", lambda sy, d=dst, s=src: sy.dma_start(d, s),
                         waits=[rd[f"lnapply_{oc}"]], inc=(f"DY{oc % 4}", 16))
            ycnts[(g2, oc)] = cnt

    ycnts = {}
    ln_finish_apply("ln1", True, 0)
    if debug:
        P.emit("g", lambda g: g.dma_start(taps["tap_x1"], AR2_f.ap()),
               waits=need("lnapplied") + need("tapr1"), inc=("DT", 16))
        mark("tapx1", "DT", P.at("DT"))
        P.emit("v", lambda v: v.tensor_copy(srow[2].ap(), srow[0].ap()),
               waits=need("tapx1"), inc=("VE", 1))
    ffn_and_ln2_and_out(0)

    s1b, s2b = ln_finish("ln1b", PB[6].ap(), PTBf32.ap()[:, 0:512], M_sb, Rs_sb,
                         srow[0], srow[1], srow[2])
    dspall = ("DSP", P.at("DSP"))
    dyall = [(f"DY{k}", P.at(f"DY{k}")) for k in range(4)]
    for oc in range(NC):
        relw = [dspall] + dyall + need("lnapplied")
        P.emit("y", lambda sy, d=AR2_f.ap()[:, oc * GT:(oc + 1) * GT],
               s=spill[oc * 128:(oc + 1) * 128, :]:
               sy.dma_start(d, s), waits=relw, inc=("DRL", 16))
    reload_done = ("DRL", P.at("DRL"))
    ln_apply(True, 1, M_sb, Rs_sb, s1b, s2b, extra=[reload_done])
    ffn_and_ln2_and_out(1)

    fin = [(f"DY{k}", P.at(f"DY{k}")) for k in range(4)]
    if debug:
        fin += [("DT", P.at("DT")), ("DT2", P.at("DT2"))]
    P.emit("y", lambda sy: None, waits=fin)

    with contextlib.ExitStack() as st:
        semh = {name: st.enter_context(nc.semaphore(name)) for name in P.cnt}
        block = st.enter_context(nc.Block())
        engmap = {"t": block.tensor, "s": block.scalar, "v": block.vector,
                  "g": block.gpsimd, "y": block.sync}

        def run_queue(e):
            def body(eng):
                for fn, waits, inc in P.q[e]:
                    for sem, tgt in waits:
                        eng.wait_ge(semh[sem], tgt)
                    ins = fn(eng)
                    if inc is not None:
                        assert ins is not None
                        ins.then_inc(semh[inc[0]], inc[1])
            return body

        for e in P.ENGINES:
            engmap[e](run_queue(e))

    return nc


_NC_CACHE = {}

def _get_nc(debug=False):
    if debug not in _NC_CACHE:
        _NC_CACHE[debug] = build(debug)
    return _NC_CACHE[debug]


def make_inputs(x, wq, wk, wv, wo, router_w, router_b, A, Bm):
    f32 = np.float32
    wq_r = np.ascontiguousarray(
        np.asarray(wq).reshape(32, 128, 32, 128).transpose(2, 1, 0, 3)).reshape(NC * 128, NC * 128)
    wo_r = np.ascontiguousarray(
        np.asarray(wo).reshape(32, 128, 32, 128).transpose(2, 1, 0, 3)).reshape(NC * 128, NC * 128)
    wkv = np.ascontiguousarray(
        np.concatenate([wk, wv], axis=1).reshape(32, 128, 256).transpose(1, 0, 2)).reshape(128, NC * 256)
    rw_in = np.ascontiguousarray(
        np.asarray(router_w).T.reshape(32, 128, 8).transpose(1, 0, 2).reshape(128, 256))
    a_in = np.ascontiguousarray(
        np.asarray(A).reshape(128, 4096).T.reshape(32, 128, 128).transpose(1, 0, 2).reshape(128, 4096))
    bm_in = np.ascontiguousarray(np.asarray(Bm).transpose(0, 2, 1).reshape(128, 4096))
    eexp = np.zeros((8, 128), f32)
    for e in range(8):
        eexp[e, e * 16:(e + 1) * 16] = 1.0
    eye_b = np.eye(128, dtype=f32).astype(BF)
    eye_f = np.eye(128, dtype=f32)
    kt_i = np.arange(128)[:, None]
    qt_i = np.arange(GT)[None, :]
    masks = np.concatenate(
        [(kt_i + 128 * m <= qt_i).astype(f32) for m in range(4)], axis=1).astype(BF)

    shared = dict(wq_r=wq_r, wo_r=wo_r, wkv=wkv, rw_in=rw_in,
                  b_row=np.asarray(router_b).reshape(1, 8).astype(f32),
                  a_in=a_in, bm_in=bm_in, eexp=eexp, eye_b=eye_b, eye_f=eye_f,
                  masks=masks)

    in_maps = []
    xT = [np.ascontiguousarray(np.asarray(x)[b].T) for b in range(2)]
    for core in range(8):
        b, j = core // 4, core % 4
        qsA, qsB = 512 * j, 512 * (7 - j)
        xt = xT[b]
        xq = np.concatenate([xt[:, qsA:qsA + 512], xt[:, qsB:qsB + 512]], axis=1)
        xq = np.ascontiguousarray(
            xq.reshape(32, 128, 1024).transpose(1, 0, 2)).reshape(128, NC * 1024)
        za = np.zeros((4096, NA * 128 - 512 - qsA), f32)
        xkA = np.concatenate([xt[:, qsA:qsA + 512], xt[:, 0:qsA], za], axis=1)
        zb = np.zeros((4096, NB * 128 - 512 - qsB), f32)
        xkB = np.concatenate([xt[:, qsB:qsB + 512], xt[:, 0:qsB], zb], axis=1)
        def xk_r(xk, ntiles):
            a = xk.reshape(4, 8, 128, ntiles, 512).transpose(3, 0, 2, 1, 4)
            return np.ascontiguousarray(a).reshape(ntiles * 4 * 128, 4096)
        xkA = xk_r(xkA, NA // 4)
        xkB = xk_r(xkB, NB // 4)
        sb_ = np.full((NA + NB,), DEADB, f32)
        sb_[:4 * j + 4] = CBIAS
        sb_[NA:NA + 32 - 4 * j] = CBIAS
        sbias = np.ascontiguousarray(np.tile(sb_[None, :], (128, 1)))
        m = dict(shared)
        m.update(xq=xq, xkA=xkA, xkB=xkB, sbias=sbias)
        in_maps.append(m)
    return in_maps


def kernel(L0_attn_x, L0_attn_wq, L0_attn_wk, L0_attn_wv, L0_attn_wo,
           L0_ffn_router_w, L0_ffn_router_b, L0_ffn_A, L0_ffn_Bm, _debug=False):
    in_maps = make_inputs(L0_attn_x, L0_attn_wq, L0_attn_wk, L0_attn_wv, L0_attn_wo,
                          L0_ffn_router_w, L0_ffn_router_b, L0_ffn_A, L0_ffn_Bm)
    nc = _get_nc(_debug)
    res = run_bass_kernel_spmd(nc, in_maps, core_ids=list(range(8)))
    out = np.empty((2, 4096, 4096), np.float32)
    for core in range(8):
        b, j = core // 4, core % 4
        y = res.results[core]["y"]
        out[b, 512 * j:512 * j + 512] = y[:, 0:512].T
        out[b, 512 * (7 - j):512 * (7 - j) + 512] = y[:, 512:1024].T
    if _debug:
        return out, res
    return out


# version 16 (final)
# speedup vs baseline: 1.1952x; 1.0077x over previous
import contextlib
import numpy as np
import ml_dtypes
import concourse.bass as bass
import concourse.mybir as mybir
from concourse.bass_utils import run_bass_kernel_spmd

F32 = mybir.dt.float32
BF16 = mybir.dt.bfloat16
AF = mybir.ActivationFunctionType
ALU = mybir.AluOpType
BF = ml_dtypes.bfloat16

D = 4096
NC = 32
H = 32
GT = 512
NA, NB = 16, 32
SCALE = float(1.0 / np.sqrt(128.0))
CBIAS = -12.0
DEADB = -1e30
EPS = 1e-5
E = 8
FFN_SCALE = 0.5


class Prog:
    ENGINES = ("t", "s", "v", "g", "y")

    def __init__(self):
        self.q = {e: [] for e in self.ENGINES}
        self.cnt = {}
        self.waited = {e: {} for e in self.ENGINES}

    def inc(self, sem, n=1):
        self.cnt[sem] = self.cnt.get(sem, 0) + n
        return self.cnt[sem]

    def at(self, sem):
        return self.cnt.get(sem, 0)

    def emit(self, e, fn, waits=(), inc=None):
        w = []
        for sem, tgt in waits:
            if tgt <= 0:
                continue
            if self.waited[e].get(sem, -1) >= tgt:
                continue
            self.waited[e][sem] = tgt
            w.append((sem, tgt))
        cnt = None
        if inc is not None:
            cnt = self.inc(*inc)
        self.q[e].append((fn, w, inc))
        return cnt


def build(debug=False):
    nc = bass.Bass("TRN2", target_bir_lowering=False)

    xq = nc.dram_tensor("xq", [128, NC * 1024], F32, kind="ExternalInput").ap()
    xkA = nc.dram_tensor("xkA", [NA * 128, 4096], F32, kind="ExternalInput").ap()
    xkB = nc.dram_tensor("xkB", [NB * 128, 4096], F32, kind="ExternalInput").ap()
    sbias = nc.dram_tensor("sbias", [128, NA + NB], F32, kind="ExternalInput").ap()
    wq_r = nc.dram_tensor("wq_r", [NC * 128, NC * 128], F32, kind="ExternalInput").ap()
    wo_r = nc.dram_tensor("wo_r", [NC * 128, NC * 128], F32, kind="ExternalInput").ap()
    wkv = nc.dram_tensor("wkv", [128, NC * 256], F32, kind="ExternalInput").ap()
    rw_in = nc.dram_tensor("rw_in", [128, NC * E], F32, kind="ExternalInput").ap()
    b_row = nc.dram_tensor("b_row", [1, E], F32, kind="ExternalInput").ap()
    a_in = nc.dram_tensor("a_in", [128, NC * 128], F32, kind="ExternalInput").ap()
    bm_in = nc.dram_tensor("bm_in", [128, NC * 128], F32, kind="ExternalInput").ap()
    eexp_in = nc.dram_tensor("eexp", [E, 128], F32, kind="ExternalInput").ap()
    eye_b_in = nc.dram_tensor("eye_b", [128, 128], BF16, kind="ExternalInput").ap()
    eye_f_in = nc.dram_tensor("eye_f", [128, 128], F32, kind="ExternalInput").ap()
    masks_in = nc.dram_tensor("masks", [128, 4 * GT], BF16, kind="ExternalInput").ap()

    y_out = nc.dram_tensor("y", [D, 1024], F32, kind="ExternalOutput").ap()
    spill = nc.dram_tensor("r1spill", [NC * 128, GT], F32, kind="Internal").ap()

    taps = {}
    if debug:
        for name, shape, dt in [
            ("tap_kA", [128, NA * 128], F32), ("tap_vA", [128, NA * 128], F32),
            ("tap_kB", [128, NB * 128], F32), ("tap_vB", [128, NB * 128], F32),
            ("tap_q", [128, H * 1024], F32), ("tap_attn", [128, H * 1024], F32),
            ("tap_r1", [128, NC * GT], F32), ("tap_x1", [128, NC * GT], F32),
            ("tap_w", [128, 4 * E], F32), ("tap_hw", [128, GT], F32),
            ("tap_r1b", [128, NC * GT], F32),
        ]:
            taps[name] = nc.dram_tensor(name, shape, dt, kind="ExternalOutput").ap()

    QA = nc.alloc_sbuf_tensor("QA", [128, H * 1024], BF16)
    AR2_f = nc.alloc_sbuf_tensor("AR2", [128, NC * GT], F32)
    XO = AR2_f.bitcast(BF16)
    KV = nc.alloc_sbuf_tensor("KV", [128, (NA + NB) * 128 * 2], BF16)
    kTA = KV.ap()[:, 0:NA * 128]
    kTB = KV.ap()[:, NA * 128:(NA + NB) * 128]
    VA = KV.ap()[:, (NA + NB) * 128:(2 * NA + NB) * 128]
    VB = KV.ap()[:, (2 * NA + NB) * 128:]
    WBt = [nc.alloc_sbuf_tensor(f"WB{i}", [128, 16 * 128], BF16) for i in range(2)]
    XKBt = [nc.alloc_sbuf_tensor(f"XKB{i}", [128, 8 * GT], BF16) for i in range(2)]
    PTt = [nc.alloc_sbuf_tensor(f"PT{i}", [128, GT], BF16) for i in range(4)]
    PTD = nc.alloc_sbuf_tensor("PTD", [128, GT], BF16)
    vstage = PTD
    masks = nc.alloc_sbuf_tensor("masks_sb", [128, 4 * GT], BF16)
    sbias_sb = nc.alloc_sbuf_tensor("sbias_sb", [128, NA + NB], F32)
    ones128 = nc.alloc_sbuf_tensor("ones128", [128, 128], BF16)
    onescol = nc.alloc_sbuf_tensor("onescol", [128, 1], BF16)
    onesrow_f = nc.alloc_sbuf_tensor("onesrow_f", [1, 128], F32)
    eye_b = nc.alloc_sbuf_tensor("eye_b_sb", [128, 128], BF16)
    eye_f = nc.alloc_sbuf_tensor("eye_f_sb", [128, 128], F32)
    wkv_sb = QA.ap()[:, 0:NC * 256]
    a_sb = KV.ap()[:, 0:NC * 128]
    bm_sb = KV.ap()[:, NC * 128:2 * NC * 128]
    rw_sb = KV.ap()[:, 2 * NC * 128:2 * NC * 128 + NC * E]
    b_sb = nc.alloc_sbuf_tensor("b_sb", [1, E], F32)
    eexp_sb = nc.alloc_sbuf_tensor("eexp_sb", [E, 128], F32)
    RESt = [nc.alloc_sbuf_tensor(f"RES{i}", [128, GT], F32) for i in range(2)]
    R1St = [nc.alloc_sbuf_tensor(f"R1S{i}", [128, GT], F32) for i in range(2)]
    lnbf = PTt[1]
    lnsq = PTt[2]
    M_sb = nc.alloc_sbuf_tensor("M_sb", [128, GT], F32)
    Rs_sb = nc.alloc_sbuf_tensor("Rs_sb", [128, GT], F32)
    rinv_ap = XKBt[1].bitcast(F32).ap()[:, 0:GT]
    wexp_sb = nc.alloc_sbuf_tensor("wexp_sb", [128, GT], F32)
    hwb = PTt[0]
    lgt = nc.alloc_sbuf_tensor("lgt", [128, E], F32)
    top8 = nc.alloc_sbuf_tensor("top8", [128, E], F32)
    m1d = nc.alloc_sbuf_tensor("m1d", [128, 1], F32)
    g1t = nc.alloc_sbuf_tensor("g1t", [128, 1], F32)
    g2t = nc.alloc_sbuf_tensor("g2t", [128, 1], F32)
    msk1 = nc.alloc_sbuf_tensor("msk1", [128, E], F32)
    msk2 = nc.alloc_sbuf_tensor("msk2", [128, E], F32)
    wful = nc.alloc_sbuf_tensor("wful", [128, 4 * E], F32)
    wtrow_full = masks.bitcast(F32)
    srow = [nc.alloc_sbuf_tensor(f"srow{i}", [1, GT], F32) for i in range(3)]

    assert nc.sbuf_bytes_remaining >= 0, f"SBUF over: {nc.sbuf_bytes_remaining}"

    PB = [nc.alloc_psum_tensor(f"PB{i}", [128, 512], F32) for i in range(7)]
    PTB = nc.alloc_psum_tensor("PTB", [128, 1024], BF16)
    assert nc.psum_banks_remaining == 0

    P = Prog()
    rd = {}

    def mark(name, sem, cnt):
        rd[name] = (sem, cnt)

    def need(name):
        return [rd[name]] if name in rd else []

    P.emit("g", lambda g: g.memset(ones128.ap(), 1.0))
    P.emit("g", lambda g: g.memset(onescol.ap(), 1.0))
    P.emit("g", lambda g: g.memset(onesrow_f.ap(), 1.0), inc=("GE", 1))
    mark("consts_g", "GE", P.at("GE"))

    for dst, src in [(masks.ap(), masks_in), (sbias_sb.ap(), sbias),
                     (eye_b.ap(), eye_b_in), (eye_f.ap(), eye_f_in),
                     (eexp_sb.ap(), eexp_in), (b_sb.ap(), b_row)]:
        P.emit("y", lambda sy, d=dst, s=src: sy.dma_start(d, s), inc=("DC", 16))
    mark("consts", "DC", P.at("DC"))

    P.emit("g", lambda g: g.dma_start(wkv_sb, wkv), inc=("DKV", 16))
    mark("wkv", "DKV", P.at("DKV"))

    for hf in range(4):
        src = xq[:, hf * 8 * 1024:(hf + 1) * 8 * 1024]
        dst = XO.ap()[:, hf * 8 * 1024:(hf + 1) * 8 * 1024]
        P.emit("g", lambda g, d=dst, s=src: g.dma_start(d, s), inc=("DXO", 16))
    mark("xo", "DXO", P.at("DXO"))

    def p2_quarter(xk, kT, V, ti, qtr, tag, ntiles):
        qi = ti * 4 + qtr
        bi = qi % 2
        buf = XKBt[bi]
        src = xk[qi * 128:(qi + 1) * 128, :]
        dsem = f"DXK{bi}"
        w = need(f"xkb{bi}_free")
        P.emit("g", lambda g, d=buf.ap(), s=src: g.dma_start(d, s), waits=w,
               inc=(dsem, 16))
        dtot = P.at(dsem)
        for which, pb, wofs in (("k", PB[4], 0), ("v", PB[5], 128)):
            w = [(dsem, dtot)] + need("wkv") + need(f"pb{which}_free")
            for ci in range(8):
                c = qtr * 8 + ci
                lhs = wkv_sb[:, c * 256 + wofs:c * 256 + wofs + 128]
                rhs = buf.ap()[:, ci * GT:(ci + 1) * GT]
                cnt = P.emit("t", lambda t, o=pb.ap(), l=lhs, r=rhs,
                             st=(c == 0), sp=(c == NC - 1):
                             t.matmul(o, l, r, start=st, stop=sp),
                             waits=(w if ci == 0 else ()), inc=("TE", 1))
            mark(f"p2{which}", "TE", cnt)
        mark(f"xkb{bi}_free", "TE", P.at("TE"))
        if qtr == 3:
            cnt = P.emit("s", lambda s, d=kT[:, ti * GT:(ti + 1) * GT], o=PB[4].ap():
                         s.activation(d, o, AF.Copy), waits=need("p2k"), inc=("SC", 1))
            mark("pbk_free", "SC", cnt)
            vst = P.emit("s", lambda s, d=vstage.ap(), o=PB[5].ap():
                         s.activation(d, o, AF.Copy), waits=need("p2v"), inc=("SC", 1))
            mark("pbv_free", "SC", vst)
            for cs in range(4):
                w = [("SC", vst)] + need("consts") + (need("ptb_free") if cs == 0 else [])
                cnt = P.emit("t", lambda t, o=PTB.ap()[:, cs * 128:(cs + 1) * 128],
                             i=vstage.ap()[:, cs * 128:(cs + 1) * 128], e=eye_b.ap():
                             t.transpose(o, i, e), waits=w, inc=("TE", 1))
            mark("vtr", "TE", cnt)
            cnt = P.emit("s", lambda s, d=V[:, ti * GT:(ti + 1) * GT],
                         o=PTB.ap()[:, 0:512]:
                         s.activation(d, o, AF.Copy), waits=need("vtr"), inc=("SC", 1))
            mark("ptb_free", "SC", cnt)
            if ti == ntiles - 1:
                mark(f"kv{tag}", "SC", P.at("SC"))

    qbank = {0: [PB[0], PB[2]], 1: [PB[1], PB[3]]}

    def p1_half(dq, half):
        bi = (dq * 2 + half) % 2
        buf = WBt[bi]
        dsem = f"DWB{bi}"
        w = need(f"wb{bi}_free")
        src = wq_r[dq * 128:(dq + 1) * 128, half * 2048:(half + 1) * 2048]
        P.emit("g", lambda g, d=buf.ap(), s=src: g.dma_start(d, s), waits=w,
               inc=(dsem, 16))
        dtot = P.at(dsem)
        for g2 in range(2):
            pb = qbank[g2][dq % 2]
            w = [(dsem, dtot)] + need("xo") + need(f"qb{g2}{dq % 2}_free")
            for ci in range(16):
                c = half * 16 + ci
                lhs = buf.ap()[:, ci * 128:(ci + 1) * 128]
                rhs = XO.ap()[:, c * 1024 + g2 * GT:c * 1024 + g2 * GT + GT]
                cnt = P.emit("t", lambda t, o=pb.ap(), l=lhs, r=rhs,
                             st=(c == 0), sp=(c == NC - 1):
                             t.matmul(o, l, r, start=st, stop=sp),
                             waits=(w if ci == 0 else ()), inc=("TE", 1))
        mark(f"wb{bi}_free", "TE", cnt)
        if half == 1:
            for g2 in range(2):
                pb = qbank[g2][dq % 2]
                dst = QA.ap()[:, dq * 1024 + g2 * GT: dq * 1024 + g2 * GT + GT]
                scnt = P.emit("s", lambda s, d=dst, o=pb.ap():
                              s.activation(d, o, AF.Copy), waits=[("TE", cnt)],
                              inc=("SC", 1))
                mark(f"qb{g2}{dq % 2}_free", "SC", scnt)

    p2_units = [("A", xkA, kTA, VA, NA // 4, ti, qtr)
                for ti in range(NA // 4) for qtr in range(4)]
    p2_units += [("B", xkB, kTB, VB, NB // 4, ti, qtr)
                 for ti in range(NB // 4) for qtr in range(4)]
    p1_order = list(range(8, NC)) + list(range(8))
    p1_units = [(dq, half) for dq in p1_order for half in range(2)]
    ip2 = ip1 = 0
    while ip2 < len(p2_units) or ip1 < len(p1_units):
        if ip2 < len(p2_units):
            tag, xk_, kT_, V_, nt_, ti_, qtr_ = p2_units[ip2]
            p2_quarter(xk_, kT_, V_, ti_, qtr_, tag, nt_)
            ip2 += 1
        if ip1 < len(p1_units):
            p1_half(*p1_units[ip1])
            ip1 += 1
    mark("qT", "SC", P.at("SC"))

    if debug:
        P.emit("g", lambda g: g.dma_start(taps["tap_kA"], kTA), waits=need("kvA"), inc=("DT", 16))
        P.emit("g", lambda g: g.dma_start(taps["tap_vA"], VA), waits=need("kvA"), inc=("DT", 16))
        P.emit("g", lambda g: g.dma_start(taps["tap_kB"], kTB), waits=need("kvB"), inc=("DT", 16))
        P.emit("g", lambda g: g.dma_start(taps["tap_vB"], VB), waits=need("kvB"), inc=("DT", 16))
        P.emit("g", lambda g: g.dma_start(taps["tap_q"], QA.ap()), waits=need("qT"), inc=("DT", 16))
        mark("tapq", "DT", P.at("DT"))

    srot = [PB[0], PB[1], PB[2]]
    obank = [PB[3], PB[4]]
    rbank = [PB[5], PB[6]]
    for k in range(2):
        mark(f"obank{k}_free", "SC", P.at("SC"))
        mark(f"rbank{k}_free", "SC", P.at("SC"))
    slots = []
    pairidx = 0
    for (Ng, kT, V, gbase, g2) in [(NA, kTA, VA, 0, 0), (NB, kTB, VB, NA, 1)]:
        for h in range(H):
            ctx = dict(Ng=Ng, kT=kT, V=V, gbase=gbase, pidx=pairidx,
                       qsl=QA.ap()[:, h * 1024 + g2 * GT: h * 1024 + g2 * GT + GT],
                       ob=obank[pairidx % 2], rb=rbank[pairidx % 2], pready={})
            for s in range(Ng):
                slots.append((ctx, s))
            pairidx += 1

    pending_tails = []

    def emit_S_exp(ctx, s, gi):
        bank = srot[gi % 3]
        w = need(f"srot{gi % 3}_free") + (need("qT") if ctx["pidx"] == 0 else [])
        scnt = P.emit("t", lambda t, o=bank.ap(),
                      l=ctx["kT"][:, s * 128:(s + 1) * 128], r=ctx["qsl"]:
                      t.matmul(o, l, r, start=True, stop=True),
                      waits=w, inc=("TE", 1))
        dstP = PTD.ap() if s < 4 else PTt[s % 4].ap()
        w = [("TE", scnt)] + need("consts") + \
            (need("ptd_free") if s < 4 else need(f"pt{s % 4}_free"))
        ecnt = P.emit("s", lambda ss, d=dstP, o=bank.ap(),
                      b=sbias_sb.ap()[:, ctx["gbase"] + s:ctx["gbase"] + s + 1]:
                      ss.activation(d, o, AF.Exp, bias=b, scale=SCALE),
                      waits=w, inc=("SC", 1))
        mark(f"srot{gi % 3}_free", "SC", ecnt)
        if s < 4:
            w = [("SC", ecnt)] + need(f"pt{s % 4}_free") + need("consts")
            vcnt = P.emit("v", lambda v, d=PTt[s % 4].ap(), a=PTD.ap(),
                          m=masks.ap()[:, s * GT:(s + 1) * GT]:
                          v.tensor_tensor(d, a, m, ALU.mult),
                          waits=w, inc=("VE", 1))
            mark("ptd_free", "VE", vcnt)
            ctx["pready"][s] = ("VE", vcnt)
        else:
            ctx["pready"][s] = ("SC", ecnt)

    def emit_RPV(ctx, s):
        Ng, pidx, rb, ob = ctx["Ng"], ctx["pidx"], ctx["rb"], ctx["ob"]
        w = [ctx["pready"][s]] + ([need("consts_g")[0]] if pidx == 0 and s == 0 else []) \
            + (need(f"rbank{pidx % 2}_free") if s == 0 else [])
        P.emit("t", lambda t, o=rb.ap(), l=ones128.ap(), r=PTt[s % 4].ap(),
               st=(s == 0), sp=(s == Ng - 1):
               t.matmul(o, l, r, start=st, stop=sp), waits=w, inc=("TE", 1))
        w = need(f"obank{pidx % 2}_free") if s == 0 else []
        cnt = P.emit("t", lambda t, o=ob.ap(), l=ctx["V"][:, s * 128:(s + 1) * 128],
                     r=PTt[s % 4].ap(), st=(s == 0), sp=(s == Ng - 1):
                     t.matmul(o, l, r, start=st, stop=sp), waits=w, inc=("TE", 1))
        mark(f"pt{s % 4}_free", "TE", cnt)
        if s == Ng - 1:
            ostop = P.at("TE")

            def tail(ostop=ostop, rb=rb, ob=ob, qsl=ctx["qsl"], pidx=pidx):
                rcnt = P.emit("v", lambda v, o=rb.ap(): v.reciprocal(rinv_ap, o),
                              waits=[("TE", ostop)] + need("tail_done"), inc=("VE", 1))
                mark(f"rbank{pidx % 2}_free", "VE", rcnt)
                mcnt = P.emit("v", lambda v, d=qsl, o=ob.ap():
                              v.tensor_tensor(d, o, rinv_ap, ALU.mult),
                              waits=[("VE", rcnt)], inc=("VE", 1))
                mark(f"obank{pidx % 2}_free", "VE", mcnt)
                mark("tail_done", "VE", mcnt)
            pending_tails.append(tail)

    for gi, (ctx, s) in enumerate(slots):
        emit_S_exp(ctx, s, gi)
        if s == 4 and pending_tails:
            pending_tails.pop(0)()
        if gi >= 2:
            emit_RPV(*slots[gi - 2])
    emit_RPV(*slots[-2])
    emit_RPV(*slots[-1])
    for t_ in pending_tails:
        t_()
    pending_tails = []
    mark("attnT", "VE", P.at("VE"))
    p3_te = P.at("TE")

    if debug:
        P.emit("g", lambda g: g.dma_start(taps["tap_attn"], QA.ap()),
               waits=need("attnT") + need("tapq"), inc=("DT", 16))
        mark("tapattn", "DT", P.at("DT"))

    ybank = {0: [PB[0], PB[1]], 1: [PB[2], PB[3]]}
    statb = [PB[4], PB[5]]
    PTBf32 = PTB.bitcast(F32)
    lnbf2 = PTD
    lnsq2 = PTt[3]
    pending_stats = pending_stats2 = None
    for oc in range(NC):
        for half in range(2):
            bi = (oc * 2 + half) % 2
            buf = WBt[bi]
            dsem = f"DWB{bi}"
            w = need(f"wb{bi}_free")
            src = wo_r[oc * 128:(oc + 1) * 128, half * 2048:(half + 1) * 2048]
            P.emit("g", lambda g, d=buf.ap(), s=src: g.dma_start(d, s), waits=w,
                   inc=(dsem, 16))
            dtot = P.at(dsem)
            for g2 in range(2):
                pb = ybank[g2][oc % 2]
                w = [(dsem, dtot)] + need("attnT") + need(f"yb{g2}{oc % 2}_free")
                for ci in range(16):
                    c = half * 16 + ci
                    lhs = buf.ap()[:, ci * 128:(ci + 1) * 128]
                    rhs = QA.ap()[:, c * 1024 + g2 * GT:c * 1024 + g2 * GT + GT]
                    cnt = P.emit("t", lambda t, o=pb.ap(), l=lhs, r=rhs,
                                 st=(c == 0), sp=(c == NC - 1):
                                 t.matmul(o, l, r, start=st, stop=sp),
                                 waits=(w if ci == 0 else ()), inc=("TE", 1))
            mark(f"wb{bi}_free", "TE", cnt)
        if pending_stats is not None:
            pending_stats()
            pending_stats = None
        if pending_stats2 is not None:
            pending_stats2()
            pending_stats2 = None
        for g2 in range(2):
            rbuf = RESt[g2]
            rsem = f"DR{g2}"
            w = need(f"res{g2}_free")
            P.emit("y", lambda sy, d=rbuf.ap(),
                   s=xq[:, oc * 1024 + g2 * GT:oc * 1024 + (g2 + 1) * GT]:
                   sy.dma_start(d, s), waits=w, inc=(rsem, 16))
            rtot = P.at(rsem)
            pb = ybank[g2][oc % 2]
            if g2 == 0:
                dst = AR2_f.ap()[:, oc * GT:(oc + 1) * GT]
            else:
                dst = R1St[oc % 2].ap()
            w = [("TE", cnt), (rsem, rtot)] + \
                (need(f"r1s{oc % 2}_free") + need(f"r1s{oc % 2}_read") if g2 == 1 else [])
            vcnt = P.emit("v", lambda v, d=dst, o=pb.ap(), r=rbuf.ap():
                          v.tensor_tensor(d, o, r, ALU.add), waits=w, inc=("VE", 1))
            mark(f"res{g2}_free", "VE", vcnt)
            mark(f"yb{g2}{oc % 2}_free", "VE", vcnt)
            if g2 == 0:
                scnt = P.emit("s", lambda s, a=dst: s.activation(lnsq.ap(), a, AF.Square),
                              waits=[("VE", vcnt)] + need("lnsq_free"), inc=("SC", 1))
                vcnt2 = P.emit("v", lambda v, a=dst: v.tensor_copy(lnbf.ap(), a),
                               waits=[("VE", vcnt)] + need("lnbf_free"), inc=("VE", 1))

                def stats(oc=oc, scnt=scnt, vcnt2=vcnt2):
                    c1 = P.emit("t", lambda t, st=(oc == 0), sp=(oc == NC - 1):
                                t.matmul(statb[0].ap()[0:1, :], onescol.ap(), lnbf.ap(),
                                         start=st, stop=sp),
                                waits=[("VE", vcnt2)] + need("consts_g"), inc=("TE", 1))
                    mark("lnbf_free", "TE", c1)
                    c2 = P.emit("t", lambda t, st=(oc == 0), sp=(oc == NC - 1):
                                t.matmul(statb[1].ap()[0:1, :], onescol.ap(), lnsq.ap(),
                                         start=st, stop=sp), waits=[("SC", scnt)],
                                inc=("TE", 1))
                    mark("lnsq_free", "TE", c2)
                    mark("ln1stats", "TE", c2)
                pending_stats = stats
            else:
                ycnt = P.emit("y", lambda sy, d=spill[oc * 128:(oc + 1) * 128, :], s=dst:
                              sy.dma_start(d, s), waits=[("VE", vcnt)],
                              inc=(f"DSP{oc % 2}", 16))
                mark(f"r1s{oc % 2}_free", f"DSP{oc % 2}", ycnt)
                scnt2 = P.emit("s", lambda s, a=dst: s.activation(lnsq2.ap(), a, AF.Square),
                               waits=[("VE", vcnt)] + need("lnsq2_free"), inc=("SC", 1))
                mark(f"r1s{oc % 2}_read", "SC", scnt2)
                vcnt3 = P.emit("v", lambda v, a=dst: v.tensor_copy(lnbf2.ap(), a),
                               waits=[("VE", vcnt)] + need("lnbf2_free"), inc=("VE", 1))

                def statsb(oc=oc, scnt2=scnt2, vcnt3=vcnt3):
                    c1 = P.emit("t", lambda t, st=(oc == 0), sp=(oc == NC - 1):
                                t.matmul(PB[6].ap()[0:1, :], onescol.ap(), lnbf2.ap(),
                                         start=st, stop=sp),
                                waits=[("VE", vcnt3)], inc=("TE", 1))
                    mark("lnbf2_free", "TE", c1)
                    c2 = P.emit("t", lambda t, st=(oc == 0), sp=(oc == NC - 1):
                                t.matmul(PTBf32.ap()[0:1, 0:512], onescol.ap(), lnsq2.ap(),
                                         start=st, stop=sp), waits=[("SC", scnt2)],
                                inc=("TE", 1))
                    mark("lnsq2_free", "TE", c2)
                    mark("ln1bstats", "TE", c2)
                pending_stats2 = statsb
    for ps in (pending_stats, pending_stats2):
        if ps is not None:
            ps()
    pending_stats = pending_stats2 = None
    mark("p4_te", "TE", P.at("TE"))
    mark("p4", "VE", P.at("VE"))

    for i, (dstw, srcw) in enumerate([(a_sb, a_in), (bm_sb, bm_in), (rw_sb, rw_in)]):
        w = [("TE", p3_te)] if i == 0 else []
        P.emit("g", lambda g, d=dstw, s=srcw: g.dma_start(d, s), waits=w, inc=("DF", 16))
    mark("ffnw", "DF", P.at("DF"))

    if debug:
        P.emit("g", lambda g: g.dma_start(taps["tap_r1"], AR2_f.ap()),
               waits=need("p4") + need("tapattn"), inc=("DT", 16))
        mark("tapr1", "DT", P.at("DT"))
        P.emit("y", lambda sy: sy.dma_start(
               taps["tap_r1b"].rearrange("p (c n) -> p c n", n=GT),
               spill.rearrange("(c p) n -> p c n", p=128)),
               waits=[("DSP", P.at("DSP"))], inc=("DT2", 16))

    def ln_finish(tag, sA, sB, Mdst, Rdst, r0, r1_, r2_, extra=()):
        sem_stats = need(f"{tag}stats")
        c = P.emit("v", lambda v: v.tensor_scalar(r0.ap(), sA[0:1, :],
                   1.0 / D, None, ALU.mult), waits=sem_stats + list(extra), inc=("VE", 1))
        c = P.emit("v", lambda v: v.tensor_scalar(r1_.ap(), sB[0:1, :],
                   1.0 / D, EPS, ALU.mult, ALU.add), inc=("VE", 1))
        c = P.emit("v", lambda v: v.tensor_tensor(r2_.ap(), r0.ap(),
                   r0.ap(), ALU.mult), inc=("VE", 1))
        c = P.emit("v", lambda v: v.tensor_tensor(r1_.ap(), r1_.ap(),
                   r2_.ap(), ALU.subtract), inc=("VE", 1))
        mark(f"{tag}statdone", "VE", c)
        sc = P.emit("s", lambda s: s.activation(r2_.ap(), r1_.ap(), AF.Sqrt),
                    waits=[("VE", c)], inc=("SC", 1))
        rc = P.emit("v", lambda v: v.reciprocal(r1_.ap(), r2_.ap()),
                    waits=[("SC", sc)], inc=("VE", 1))
        b1 = P.emit("t", lambda t: t.matmul(PB[0].ap(), onesrow_f.ap(), r0.ap(),
                    start=True, stop=True),
                    waits=[("VE", c)] + need("consts_g") + need("bc_free"), inc=("TE", 1))
        s1 = P.emit("s", lambda s: s.activation(Mdst.ap(), PB[0].ap(), AF.Copy),
                    waits=[("TE", b1)], inc=("SC", 1))
        b2 = P.emit("t", lambda t: t.matmul(PB[0].ap(), onesrow_f.ap(), r1_.ap(),
                    start=True, stop=True), waits=[("VE", rc), ("SC", s1)], inc=("TE", 1))
        s2 = P.emit("s", lambda s: s.activation(Rdst.ap(), PB[0].ap(), AF.Copy),
                    waits=[("TE", b2)], inc=("SC", 1))
        mark("bc_free", "SC", s2)
        return s1, s2

    def ln_apply(x1_to_QA, g2, Msb, Rsb, s1, s2, extra=()):
        for c_ in range(NC):
            ap = AR2_f.ap()[:, c_ * GT:(c_ + 1) * GT]
            P.emit("v", lambda v, a=ap: v.tensor_tensor(a, a, Msb.ap(), ALU.subtract),
                   waits=[("SC", s1), ("SC", s2)] + list(extra), inc=("VE", 1))
            v2 = P.emit("v", lambda v, a=ap: v.tensor_tensor(a, a, Rsb.ap(), ALU.mult),
                        inc=("VE", 1))
            mark(f"lnapply_{c_}", "VE", v2)
            if x1_to_QA:
                dst = QA.ap()[:, g2 * (H * 512) + c_ * GT: g2 * (H * 512) + (c_ + 1) * GT]
                sc2 = P.emit("s", lambda s, d=dst, a=ap: s.activation(d, a, AF.Copy),
                             waits=[("VE", v2)] + need("p4_te"), inc=("SC", 1))
                mark(f"x1b_{c_}", "SC", sc2)
        mark("lnapplied", "VE", P.at("VE"))

    def ln_finish_apply(tag, x1_to_QA, g2):
        s1, s2 = ln_finish(tag, statb[0].ap(), statb[1].ap(), M_sb, Rs_sb,
                           srow[0], srow[1], srow[2])
        ln_apply(x1_to_QA, g2, M_sb, Rs_sb, s1, s2)

    def ffn_and_ln2_and_out(g2):
        x1b = QA.ap()[:, g2 * (H * 512):(g2 + 1) * (H * 512)]
        for ts in range(4):
            for c in range(NC):
                w = need("ffnw") + [rd[f"x1b_{c}"]] + (need("lg_free") if c == 0 else [])
                lhs = x1b[:, c * GT + ts * 128: c * GT + ts * 128 + 128]
                rhs = rw_sb[:, c * E:(c + 1) * E]
                P.emit("t", lambda t, o=PB[0].ap()[:, 0:E], l=lhs, r=rhs, st=(c == 0):
                       t.matmul(o, l, r, start=st, stop=False),
                       waits=w, inc=("TE", 1))
            cnt = P.emit("t", lambda t: t.matmul(PB[0].ap()[:, 0:E], onesrow_f.ap(),
                         b_sb.ap(), start=False, stop=True),
                         waits=need("consts") + need("consts_g"), inc=("TE", 1))
            sc = P.emit("s", lambda s: s.activation(lgt.ap(), PB[0].ap()[:, 0:E], AF.Copy),
                        waits=[("TE", cnt)], inc=("SC", 1))
            mark("lg_free", "SC", sc)
            vm = P.emit("v", lambda v: v.max(out=top8.ap(), in_=lgt.ap()),
                        waits=[("SC", sc)], inc=("VE", 1))
            vd = P.emit("v", lambda v: v.tensor_tensor(m1d.ap(), top8.ap()[:, 0:1],
                        top8.ap()[:, 1:2], ALU.subtract), waits=[("VE", vm)], inc=("VE", 1))
            sg = P.emit("s", lambda s: s.activation(g1t.ap(), m1d.ap(), AF.Sigmoid),
                        waits=[("VE", vd)], inc=("SC", 1))
            vg = P.emit("v", lambda v: v.tensor_scalar(g2t.ap(), g1t.ap(), -1.0, 1.0,
                        ALU.mult, ALU.add), waits=[("SC", sg)], inc=("VE", 1))
            P.emit("v", lambda v: v.tensor_scalar(msk1.ap(), lgt.ap(), top8.ap()[:, 0:1],
                   None, ALU.is_equal), waits=[("VE", vm)], inc=("VE", 1))
            P.emit("v", lambda v: v.tensor_scalar(msk2.ap(), lgt.ap(), top8.ap()[:, 1:2],
                   None, ALU.is_equal), inc=("VE", 1))
            P.emit("v", lambda v: v.tensor_scalar(msk1.ap(), msk1.ap(), g1t.ap(),
                   None, ALU.mult), inc=("VE", 1))
            v4 = P.emit("v", lambda v, d=wful.ap()[:, ts * E:(ts + 1) * E]:
                        v.scalar_tensor_tensor(d, msk2.ap(), g2t.ap(), msk1.ap(),
                        ALU.mult, ALU.add), waits=[("VE", vg)], inc=("VE", 1))
            tc = P.emit("t", lambda t, i=wful.ap()[:, ts * E:(ts + 1) * E]:
                        t.transpose(PB[1].ap()[0:E, 0:128], i, eye_f.ap()),
                        waits=[("VE", v4)] + need("wt_free") + need("consts"),
                        inc=("TE", 1))
            sc2 = P.emit("s", lambda s, d=wtrow_full.ap()[0:8, ts * 128:(ts + 1) * 128]:
                         s.activation(d, PB[1].ap()[0:E, 0:128], AF.Copy),
                         waits=[("TE", tc)], inc=("SC", 1))
            mark("wt_free", "SC", sc2)
            mark(f"wtr_{ts}", "SC", sc2)
        if debug and g2 == 0:
            P.emit("g", lambda g: g.dma_start(taps["tap_w"], wful.ap()),
                   waits=[rd["wtr_3"]], inc=("DT", 16))
        wt = [rd[f"wtr_{ts}"] for ts in range(4)]
        tc = P.emit("t", lambda t: t.matmul(PB[2].ap(), eexp_sb.ap(), wtrow_full.ap()[0:8, 0:512],
                    start=True, stop=True), waits=wt + need("consts"), inc=("TE", 1))
        wsc = P.emit("s", lambda s: s.activation(wexp_sb.ap(), PB[2].ap(), AF.Copy),
                     waits=[("TE", tc)], inc=("SC", 1))
        for c in range(NC):
            lhs = a_sb[:, c * 128:(c + 1) * 128]
            rhs = x1b[:, c * GT:(c + 1) * GT]
            w = [rd[f"x1b_{c}"]] + (need("ffnw") + need("h_free") if c == 0 else [])
            cnt = P.emit("t", lambda t, l=lhs, r=rhs, st=(c == 0), sp=(c == NC - 1):
                         t.matmul(PB[1].ap(), l, r, start=st, stop=sp),
                         waits=w, inc=("TE", 1))
        hv = P.emit("v", lambda v: v.tensor_tensor(hwb.ap(), PB[1].ap(), wexp_sb.ap(),
                    ALU.mult), waits=[("TE", cnt), ("SC", wsc)], inc=("VE", 1))
        mark("h_free", "VE", hv)
        if debug and g2 == 0:
            P.emit("g", lambda g: g.dma_start(taps["tap_hw"], hwb.ap()),
                   waits=[("VE", hv)], inc=("DT", 16))
        ob2 = [PB[3], PB[1], PB[2], PB[0]]
        pend2 = None
        for oc in range(NC):
            pb = ob2[oc % 4]
            w = [("VE", hv)] + need(f"ob{oc % 4}_free") + \
                (need("bc_free") + need("lg_free") if oc % 4 == 3 else [])
            cnt = P.emit("t", lambda t, o=pb.ap(), l=bm_sb[:, oc * 128:(oc + 1) * 128]:
                         t.matmul(o, l, hwb.ap(), start=True, stop=True),
                         waits=w, inc=("TE", 1))
            if pend2 is not None:
                pend2()
                pend2 = None
            ap = AR2_f.ap()[:, oc * GT:(oc + 1) * GT]
            vcnt = P.emit("v", lambda v, a=ap, o=pb.ap():
                          v.scalar_tensor_tensor(a, o, FFN_SCALE, a, ALU.mult, ALU.add),
                          waits=[("TE", cnt), rd[f"lnapply_{oc}"]], inc=("VE", 1))
            mark(f"ob{oc % 4}_free", "VE", vcnt)
            scnt = P.emit("s", lambda s, a=ap: s.activation(lnsq.ap(), a, AF.Square),
                          waits=[("VE", vcnt)] + need("lnsq_free"), inc=("SC", 1))
            vcnt2 = P.emit("v", lambda v, a=ap: v.tensor_copy(lnbf.ap(), a),
                           waits=[("VE", vcnt)] + need("lnbf_free"), inc=("VE", 1))

            def stats2(oc=oc, scnt=scnt, vcnt2=vcnt2):
                c1 = P.emit("t", lambda t, st=(oc == 0), sp=(oc == NC - 1):
                            t.matmul(statb[0].ap()[0:1, :], onescol.ap(), lnbf.ap(),
                                     start=st, stop=sp), waits=[("VE", vcnt2)],
                            inc=("TE", 1))
                mark("lnbf_free", "TE", c1)
                c2 = P.emit("t", lambda t, st=(oc == 0), sp=(oc == NC - 1):
                            t.matmul(statb[1].ap()[0:1, :], onescol.ap(), lnsq.ap(),
                                     start=st, stop=sp), waits=[("SC", scnt)],
                            inc=("TE", 1))
                mark("lnsq_free", "TE", c2)
                mark("ln2stats", "TE", c2)
            pend2 = stats2
        pend2()
        pend2 = None
        ln_finish_apply("ln2", False, g2)
        for oc in range(NC):
            src = AR2_f.ap()[:, oc * GT:(oc + 1) * GT]
            dst = y_out[oc * 128:(oc + 1) * 128, g2 * GT:(g2 + 1) * GT]
            cnt = P.emit("y", lambda sy, d=dst, s=src: sy.dma_start(d, s),
                         waits=[rd[f"lnapply_{oc}"]], inc=(f"DY{oc % 4}", 16))
            ycnts[(g2, oc)] = cnt

    ycnts = {}
    ln_finish_apply("ln1", True, 0)
    if debug:
        P.emit("g", lambda g: g.dma_start(taps["tap_x1"], AR2_f.ap()),
               waits=need("lnapplied") + need("tapr1"), inc=("DT", 16))
        mark("tapx1", "DT", P.at("DT"))
        P.emit("v", lambda v: v.tensor_copy(srow[2].ap(), srow[0].ap()),
               waits=need("tapx1"), inc=("VE", 1))
    ffn_and_ln2_and_out(0)

    s1b, s2b = ln_finish("ln1b", PB[6].ap(), PTBf32.ap()[:, 0:512], M_sb, Rs_sb,
                         srow[0], srow[1], srow[2])
    dspall = [(f"DSP{k}", P.at(f"DSP{k}")) for k in range(2)]
    dyall = [(f"DY{k}", P.at(f"DY{k}")) for k in range(4)]
    for oc in range(NC):
        relw = dspall + dyall + need("lnapplied")
        P.emit("y", lambda sy, d=AR2_f.ap()[:, oc * GT:(oc + 1) * GT],
               s=spill[oc * 128:(oc + 1) * 128, :]:
               sy.dma_start(d, s), waits=relw, inc=("DRL", 16))
    reload_done = ("DRL", P.at("DRL"))
    ln_apply(True, 1, M_sb, Rs_sb, s1b, s2b, extra=[reload_done])
    ffn_and_ln2_and_out(1)

    fin = [(f"DY{k}", P.at(f"DY{k}")) for k in range(4)]
    if debug:
        fin += [("DT", P.at("DT")), ("DT2", P.at("DT2"))]
    P.emit("y", lambda sy: None, waits=fin)

    with contextlib.ExitStack() as st:
        semh = {name: st.enter_context(nc.semaphore(name)) for name in P.cnt}
        block = st.enter_context(nc.Block())
        engmap = {"t": block.tensor, "s": block.scalar, "v": block.vector,
                  "g": block.gpsimd, "y": block.sync}

        def run_queue(e):
            def body(eng):
                for fn, waits, inc in P.q[e]:
                    for sem, tgt in waits:
                        eng.wait_ge(semh[sem], tgt)
                    ins = fn(eng)
                    if inc is not None:
                        assert ins is not None
                        ins.then_inc(semh[inc[0]], inc[1])
            return body

        for e in P.ENGINES:
            engmap[e](run_queue(e))

    return nc


_NC_CACHE = {}

def _get_nc(debug=False):
    if debug not in _NC_CACHE:
        _NC_CACHE[debug] = build(debug)
    return _NC_CACHE[debug]


def make_inputs(x, wq, wk, wv, wo, router_w, router_b, A, Bm):
    f32 = np.float32
    wq_r = np.ascontiguousarray(
        np.asarray(wq).reshape(32, 128, 32, 128).transpose(2, 1, 0, 3)).reshape(NC * 128, NC * 128)
    wo_r = np.ascontiguousarray(
        np.asarray(wo).reshape(32, 128, 32, 128).transpose(2, 1, 0, 3)).reshape(NC * 128, NC * 128)
    wkv = np.ascontiguousarray(
        np.concatenate([wk, wv], axis=1).reshape(32, 128, 256).transpose(1, 0, 2)).reshape(128, NC * 256)
    rw_in = np.ascontiguousarray(
        np.asarray(router_w).T.reshape(32, 128, 8).transpose(1, 0, 2).reshape(128, 256))
    a_in = np.ascontiguousarray(
        np.asarray(A).reshape(128, 4096).T.reshape(32, 128, 128).transpose(1, 0, 2).reshape(128, 4096))
    bm_in = np.ascontiguousarray(np.asarray(Bm).transpose(0, 2, 1).reshape(128, 4096))
    eexp = np.zeros((8, 128), f32)
    for e in range(8):
        eexp[e, e * 16:(e + 1) * 16] = 1.0
    eye_b = np.eye(128, dtype=f32).astype(BF)
    eye_f = np.eye(128, dtype=f32)
    kt_i = np.arange(128)[:, None]
    qt_i = np.arange(GT)[None, :]
    masks = np.concatenate(
        [(kt_i + 128 * m <= qt_i).astype(f32) for m in range(4)], axis=1).astype(BF)

    shared = dict(wq_r=wq_r, wo_r=wo_r, wkv=wkv, rw_in=rw_in,
                  b_row=np.asarray(router_b).reshape(1, 8).astype(f32),
                  a_in=a_in, bm_in=bm_in, eexp=eexp, eye_b=eye_b, eye_f=eye_f,
                  masks=masks)

    in_maps = []
    xT = [np.ascontiguousarray(np.asarray(x)[b].T) for b in range(2)]
    for core in range(8):
        b, j = core // 4, core % 4
        qsA, qsB = 512 * j, 512 * (7 - j)
        xt = xT[b]
        xq = np.concatenate([xt[:, qsA:qsA + 512], xt[:, qsB:qsB + 512]], axis=1)
        xq = np.ascontiguousarray(
            xq.reshape(32, 128, 1024).transpose(1, 0, 2)).reshape(128, NC * 1024)
        za = np.zeros((4096, NA * 128 - 512 - qsA), f32)
        xkA = np.concatenate([xt[:, qsA:qsA + 512], xt[:, 0:qsA], za], axis=1)
        zb = np.zeros((4096, NB * 128 - 512 - qsB), f32)
        xkB = np.concatenate([xt[:, qsB:qsB + 512], xt[:, 0:qsB], zb], axis=1)
        def xk_r(xk, ntiles):
            a = xk.reshape(4, 8, 128, ntiles, 512).transpose(3, 0, 2, 1, 4)
            return np.ascontiguousarray(a).reshape(ntiles * 4 * 128, 4096)
        xkA = xk_r(xkA, NA // 4)
        xkB = xk_r(xkB, NB // 4)
        sb_ = np.full((NA + NB,), DEADB, f32)
        sb_[:4 * j + 4] = CBIAS
        sb_[NA:NA + 32 - 4 * j] = CBIAS
        sbias = np.ascontiguousarray(np.tile(sb_[None, :], (128, 1)))
        m = dict(shared)
        m.update(xq=xq, xkA=xkA, xkB=xkB, sbias=sbias)
        in_maps.append(m)
    return in_maps


def kernel(L0_attn_x, L0_attn_wq, L0_attn_wk, L0_attn_wv, L0_attn_wo,
           L0_ffn_router_w, L0_ffn_router_b, L0_ffn_A, L0_ffn_Bm, _debug=False):
    in_maps = make_inputs(L0_attn_x, L0_attn_wq, L0_attn_wk, L0_attn_wv, L0_attn_wo,
                          L0_ffn_router_w, L0_ffn_router_b, L0_ffn_A, L0_ffn_Bm)
    nc = _get_nc(_debug)
    res = run_bass_kernel_spmd(nc, in_maps, core_ids=list(range(8)))
    out = np.empty((2, 4096, 4096), np.float32)
    for core in range(8):
        b, j = core // 4, core % 4
        y = res.results[core]["y"]
        out[b, 512 * j:512 * j + 512] = y[:, 0:512].T
        out[b, 512 * (7 - j):512 * (7 - j) + 512] = y[:, 512:1024].T
    if _debug:
        return out, res
    return out
